# version 08
# baseline (speedup 1.0000x reference)
import numpy as np
from contextlib import ExitStack

import concourse.bass as bass
import concourse.mybir as mybir
from concourse.bass_utils import run_bass_kernel_spmd

F32 = mybir.dt.float32
BF16 = mybir.dt.bfloat16
AF = mybir.ActivationFunctionType
ALU = mybir.AluOpType

B, S, H = 4, 2048, 4096
TPC = 1024
NH, P, N, K = 64, 128, 128, 4
DSSM = NH * P
CONV = DSSM + 2 * N
PROJ = DSSM + CONV + NH
NKT = H // 128
NXT = CONV // 128
NZT = DSSM // 128
NDM = H // 128
NC = 8
CL = 128
EPS = 1e-5
BIGV = 1e30

ENGINES = ("tensor", "vector", "scalar", "gpsimd", "sync")
SEMS = ("s_pe", "s_v", "s_ac", "s_gp", "s_gpd", "s_syd", "s_scd", "s_cc")


class Emitter:
    def __init__(self):
        self.q = {e: [] for e in ENGINES}
        self.cnt = {s: 0 for s in SEMS}

    ENG_SEM = {"tensor": "s_pe", "vector": "s_v", "scalar": "s_ac",
               "gpsimd": "s_gp"}

    def op(self, eng, fn, waits=(), inc=None, amt=1, serial=False):
        waits = list(waits)
        if serial:
            own = self.ENG_SEM[eng]
            waits.append((own, self.cnt[own]))
        waits = tuple((s, v) for (s, v) in waits if v)
        self.q[eng].append((fn, waits, inc, amt))
        if inc is not None:
            self.cnt[inc] += amt
            return self.cnt[inc]
        return None

    def dma(self, queue, out, in_, waits=()):
        sem = {"sync": "s_syd", "scalar": "s_scd", "gpsimd": "s_gpd"}[queue]
        return self.op(queue, lambda e: e.dma_start(out=out, in_=in_),
                       waits=waits, inc=sem, amt=16)

    def bar(self, skip=("s_cc",)):
        return [(s, v) for s, v in self.cnt.items() if s not in skip and v]

    def check_deadlock(self):
        ptr = {e: 0 for e in ENGINES}
        cnt = {s: 0 for s in SEMS}
        progress = True
        while progress:
            progress = False
            for e in ENGINES:
                while ptr[e] < len(self.q[e]):
                    fn, waits, inc, amt = self.q[e][ptr[e]]
                    if all(cnt[s] >= v for (s, v) in waits):
                        if inc is not None:
                            cnt[inc] += amt
                        ptr[e] += 1
                        progress = True
                    else:
                        break
        stuck = {e: ptr[e] for e in ENGINES if ptr[e] < len(self.q[e])}
        if stuck:
            lines = []
            for e, i in stuck.items():
                fn, waits, inc, amt = self.q[e][i]
                missing = [(s, v, cnt[s]) for (s, v) in waits if cnt[s] < v]
                lines.append(f"{e}@{i}/{len(self.q[e])} missing={missing}")
            raise RuntimeError("DEADLOCK in schedule:\n" + "\n".join(lines))

    def replay_one(self, ename, eng, sems):
        last_wait = {}
        for fn, waits, inc, amt in self.q[ename]:
            for (s, v) in waits:
                if last_wait.get(s, -1) < v:
                    eng.wait_ge(sems[s], v)
                    last_wait[s] = v
            inst = fn(eng)
            if inc is not None and inst is not None:
                if inc == "s_cc":
                    inst.then_inc(sems[inc])
                else:
                    inst.then_inc(sems[inc], amt)

    def n_ops(self):
        return {e: len(self.q[e]) for e in ENGINES}


class Arena:
    def __init__(self, nelem):
        self.n = nelem
        self.off = 0
        self.marks = []
        self.peak = 0

    def take(self, nelem, align=64):
        o = (self.off + align - 1) // align * align
        self.off = o + nelem
        self.peak = max(self.peak, self.off)
        assert self.off <= self.n, f"SBUF arena overflow: {self.off} > {self.n}"
        return o

    def push(self):
        self.marks.append(self.off)

    def pop(self):
        self.off = self.marks.pop()


def build_program(debug=False):
    nc = bass.Bass("TRN2", target_bir_lowering=False, debug=False)
    em = Emitter()
    C = em.cnt

    def din(name, shape):
        return nc.dram_tensor(name, list(shape), F32, kind="ExternalInput").ap()

    xT = din("xT", (H, TPC))
    xhT = din("xhT", (H, 4))
    winp = din("winp", (128, NKT * PROJ))
    woutp = din("woutp", (128, DSSM * H // 128))
    wbp = din("wbp", (128, NKT * H))
    apack = din("apack", (128, NKT * 128))
    bmpack = din("bmpack", (128, H))
    wrtp = din("wrtp", (128, NKT * 8))
    convdiag = din("convdiag", (128, NXT * 4 * 128))
    i128 = din("i128", (128, 128))
    i64 = din("i64", (64, 64))
    trilbig = din("trilbig", (128, 128))
    triu01 = din("triu01", (128, 128))
    onescol = din("onescol", (128, 1))
    onesrow = din("onesrow", (1, 512))
    dtbias = din("dtbias", (64, 1))
    alog = din("alog", (64, 1))
    drow = din("drow", (1, 64))
    normw = din("normw", (128, 64))
    biasrow = din("biasrow", (1, H))
    gcol = din("gcol", (128, 1))
    rep2 = din("rep2", (8, 128))
    cvals = din("cvals", (128, 2))

    out_T = nc.dram_tensor("out_T", [H, TPC], F32, kind="ExternalOutput").ap()

    d_xbc = nc.dram_tensor("d_xbc", [CONV, TPC], BF16, kind="Internal").ap()
    d_xtm = nc.dram_tensor("d_xtm", [NC * 64 * 128, 128], BF16, kind="Internal").ap()
    d_z = nc.dram_tensor("d_z", [DSSM, TPC], BF16, kind="Internal").ap()
    d_hloc = nc.dram_tensor("d_hloc", [NC * 128, DSSM], BF16, kind="Internal").ap()
    d_ygn = nc.dram_tensor("d_ygn", [DSSM, TPC], BF16, kind="Internal").ap()
    d_attn = nc.dram_tensor("d_attn", [H, TPC], BF16, kind="Internal").ap()
    d_ffn = nc.dram_tensor("d_ffn", [H, TPC], BF16, kind="Internal").ap()
    d_s2 = nc.dram_tensor("d_s2", [H, TPC], BF16, kind="Internal").ap()
    d_part = nc.dram_tensor("d_part", [H, TPC], F32, kind="Internal").ap()
    cc_hin = nc.dram_tensor("cc_hin", [128, DSSM], BF16, kind="Internal").ap()
    cc_hout = nc.dram_tensor("cc_hout", [256, DSSM], BF16, kind="Internal").ap()

    d_dbgxf = nc.dram_tensor("d_dbgxf", [H, TPC], BF16, kind="Internal").ap()
    d_dbgw = nc.dram_tensor("d_dbgw", [128, 3 * TPC], BF16, kind="Internal").ap()
    dbg = {}
    if debug:
        for nm, src in [("xbc", d_xbc), ("z", d_z), ("xtm", d_xtm),
                        ("hloc", d_hloc), ("ygn", d_ygn), ("attn", d_attn),
                        ("ffn", d_ffn), ("hout", cc_hout), ("s2", d_s2),
                        ("xf", d_dbgxf), ("w", d_dbgw)]:
            dbg[nm] = (nc.dram_tensor("dbg_" + nm, list(src.shape), BF16,
                                      kind="ExternalOutput").ap(), src)

    with ExitStack() as ctx:
        BIGN = 74000
        big = ctx.enter_context(nc.sbuf_tensor("big", [128, BIGN], BF16))
        ar = Arena(BIGN)

        def bfv(off, shape):
            v = big[:, off:off + shape[1]]
            return v if shape[0] == 128 else v[0:shape[0], :]

        def f32v(off, shape):
            v = big[:, off:off + 2 * shape[1]].bitcast(F32)
            return v if shape[0] == 128 else v[0:shape[0], :]

        def sb(name, shape, dt):
            return ctx.enter_context(nc.sbuf_tensor(name, list(shape), dt))

        sb_i128 = sb("sb_i128", (128, 128), F32)
        sb_i64 = sb("sb_i64", (64, 64), F32)
        sb_tril = sb("sb_tril", (128, 128), F32)
        sb_triu = sb("sb_triu", (128, 128), BF16)
        sb_identbf = sb("sb_identbf", (128, 128), BF16)
        sb_onescolbf = sb("sb_onescolbf", (128, 1), BF16)
        sb_onesrowf = sb("sb_onesrowf", (1, 512), F32)
        sb_onesrowbf = sb("sb_onesrowbf", (1, 512), BF16)
        sb_dtb = sb("sb_dtb", (64, 1), F32)
        sb_alog = sb("sb_alog", (64, 1), F32)
        sb_drow = sb("sb_drow", (1, 64), F32)
        sb_normw = sb("sb_normw", (128, 64), F32)
        sb_biasbf = sb("sb_biasbf", (1, H), BF16)
        sb_g = sb("sb_g", (128, 1), F32)
        sb_rep2 = sb("sb_rep2", (8, 128), BF16)
        sb_cvals = sb("sb_cvals", (128, 2), F32)
        sb_dt = sb("sb_dt", (64, TPC), F32)
        sb_na = sb("sb_na", (64, TPC), F32)
        sb_sc1 = sb("sb_sc1", (64, TPC), F32)
        sb_sc2 = sb("sb_sc2", (64, TPC), F32)
        sb_natm = sb("sb_natm", (128, NC * 64), F32)
        sb_dttm = sb("sb_dttm", (128, NC * 64), F32)
        sb_nalb = sb("sb_nalb", (128, NC * 64), F32)
        sb_gexp = sb("sb_gexp", (128, NC * 64), F32)
        sb_lstep = sb("sb_lstep", (128, 64), F32)
        sb_wdt = sb("sb_wdt", (128, 64), F32)
        sb_bfm = sb("sb_bfm", (128, TPC), BF16)
        sb_cfm = sb("sb_cfm", (128, TPC), BF16)
        sb_btm = sb("sb_btm", (128, TPC), BF16)
        sb_expa = sb("sb_expa", (64, 1), F32)
        sb_row = sb("sb_row", (1, 128), F32)
        sb_stat = sb("sb_stat", (1, 3 * TPC), F32)

        pb = [ctx.enter_context(nc.psum_tensor(f"pb{i}", [128, 512], F32))
              for i in range(6)]
        pbf = ctx.enter_context(nc.psum_tensor("pbf", [128, 1024], BF16))
        pb7 = ctx.enter_context(nc.psum_tensor("pb7", [128, 512], F32))

        sems = {s: ctx.enter_context(nc.semaphore(s)) for s in SEMS}

        em.dma("sync", sb_i128[:], i128)
        em.dma("sync", sb_i64[:], i64)
        em.dma("sync", sb_tril[:], trilbig)
        em.dma("sync", sb_onesrowf[:], onesrow)
        em.dma("sync", sb_dtb[:], dtbias)
        em.dma("sync", sb_alog[:], alog)
        em.dma("sync", sb_drow[:], drow)
        em.dma("sync", sb_normw[:], normw)
        em.dma("sync", sb_g[:], gcol)
        em.dma("sync", sb_cvals[:], cvals)
        em.dma("gpsimd", sb_identbf[:], i128)
        em.dma("gpsimd", sb_triu[:], triu01)
        em.dma("gpsimd", sb_onescolbf[:], onescol)
        em.dma("gpsimd", sb_onesrowbf[:], onesrow)
        em.dma("gpsimd", sb_biasbf[:], biasrow)
        em.dma("gpsimd", sb_rep2[:], rep2)
        WCONST = [("s_syd", em.cnt["s_syd"]), ("s_gpd", em.cnt["s_gpd"])]

        srow = sb_stat[0:1, 0:TPC]
        qrow = sb_stat[0:1, TPC:2 * TPC]
        trow = sb_stat[0:1, 2 * TPC:3 * TPC]

        def emit_stats_tile(tile, sqtile, first, last, extra_wait_sum=(),
                            extra_wait_sq=(), sums=(2, 3), sqs=(4, 5)):
            toks = []
            for g2 in range(2):
                em.op("tensor", lambda e, g2=g2, t=tile, k0=first, k1=last:
                      e.matmul(pb[sums[g2]][0:1, :], sb_onescolbf[:],
                               t[:, g2 * 512:(g2 + 1) * 512], start=k0, stop=k1),
                      waits=extra_wait_sum, inc=None)
                tok = em.op("tensor", lambda e, g2=g2, t=sqtile, k0=first, k1=last:
                            e.matmul(pb[sqs[g2]][0:1, :], sb_onescolbf[:],
                                     t[:, g2 * 512:(g2 + 1) * 512], start=k0,
                                     stop=k1),
                            waits=extra_wait_sq,
                            inc=("s_pe" if (g2 == 1) else None))
                if tok:
                    toks.append(tok)
            return toks[-1]

        def finalize_stats(mbv, rbv, wtok_pe, denom, sums=(2, 3), sqs=(4, 5)):
            for g2 in range(2):
                em.op("vector", lambda e, g2=g2: e.tensor_copy(
                    srow[:, g2 * 512:(g2 + 1) * 512], pb[sums[g2]][0:1, :]),
                    waits=[("s_pe", wtok_pe)], inc="s_v")
                em.op("vector", lambda e, g2=g2: e.tensor_copy(
                    qrow[:, g2 * 512:(g2 + 1) * 512], pb[sqs[g2]][0:1, :]),
                    waits=[("s_pe", wtok_pe)], inc="s_v")
            t_ev = C["s_v"]
            em.op("scalar", lambda e: e.mul(srow, srow, 1.0 / denom),
                  waits=[("s_v", t_ev)], inc="s_ac", serial=True)
            em.op("scalar", lambda e: e.activation(trow, srow, AF.Square),
                  inc="s_ac", serial=True)
            t_m = C["s_ac"]
            em.op("vector", lambda e: e.scalar_tensor_tensor(
                qrow, qrow, 1.0 / denom, trow, ALU.mult, ALU.subtract),
                waits=[("s_ac", t_m)], inc="s_v", serial=True)
            em.op("scalar", lambda e: e.activation(qrow, qrow, AF.Ln,
                  bias=sb_cvals[0:1, 0:1]),
                  waits=[("s_v", C["s_v"])], inc="s_ac", serial=True)
            em.op("scalar", lambda e: e.activation(qrow, qrow, AF.Exp, scale=-0.5),
                  inc="s_ac", serial=True)
            t_r = C["s_ac"]
            t_last = 0
            for g2 in range(2):
                em.op("tensor", lambda e, g2=g2: e.matmul(
                    pb[0][:, :], sb_onesrowf[0:1, 0:128],
                    srow[:, g2 * 512:(g2 + 1) * 512], start=True, stop=True),
                    waits=[("s_ac", t_r), ("s_v", t_last)], inc="s_pe")
                em.op("tensor", lambda e, g2=g2: e.matmul(
                    pb[1][:, :], sb_onesrowf[0:1, 0:128],
                    qrow[:, g2 * 512:(g2 + 1) * 512], start=True, stop=True),
                    inc="s_pe")
                tb = C["s_pe"]
                em.op("vector", lambda e, g2=g2, mbv=mbv: e.tensor_copy(
                    mbv[:, g2 * 512:(g2 + 1) * 512], pb[0][:, :]),
                    waits=[("s_pe", tb)], inc="s_v")
                em.op("vector", lambda e, g2=g2, rbv=rbv: e.tensor_copy(
                    rbv[:, g2 * 512:(g2 + 1) * 512], pb[1][:, :]),
                    waits=[("s_pe", tb)], inc="s_v")
                t_last = C["s_v"]
            return t_last

        ar.push()
        o_xn = ar.take(NKT * TPC)
        o_sq = ar.take(2 * TPC)
        o_mb = ar.take(2 * TPC)
        o_rb = ar.take(2 * TPC)
        o_xh = ar.take(NKT * 4)
        o_hb = ar.take(64)
        xn = bfv(o_xn, (128, NKT * TPC))
        mb = f32v(o_mb, (128, TPC))
        rb = f32v(o_rb, (128, TPC))
        xh = bfv(o_xh, (128, NKT * 4))
        mhb = f32v(o_hb, (128, 8))[:, 0:4]
        rhb = f32v(o_hb + 16, (128, 8))[:, 0:4]

        t_x = em.dma("gpsimd", xn.rearrange("p (a t) -> p a t", a=NKT),
                     xT.rearrange("(a p) t -> p a t", p=128), waits=WCONST)
        t_xh = em.dma("gpsimd", xh.rearrange("p (a t) -> p a t", a=NKT),
                      xhT.rearrange("(a p) t -> p a t", p=128), waits=WCONST)

        t_sqpe = [0, 0]
        for kt in range(NKT):
            xt = xn[:, kt * TPC:(kt + 1) * TPC]
            slot = kt % 2
            sq = bfv(o_sq + slot * TPC, (128, TPC))
            em.op("scalar", lambda e, a=sq, b=xt: e.activation(a, b, AF.Square),
                  waits=[("s_gpd", t_x), ("s_pe", t_sqpe[slot])], inc="s_ac")
            t_aq = C["s_ac"]
            t_sqpe[slot] = emit_stats_tile(
                xt, sq, kt == 0, kt == NKT - 1,
                extra_wait_sum=[("s_gpd", t_x)],
                extra_wait_sq=[("s_ac", t_aq)])
        t_stats_mm = C["s_pe"]
        t_ln1 = finalize_stats(mb, rb, t_stats_mm, H)

        for kt in range(NKT):
            xt = xn[:, kt * TPC:(kt + 1) * TPC]
            em.op("vector", lambda e, a=xt: e.tensor_sub(a, a, mb),
                  waits=[("s_v", t_ln1), ("s_pe", t_stats_mm)], inc="s_v")
            em.op("vector", lambda e, a=xt: e.tensor_mul(a, a, rb), inc="s_v")
        t_xn = C["s_v"]

        hsq = bfv(o_sq, (128, NKT * 4))
        em.op("scalar", lambda e: e.activation(hsq, xh, AF.Square),
              waits=[("s_gpd", t_xh), ("s_pe", t_stats_mm)], inc="s_ac")
        t_hq = C["s_ac"]
        for kt in range(NKT):
            em.op("tensor", lambda e, kt=kt, k0=(kt == 0), k1=(kt == NKT - 1):
                  e.matmul(pb7[0:1, 0:4], sb_onescolbf[:],
                           xh[:, kt * 4:(kt + 1) * 4], start=k0, stop=k1),
                  waits=([("s_gpd", t_xh)] if kt == 0 else ()), inc=None)
            em.op("tensor", lambda e, kt=kt, k0=(kt == 0), k1=(kt == NKT - 1):
                  e.matmul(pb7[0:1, 4:8], sb_onescolbf[:],
                           hsq[:, kt * 4:(kt + 1) * 4], start=k0, stop=k1),
                  waits=([("s_ac", t_hq)] if kt == 0 else ()),
                  inc=("s_pe" if kt == NKT - 1 else None))
        t_hstat = C["s_pe"]
        hrow = sb_stat[0:1, 2 * TPC + 64:2 * TPC + 68]
        hrow2 = sb_stat[0:1, 2 * TPC + 68:2 * TPC + 72]
        hrow3 = sb_stat[0:1, 2 * TPC + 72:2 * TPC + 76]
        em.op("vector", lambda e: e.tensor_copy(hrow, pb7[0:1, 0:4]),
              waits=[("s_pe", t_hstat)], inc="s_v")
        em.op("vector", lambda e: e.tensor_copy(hrow2, pb7[0:1, 4:8]),
              waits=[("s_pe", t_hstat)], inc="s_v")
        em.op("scalar", lambda e: e.mul(hrow, hrow, 1.0 / H),
              waits=[("s_v", C["s_v"])], inc="s_ac", serial=True)
        em.op("scalar", lambda e: e.activation(hrow3, hrow, AF.Square),
              inc="s_ac", serial=True)
        em.op("vector", lambda e: e.scalar_tensor_tensor(
              hrow2, hrow2, 1.0 / H, hrow3, ALU.mult, ALU.subtract),
              waits=[("s_ac", C["s_ac"])], inc="s_v", serial=True)
        em.op("scalar", lambda e: e.activation(hrow2, hrow2, AF.Ln,
              bias=sb_cvals[0:1, 0:1]), waits=[("s_v", C["s_v"])], inc="s_ac",
              serial=True)
        em.op("scalar", lambda e: e.activation(hrow2, hrow2, AF.Exp, scale=-0.5),
              inc="s_ac", serial=True)
        t_hr = C["s_ac"]
        em.op("tensor", lambda e: e.matmul(pb7[:, 0:4], sb_onesrowf[0:1, 0:128],
              hrow, start=True, stop=True),
              waits=[("s_ac", t_hr), ("s_v", C["s_v"])], inc="s_pe")
        em.op("tensor", lambda e: e.matmul(pb7[:, 4:8], sb_onesrowf[0:1, 0:128],
              hrow2, start=True, stop=True), inc="s_pe")
        t_hbc = C["s_pe"]
        em.op("vector", lambda e: e.tensor_copy(mhb, pb7[:, 0:4]),
              waits=[("s_pe", t_hbc)], inc="s_v")
        em.op("vector", lambda e: e.tensor_copy(rhb, pb7[:, 4:8]),
              waits=[("s_pe", t_hbc)], inc="s_v")
        t_hln = C["s_v"]
        for kt in range(NKT):
            xht = xh[:, kt * 4:(kt + 1) * 4]
            em.op("vector", lambda e, a=xht: e.tensor_sub(a, a, mhb),
                  waits=[("s_v", t_hln), ("s_pe", t_hstat)], inc="s_v",
                  serial=True)
            em.op("vector", lambda e, a=xht: e.tensor_mul(a, a, rhb), inc="s_v",
                  serial=True)
        t_xnh = C["s_v"]
        t_pb7A = ("s_v", t_hln)

        o_wsl = ar.take(2 * NKT * 128)
        o_bst = ar.take(2 * TPC)
        o_halo = ar.take(NXT * 4)
        halo_all = bfv(o_halo, (128, NXT * 4))

        WOFF = [0]
        t_wsl = [None, None]
        t_evac = {i: ("s_v", t_ln1) for i in range(6)}
        pe_rot4 = (2, 3, 4, 5)

        t_hpb7 = [t_pb7A]

        def stream_tile(ti, width, on_evac, rot=pe_rot4, halo=False):
            slot = ti % 2
            wsl = bfv(o_wsl + slot * NKT * 128, (128, NKT * width))
            w = list(WCONST)
            if t_wsl[slot] is not None:
                w.append(t_wsl[slot])
            t_w = em.dma("gpsimd", wsl, winp[:, WOFF[0]:WOFF[0] + NKT * width],
                         waits=w)
            WOFF[0] += NKT * width
            if halo:
                for kt in range(NKT):
                    em.op("tensor", lambda e, kt=kt,
                          lw=wsl[:, kt * width:(kt + 1) * width],
                          k0=(kt == 0), k1=(kt == NKT - 1):
                          e.matmul(pb7[:, 256:260], lw,
                                   xh[:, kt * 4:(kt + 1) * 4],
                                   start=k0, stop=k1),
                          waits=([("s_gpd", t_w), ("s_v", t_xnh),
                                  t_hpb7[0]] if kt == 0 else ()),
                          inc=("s_pe" if kt == NKT - 1 else None))
                t_hmm = C["s_pe"]
                em.op("vector", lambda e, ti=ti: e.tensor_copy(
                    halo_all[:, ti * 4:(ti + 1) * 4], pb7[:, 256:260]),
                    waits=[("s_pe", t_hmm)], inc="s_v")
                t_hpb7[0] = ("s_v", C["s_v"])
            last_tok = None
            for half in range(2):
                bank = rot[(2 * ti + half) % len(rot)]
                wv = [("s_gpd", t_w), ("s_v", t_xn), t_evac[bank]]
                for kt in range(NKT):
                    em.op("tensor", lambda e, ps=pb[bank], wd=width,
                          lw=wsl[:, kt * width:(kt + 1) * width],
                          rhs=xn[:, kt * TPC + half * 512: kt * TPC + (half + 1) * 512],
                          k0=(kt == 0), k1=(kt == NKT - 1):
                          e.matmul(ps[0:wd, :] if wd < 128 else ps[:, :],
                                   lw, rhs, start=k0, stop=k1),
                          waits=(wv if kt == 0 else ()),
                          inc=("s_pe" if kt == NKT - 1 else None))
                t_pe = C["s_pe"]
                tok = on_evac(half, bank, t_pe)
                t_evac[bank] = tok
                last_tok = tok
            t_wsl[slot] = last_tok
            return last_tok

        t_bsp = [0, 0]
        for ti in range(NXT):
            slot2 = ti % 2
            bst = bfv(o_bst + slot2 * TPC, (128, TPC))

            def ev(half, bank, t_pe, bst=bst, slot2=slot2):
                w = [("s_pe", t_pe)]
                if half == 0 and t_bsp[slot2]:
                    w.append(("s_syd", t_bsp[slot2]))
                em.op("vector", lambda e, half=half, bank=bank, bst=bst:
                      e.tensor_copy(bst[:, half * 512:(half + 1) * 512],
                                    pb[bank][:, :]), waits=w, inc="s_v")
                return ("s_v", C["s_v"])

            stream_tile(ti, 128, ev, halo=True)
            t_bsp[slot2] = em.dma("sync", d_xbc[ti * 128:(ti + 1) * 128, :],
                                  bst[:], waits=[("s_v", C["s_v"])])
        t_b1_spill = em.cnt["s_syd"]
        t_halo = t_hpb7[0][1]

        def ev_dt(half, bank, t_pe):
            em.op("vector", lambda e, half=half, bank=bank: e.tensor_copy(
                sb_sc1[0:64, half * 512:(half + 1) * 512], pb[bank][0:64, :]),
                waits=[("s_pe", t_pe)], inc="s_v")
            return ("s_v", C["s_v"])
        stream_tile(NXT, 64, ev_dt)
        t_dtraw = C["s_v"]


        em.op("vector", lambda e: e.tensor_scalar(sb_sc1[0:64, :], sb_sc1[0:64, :],
              sb_dtb[:, 0:1], None, ALU.add), waits=[("s_v", t_dtraw)], inc="s_v")
        t_u = C["s_v"]
        em.op("scalar", lambda e: e.activation(sb_sc2[0:64, :], sb_sc1[0:64, :],
              AF.Abs), waits=[("s_v", t_u)], inc="s_ac", serial=True)
        em.op("scalar", lambda e: e.activation(sb_sc2[0:64, :], sb_sc2[0:64, :],
              AF.Exp, scale=-1.0), inc="s_ac", serial=True)
        em.op("scalar", lambda e: e.activation(sb_sc2[0:64, :], sb_sc2[0:64, :],
              AF.Ln, bias=sb_cvals[0:64, 1:2]), inc="s_ac", serial=True)
        t_lnx = C["s_ac"]
        em.op("vector", lambda e: e.tensor_scalar(sb_dt[0:64, :], sb_sc1[0:64, :],
              0.0, None, ALU.max), waits=[("s_ac", t_lnx)], inc="s_v")
        em.op("vector", lambda e: e.tensor_add(sb_dt[0:64, :], sb_dt[0:64, :],
              sb_sc2[0:64, :]), inc="s_v")
        t_dt = C["s_v"]
        em.op("scalar", lambda e: e.activation(sb_expa[:, 0:1], sb_alog[:, 0:1],
              AF.Exp), waits=WCONST, inc="s_ac")
        t_expa = C["s_ac"]
        em.op("vector", lambda e: e.tensor_scalar(sb_sc1[0:64, :], sb_dt[0:64, :],
              sb_expa[:, 0:1], None, ALU.mult),
              waits=[("s_v", t_dt), ("s_ac", t_expa)], inc="s_v")
        em.op("vector", lambda e: e.tensor_tensor_scan(
              sb_na[0:64, :], sb_sc1[0:64, :], sb_sc2[0:64, :], 0.0,
              ALU.add, ALU.bypass), inc="s_v")
        t_na = C["s_v"]

        t_pb7 = t_hpb7[0]
        for c in range(NC):
            em.op("tensor", lambda e, c=c: e.transpose(
                pb7[:, 0:64], sb_na[0:64, c * CL:(c + 1) * CL], sb_i64[:]),
                waits=[("s_v", t_na), t_pb7], inc="s_pe")
            em.op("vector", lambda e, c=c, t=C["s_pe"]: e.tensor_copy(
                sb_natm[:, c * 64:(c + 1) * 64], pb7[:, 0:64]),
                waits=[("s_pe", C["s_pe"])], inc="s_v")
            t_pb7 = ("s_v", C["s_v"])
            em.op("tensor", lambda e, c=c: e.transpose(
                pb7[:, 0:64], sb_dt[0:64, c * CL:(c + 1) * CL], sb_i64[:]),
                waits=[("s_v", t_na), t_pb7], inc="s_pe")
            em.op("vector", lambda e, c=c: e.tensor_copy(
                sb_dttm[:, c * 64:(c + 1) * 64], pb7[:, 0:64]),
                waits=[("s_pe", C["s_pe"])], inc="s_v")
            t_pb7 = ("s_v", C["s_v"])
            em.op("tensor", lambda e, c=c: e.transpose(
                pb7[0:1, 0:64], sb_na[0:64, c * CL + 127:c * CL + 128],
                sb_i64[:]), waits=[t_pb7], inc="s_pe")
            em.op("vector", lambda e: e.tensor_copy(sb_row[0:1, 0:64],
                  pb7[0:1, 0:64]), waits=[("s_pe", C["s_pe"])], inc="s_v")
            t_pb7 = ("s_v", C["s_v"])
            em.op("tensor", lambda e: e.matmul(pb7[:, 0:64],
                  sb_onesrowf[0:1, 0:128], sb_row[0:1, 0:64],
                  start=True, stop=True), waits=[t_pb7], inc="s_pe")
            em.op("vector", lambda e, c=c: e.tensor_copy(
                sb_nalb[:, c * 64:(c + 1) * 64], pb7[:, 0:64]),
                waits=[("s_pe", C["s_pe"])], inc="s_v")
            t_pb7 = ("s_v", C["s_v"])
            if c == 0:
                em.op("vector", lambda e: e.tensor_scalar(
                    sb_gexp[:, 0:64], sb_natm[:, 0:64], 0.0, sb_g[:, 0:1],
                    ALU.mult, ALU.add), inc="s_v")
                em.op("vector", lambda e: e.tensor_scalar(
                    sb_gexp[:, 0:64], sb_gexp[:, 0:64], 0.0, None, ALU.mult),
                    inc="s_v")
                em.op("vector", lambda e: e.tensor_scalar(
                    sb_gexp[:, 0:64], sb_gexp[:, 0:64], sb_g[:, 0:1], None,
                    ALU.add), inc="s_v")
            else:
                em.op("scalar", lambda e, c=c: e.activation(
                    sb_gexp[:, c * 64:(c + 1) * 64],
                    sb_nalb[:, (c - 1) * 64:c * 64], AF.Exp, scale=-1.0),
                    waits=[t_pb7] if c == 1 else [("s_v", C["s_v"])], inc="s_ac")
                em.op("vector", lambda e, c=c: e.tensor_scalar(
                    sb_gexp[:, c * 64:(c + 1) * 64],
                    sb_gexp[:, c * 64:(c + 1) * 64], sb_g[:, 0:1], None,
                    ALU.mult), waits=[("s_ac", C["s_ac"])], inc="s_v")
        t_prep = C["s_v"]
        t_prep_pb7 = t_pb7

        t_zsp = list(t_bsp)
        o_cin = ar.take(2 * 1056)
        o_cd = ar.take(2 * 512)
        o_cout = ar.take(2 * TPC)
        o_tst = ar.take(2 * TPC)

        t_cin = [None, None]
        t_cds = [None, None]
        t_cout = [("s_v", 0), ("s_v", 0)]
        t_tsp = [0, 0]
        t_tr = [0]
        t_convK = {4: t_evac[4], 5: t_evac[5]}

        def z_tile(zi):
            slot2 = zi % 2
            zst = bfv(o_bst + slot2 * TPC, (128, TPC))

            def evz(half, bank, t_pe, zst=zst, slot2=slot2):
                w = [("s_pe", t_pe)]
                if half == 0 and t_zsp[slot2]:
                    w.append(("s_syd", t_zsp[slot2]))
                em.op("scalar", lambda e, half=half, bank=bank, zst=zst:
                      e.activation(zst[:, half * 512:(half + 1) * 512],
                                   pb[bank][:, :], AF.Silu), waits=w, inc="s_ac")
                return ("s_ac", C["s_ac"])

            stream_tile(NXT + 1 + zi, 128, evz, rot=(2, 3))
            t_zsp[slot2] = em.dma("sync", d_z[zi * 128:(zi + 1) * 128, :],
                                  zst[:], waits=[("s_ac", C["s_ac"])])

        def conv_tile(ti):
            slot2 = ti % 2
            cin = bfv(o_cin + slot2 * 1056, (128, 1056))
            cds = bfv(o_cd + slot2 * 512, (128, 512))
            w = [("s_syd", t_b1_spill)]
            if t_cin[slot2] is not None:
                w.append(t_cin[slot2])
            t_rx = em.dma("scalar", cin[:, 3:1027],
                          d_xbc[ti * 128:(ti + 1) * 128, :], waits=w)
            em.op("vector", lambda e, cin=cin, ti=ti: e.tensor_copy(
                cin[:, 0:3], halo_all[:, ti * 4 + 1:(ti + 1) * 4]),
                waits=[("s_v", t_halo), ("s_scd", t_rx)], inc="s_v")
            t_hcp = C["s_v"]
            wd = list(WCONST)
            if t_cds[slot2] is not None:
                wd.append(t_cds[slot2])
            t_cd = em.dma("gpsimd", cds, convdiag[:, ti * 512:(ti + 1) * 512],
                          waits=wd)
            if ti == 64:
                cout = sb_bfm
            elif ti == 65:
                cout = sb_cfm
            else:
                cout = bfv(o_cout + slot2 * TPC, (128, TPC))
            for half in range(2):
                bank = 4 + half
                wv = [("s_scd", t_rx), ("s_gpd", t_cd), ("s_v", t_hcp),
                      t_cout[slot2], t_convK[bank]]
                for k in range(4):
                    em.op("tensor", lambda e, ps=pb[bank],
                          lw=cds[:, k * 128:(k + 1) * 128],
                          rhs=cin[:, half * 512 + k: half * 512 + k + 512],
                          k0=(k == 0), k1=(k == 3):
                          e.matmul(ps[:, :], lw, rhs, start=k0, stop=k1),
                          waits=(wv if k == 0 else ()),
                          inc=("s_pe" if k == 3 else None))
                t_cmm = C["s_pe"]
                em.op("scalar", lambda e, o=cout[:, half * 512:(half + 1) * 512],
                      ps=pb[bank]: e.activation(o, ps[:, :], AF.Silu),
                      waits=[("s_pe", t_cmm)], inc="s_ac")
                t_convK[bank] = ("s_ac", C["s_ac"])
            t_sil = ("s_ac", C["s_ac"])
            t_cin[slot2] = ("s_pe", C["s_pe"])
            t_cds[slot2] = ("s_pe", C["s_pe"])

            if ti == 64:
                for cg in range(2):
                    for c4 in range(4):
                        cc = cg * 4 + c4
                        em.op("tensor", lambda e, cc=cc, c4=c4: e.transpose(
                            pbf[:, c4 * 256:c4 * 256 + 128],
                            sb_bfm[:, cc * 128:(cc + 1) * 128], sb_identbf[:]),
                            waits=([t_sil, ("s_v", t_tr[0])] if c4 == 0 else ()),
                            inc=("s_pe" if c4 == 3 else None))
                    t_t4 = C["s_pe"]
                    for c4 in range(4):
                        cc = cg * 4 + c4
                        em.op("vector", lambda e, cc=cc, c4=c4: e.tensor_copy(
                            sb_btm[:, cc * 128:(cc + 1) * 128],
                            pbf[:, c4 * 256:c4 * 256 + 128]),
                            waits=[("s_pe", t_t4)], inc="s_v")
                    t_tr[0] = C["s_v"]
                t_cout[slot2] = t_sil
                return
            if ti == 65:
                t_cout[slot2] = t_sil
                return

            tst = bfv(o_tst + slot2 * TPC, (128, TPC))
            for cg in range(2):
                for c4 in range(4):
                    cc = cg * 4 + c4
                    wv = [t_sil, ("s_v", t_tr[0])]
                    if cg == 0 and c4 == 0 and t_tsp[slot2]:
                        wv.append(("s_syd", t_tsp[slot2]))
                    em.op("tensor", lambda e, cc=cc, c4=c4, src=cout: e.transpose(
                        pbf[:, c4 * 256:c4 * 256 + 128],
                        src[:, cc * 128:(cc + 1) * 128], sb_identbf[:]),
                        waits=(wv if c4 == 0 else ()),
                        inc=("s_pe" if c4 == 3 else None))
                t_t4 = C["s_pe"]
                for c4 in range(4):
                    cc = cg * 4 + c4
                    em.op("vector", lambda e, cc=cc, c4=c4, tst=tst: e.tensor_copy(
                        tst[:, cc * 128:(cc + 1) * 128],
                        pbf[:, c4 * 256:c4 * 256 + 128]),
                        waits=[("s_pe", t_t4)], inc="s_v")
                t_tr[0] = C["s_v"]
            t_cout[slot2] = ("s_pe", C["s_pe"])
            t_tsp[slot2] = em.dma(
                "sync", d_xtm.rearrange("(c h s) p -> h s c p", c=NC, h=64)[ti],
                tst.rearrange("s (c p) -> s c p", c=NC),
                waits=[("s_v", t_tr[0])])

        for i in range(NXT):
            if i < NZT:
                z_tile(i)
            conv_tile(i)
        t_zdone = em.cnt["s_syd"]
        t_tr_evac = t_tr[0]
        t_b2 = em.cnt["s_syd"]
        t_btm = ("s_v", t_tr_evac)
        t_bcfm = [t_cout[0], t_cout[1]]
        t_conv_final = [t_convK[4], t_convK[5]]

        ar.pop()
        ar.push()
        o_xtm1 = ar.take(2 * DSSM)
        o_hA = ar.take(2 * DSSM)
        o_hB = ar.take(2 * DSSM)
        o_hbs = ar.take(DSSM)
        o_xw = ar.take(2 * 128)
        hbuf = [f32v(o_hA, (128, DSSM)), f32v(o_hB, (128, DSSM))]
        hbstage = bfv(o_hbs, (128, DSSM))

        t_xtm1 = [None, None]
        t_hb_spill = 0
        t_chain_v = 0
        for c in range(NC):
            slot2 = c % 2
            xtm1 = bfv(o_xtm1 + slot2 * DSSM, (128, DSSM))
            w = [("s_syd", t_b2)]
            if c == 0:
                w += t_bcfm + [t_btm]
            if t_xtm1[slot2] is not None:
                w.append(t_xtm1[slot2])
            t_lx = em.dma("scalar", xtm1.rearrange("s (h p) -> s h p", h=64),
                          d_xtm.rearrange("(c h s) p -> c s h p", c=NC, h=64)[c],
                          waits=w)
            if c > 0:
                em.op("vector", lambda e, c=c: e.tensor_sub(
                    sb_lstep[:, 0:64], sb_nalb[:, (c - 1) * 64:c * 64],
                    sb_nalb[:, c * 64:(c + 1) * 64]),
                    waits=[("s_v", t_prep), ("s_v", t_chain_v)], inc="s_v")
                em.op("scalar", lambda e: e.activation(sb_lstep[:, 0:64],
                      sb_lstep[:, 0:64], AF.Exp), waits=[("s_v", C["s_v"])],
                      inc="s_ac")
            t_ls = C["s_ac"]
            em.op("vector", lambda e, c=c: e.tensor_sub(
                sb_wdt[:, 0:64], sb_natm[:, c * 64:(c + 1) * 64],
                sb_nalb[:, c * 64:(c + 1) * 64]),
                waits=[("s_v", t_prep), ("s_v", t_chain_v)], inc="s_v")
            em.op("scalar", lambda e: e.activation(sb_wdt[:, 0:64],
                  sb_wdt[:, 0:64], AF.Exp), waits=[("s_v", C["s_v"])], inc="s_ac")
            em.op("vector", lambda e, c=c: e.tensor_mul(sb_wdt[:, 0:64],
                  sb_wdt[:, 0:64], sb_dttm[:, c * 64:(c + 1) * 64]),
                  waits=[("s_ac", C["s_ac"])], inc="s_v")
            t_wd = C["s_v"]

            hs_src = hbuf[c % 2]
            hs_dst = hbuf[(c + 1) % 2]
            t_smm = {}
            tok_xw = {}
            tok_stt = {}
            PL = 4

            def p1_front(h):
                xw = bfv(o_xw + (h % 8) * 128, (128, 128))
                wv = [("s_scd", t_lx), ("s_v", t_wd), ("s_ac", t_ls),
                      ("s_pe", t_smm.get(h - 8, 0))]
                em.op("vector", lambda e, h=h, xw=xw, xtm1=xtm1: e.tensor_scalar(
                    xw, xtm1[:, h * 128:(h + 1) * 128], sb_wdt[:, h:h + 1],
                    None, ALU.mult), waits=wv, inc="s_v")
                tok_xw[h] = C["s_v"]
                bank = h % 4
                em.op("tensor", lambda e, h=h, xw=xw, bank=bank, c=c: e.matmul(
                    pb[bank][:, 0:128], sb_btm[:, c * 128:(c + 1) * 128], xw,
                    start=True, stop=True),
                    waits=[("s_v", tok_xw[h]), t_btm,
                           ("s_v", tok_stt.get(h - 4, 0))], inc="s_pe")
                t_smm[h] = C["s_pe"]

            def p1_back(h):
                bank = h % 4
                if c == 0:
                    em.op("vector", lambda e, h=h, bank=bank, d=hs_dst:
                          e.tensor_copy(d[:, h * 128:(h + 1) * 128],
                                        pb[bank][:, 0:128]),
                          waits=[("s_pe", t_smm[h])], inc="s_v")
                else:
                    em.op("vector", lambda e, h=h, bank=bank, d=hs_dst, s=hs_src:
                          e.scalar_tensor_tensor(
                              d[:, h * 128:(h + 1) * 128],
                              s[:, h * 128:(h + 1) * 128], sb_lstep[:, h:h + 1],
                              pb[bank][:, 0:128], ALU.mult, ALU.add),
                          waits=[("s_pe", t_smm[h])], inc="s_v")
                tok_stt[h] = C["s_v"]
                wv2 = [("s_v", tok_stt[h])]
                if h == 0 and t_hb_spill:
                    wv2.append(("s_syd", t_hb_spill))
                em.op("scalar", lambda e, h=h, d=hs_dst: e.copy(
                    hbstage[:, h * 128:(h + 1) * 128],
                    d[:, h * 128:(h + 1) * 128]), waits=wv2, inc="s_ac")

            for i in range(NH + PL):
                if i < NH:
                    p1_front(i)
                if i >= PL:
                    p1_back(i - PL)
            t_chain_v = C["s_v"]
            t_chain_ac = C["s_ac"]
            t_xtm1[slot2] = ("s_v", t_chain_v)
            dstp = cc_hin if c == NC - 1 else d_hloc[c * 128:(c + 1) * 128, :]
            t_hb_spill = em.dma("sync", dstp, hbstage,
                                waits=[("s_ac", t_chain_ac)])
        t_chain_final = t_chain_v
        em.op("gpsimd", lambda e: e.collective_compute(
            "AllGather", ALU.bypass,
            replica_groups=[[0, 1], [2, 3], [4, 5], [6, 7]],
            ins=[cc_hin.opt()], outs=[cc_hout.opt()]),
            waits=[("s_syd", t_hb_spill)], inc="s_cc")
        t_ag2 = C["s_cc"]
        ar.pop()

        ar.push()
        o_xtm = ar.take(2 * DSSM)
        o_hpeer = ar.take(DSSM)
        o_hloc = ar.take(DSSM)
        o_zc = ar.take(DSSM)
        o_yc = ar.take(DSSM)
        o_idsc = ar.take(DSSM)
        NSLOT = 10
        KLAG = 7
        o_g = ar.take(128)
        o_e = ar.take(2 * 128)
        o_e2 = ar.take(2 * 128)
        o_ar2 = ar.take(2 * 128)
        o_mp = ar.take(NSLOT * 128)
        o_cpp = ar.take(NSLOT * 128)
        o_hbt = ar.take(NSLOT * 128)
        o_rb2 = ar.take(256)
        o_dbc = ar.take(128)
        hpeer = bfv(o_hpeer, (128, DSSM))
        hloc = bfv(o_hloc, (128, DSSM))
        ych = bfv(o_yc, (128, DSSM))
        zc = bfv(o_zc, (128, DSSM))
        idsc = bfv(o_idsc, (128, DSSM))
        gtile = bfv(o_g, (128, 128))
        rb2 = f32v(o_rb2, (128, 128))
        dbc = f32v(o_dbc, (128, 64))

        t_hpeer = em.dma("scalar", hpeer, cc_hout[0:128, :],
                         waits=[("s_cc", t_ag2)])
        em.op("tensor", lambda e: e.matmul(pb7[:, 0:64], sb_onesrowf[0:1, 0:128],
              sb_drow[0:1, 0:64], start=True, stop=True),
              waits=WCONST + [t_prep_pb7], inc="s_pe")
        t_db = C["s_pe"]
        em.op("vector", lambda e: e.tensor_copy(dbc, pb7[:, 0:64]),
              waits=[("s_pe", t_db)], inc="s_v")
        t_dbc = C["s_v"]
        for h in range(NH):
            em.op("vector", lambda e, h=h: e.tensor_scalar(
                idsc[:, h * 128:(h + 1) * 128], sb_identbf[:], dbc[:, h:h + 1],
                None, ALU.mult),
                waits=[("s_v", t_dbc), ("s_syd", t_hb_spill)], inc="s_v")
        t_idsc = C["s_v"]

        t_xtm2 = [None, None]
        t_zld = 0
        t_hlld = 0
        t_ygn_spill = 0
        t_rmsmm = 0
        gi_pA = {}
        gi_pB = {}
        gi_AR = {}
        gi_HBT = {}
        gi_Y = {}
        gi_Yev = {}
        gi_MP = {}
        gi_CPP = {}
        t_gv = 0
        t_rbv = 0
        for c in range(NC):
            slot2 = c % 2
            xtm = bfv(o_xtm + slot2 * DSSM, (128, DSSM))
            w = [("s_syd", t_b2)]
            if c == 0:
                w.append(("s_syd", t_hb_spill))
            if t_xtm2[slot2] is not None:
                w.append(t_xtm2[slot2])
            t_lx = em.dma("scalar", xtm.rearrange("s (h p) -> s h p", h=64),
                          d_xtm.rearrange("(c h s) p -> c s h p", c=NC, h=64)[c],
                          waits=w)
            wz = [("s_syd", t_zdone)]
            if c == 0:
                wz.append(("s_syd", t_hb_spill))
            if t_ygn_spill:
                wz.append(("s_syd", t_ygn_spill))
            if t_rmsmm:
                wz.append(("s_pe", t_rmsmm))
            t_lz = em.dma("sync", zc.rearrange("p (h t) -> p h t", h=64),
                          d_z.rearrange("(h p) t -> p h t", p=128)[:, :, c * CL:(c + 1) * CL],
                          waits=wz)
            if c > 0:
                whl = [("s_syd", t_hb_spill)]
                if t_hlld:
                    whl.append(("s_v", t_hlld))
                t_lh = em.dma("scalar", hloc, d_hloc[(c - 1) * 128:c * 128, :],
                              waits=whl)
            else:
                t_lh = 0

            em.op("tensor", lambda e, c=c: e.matmul(
                pb7[:, 128:256], sb_bfm[:, c * CL:(c + 1) * CL],
                sb_cfm[:, c * CL:(c + 1) * CL], start=True, stop=True),
                waits=t_bcfm + [("s_v", t_dbc), ("s_v", t_rbv)] +
                ([("s_syd", t_hb_spill)] if c == 0 else []), inc="s_pe")
            t_g = C["s_pe"]
            em.op("vector", lambda e: e.tensor_copy(gtile, pb7[:, 128:256]),
                  waits=[("s_pe", t_g)], inc="s_v")
            t_gv = C["s_v"]

            def front(h):
                gi = c * NH + h
                par = h % 2
                bA = par
                slot = gi % NSLOT
                E = bfv(o_e + par * 128, (128, 128))
                E2 = bfv(o_e2 + par * 128, (128, 128))
                AR = bfv(o_ar2 + par * 128, (128, 128))
                MP = bfv(o_mp + slot * 128, (128, 128))
                CPP = bfv(o_cpp + slot * 128, (128, 128))
                HBT = bfv(o_hbt + slot * 128, (128, 128))

                bB = 2 + par
                wv = [("s_v", t_na), ("s_v", t_prep),
                      ("s_ac", gi_pA.get(gi - 2, 0))]
                if gi < 2:
                    wv.append(("s_v", t_chain_final))
                em.op("tensor", lambda e, h=h, c=c, bA=bA: e.matmul(
                    pb[bA][:, 0:128],
                    sb_i64[:, h:h + 1].to_broadcast([64, 128]),
                    sb_na[0:64, c * CL:(c + 1) * CL], start=True, stop=True),
                    waits=wv, inc="s_pe")
                t_pA = C["s_pe"]
                wvB = [("s_ac", gi_pB.get(gi - 2, 0))]
                if gi < 2:
                    wvB.append(("s_v", t_chain_final))
                em.op("tensor", lambda e, h=h, c=c, bB=bB: e.matmul(
                    pb[bB][:, 0:128],
                    sb_i64[:, h:h + 1].to_broadcast([64, 128]),
                    sb_na[0:64, c * CL:(c + 1) * CL], start=True, stop=False),
                    waits=wvB, inc=None)
                em.op("tensor", lambda e, bB=bB: e.matmul(
                    pb[bB][:, 0:128], sb_i128[:], sb_tril[:],
                    start=False, stop=True), inc="s_pe")
                t_pB = C["s_pe"]
                if c == 0:
                    em.op("scalar", lambda e, E2=E2, bA=bA: e.activation(
                        E2, pb[bA][:, 0:128], AF.Exp, scale=-1.0),
                        waits=[("s_pe", t_pA), ("s_v", gi_CPP.get(gi - 2, 0))],
                        inc="s_ac")
                else:
                    em.op("scalar", lambda e, E2=E2, bA=bA, c=c, h=h: e.activation(
                        E2, pb[bA][:, 0:128], AF.Exp, scale=-1.0,
                        bias=sb_nalb[:, (c - 1) * 64 + h:(c - 1) * 64 + h + 1]),
                        waits=[("s_pe", t_pA), ("s_v", gi_CPP.get(gi - 2, 0))],
                        inc="s_ac")
                gi_pA[gi] = C["s_ac"]
                em.op("scalar", lambda e, E=E, bB=bB, c=c, h=h: e.activation(
                    E, pb[bB][:, 0:128], AF.Exp, scale=-1.0,
                    bias=sb_natm[:, c * 64 + h:c * 64 + h + 1]),
                    waits=[("s_pe", t_pB), ("s_v", gi_MP.get(gi - 2, 0))],
                    inc="s_ac")
                gi_pB[gi] = C["s_ac"]
                t_e = C["s_ac"]
                em.op("vector", lambda e, MP=MP, E=E, c=c, h=h:
                      e.scalar_tensor_tensor(
                          MP, E, sb_dttm[:, c * 64 + h:c * 64 + h + 1], gtile,
                          ALU.mult, ALU.mult),
                      waits=[("s_ac", t_e), ("s_v", t_gv),
                             ("s_pe", gi_Y.get(gi - NSLOT, 0))], inc="s_v")
                gi_MP[gi] = C["s_v"]
                em.op("vector", lambda e, CPP=CPP, E2=E2, c=c: e.tensor_mul(
                    CPP, sb_cfm[:, c * CL:(c + 1) * CL], E2),
                    waits=[("s_ac", gi_pA[gi]),
                           ("s_pe", gi_Y.get(gi - NSLOT, 0))],
                    inc="s_v")
                gi_CPP[gi] = C["s_v"]
                wv = [("s_scd", t_hpeer), ("s_v", t_prep),
                      ("s_pe", gi_Y.get(gi - NSLOT, 0))]
                if c > 0:
                    wv.append(("s_scd", t_lh))
                if c == 0:
                    em.op("vector", lambda e, HBT=HBT, h=h: e.tensor_scalar(
                        HBT, hpeer[:, h * 128:(h + 1) * 128],
                        sb_gexp[:, h:h + 1], None, ALU.mult),
                        waits=wv, inc="s_v")
                else:
                    em.op("vector", lambda e, HBT=HBT, h=h, c=c:
                          e.scalar_tensor_tensor(
                              HBT, hpeer[:, h * 128:(h + 1) * 128],
                              sb_gexp[:, c * 64 + h:c * 64 + h + 1],
                              hloc[:, h * 128:(h + 1) * 128],
                              ALU.mult, ALU.add),
                          waits=wv, inc="s_v")
                gi_HBT[gi] = C["s_v"]

            def back(h):
                gi = c * NH + h
                slot = gi % NSLOT
                bY = 4 + h % 2
                MP = bfv(o_mp + slot * 128, (128, 128))
                CPP = bfv(o_cpp + slot * 128, (128, 128))
                HBT = bfv(o_hbt + slot * 128, (128, 128))
                wv = [("s_v", gi_MP[gi]), ("s_v", gi_HBT[gi]), ("s_scd", t_lx),
                      ("s_v", t_idsc), ("s_ac", gi_Yev.get(gi - 2, 0))]
                if gi < 2:
                    wv.append(t_conv_final[h % 2])
                em.op("tensor", lambda e, h=h, bY=bY, MP=MP, xtm=xtm: e.matmul(
                    pb[bY][:, 0:128], xtm[:, h * 128:(h + 1) * 128], MP,
                    start=True, stop=False), waits=wv, inc=None)
                em.op("tensor", lambda e, bY=bY, HBT=HBT, CPP=CPP: e.matmul(
                    pb[bY][:, 0:128], HBT, CPP, start=False, stop=False),
                    waits=[("s_v", gi_CPP[gi])], inc=None)
                em.op("tensor", lambda e, h=h, bY=bY, xtm=xtm: e.matmul(
                    pb[bY][:, 0:128], xtm[:, h * 128:(h + 1) * 128],
                    idsc[:, h * 128:(h + 1) * 128], start=False, stop=True),
                    inc="s_pe")
                gi_Y[gi] = C["s_pe"]
                wy = [("s_pe", gi_Y[gi])]
                if h == 0:
                    wy.append(("s_syd", t_ygn_spill))
                    wy.append(("s_pe", t_rmsmm))
                em.op("scalar", lambda e, h=h, bY=bY: e.activation(
                    ych[:, h * 128:(h + 1) * 128], pb[bY][:, 0:128], AF.Copy),
                    waits=wy, inc="s_ac")
                gi_Yev[gi] = C["s_ac"]

            for i in range(NH + KLAG):
                if i < NH:
                    front(i)
                if i >= KLAG:
                    back(i - KLAG)
            t_ally = C["s_ac"]
            t_hlld = C["s_v"]
            t_xtm2[slot2] = ("s_pe", C["s_pe"])

            em.op("vector", lambda e, zc=zc, ych=ych: e.tensor_mul(ych, ych, zc),
                  waits=[("s_ac", t_ally), ("s_syd", t_lz)], inc="s_v")
            t_yg = C["s_v"]
            em.op("scalar", lambda e, zc=zc, ych=ych: e.activation(
                zc, ych, AF.Square), waits=[("s_v", t_yg)], inc="s_ac")
            t_sq2 = C["s_ac"]
            for h in range(NH):
                em.op("tensor", lambda e, h=h, zc=zc, k0=(h == 0),
                      k1=(h == NH - 1): e.matmul(
                          pb7[0:1, 0:128], sb_onescolbf[:],
                          zc[:, h * 128:(h + 1) * 128], start=k0, stop=k1),
                      waits=([("s_ac", t_sq2), ("s_v", t_gv), ("s_v", t_rbv)]
                             if h == 0 else ()),
                      inc=("s_pe" if h == NH - 1 else None))
            t_rmsmm = C["s_pe"]
            em.op("vector", lambda e: e.tensor_copy(sb_row[0:1, 0:128],
                  pb7[0:1, 0:128]), waits=[("s_pe", t_rmsmm)], inc="s_v",
                  serial=True)
            em.op("scalar", lambda e: e.activation(sb_row[0:1, 0:128],
                  sb_row[0:1, 0:128], AF.Ln, scale=1.0 / DSSM,
                  bias=sb_cvals[0:1, 0:1]),
                  waits=[("s_v", C["s_v"])], inc="s_ac", serial=True)
            em.op("scalar", lambda e: e.activation(sb_row[0:1, 0:128],
                  sb_row[0:1, 0:128], AF.Exp, scale=-0.5), inc="s_ac",
                  serial=True)
            t_rr = C["s_ac"]
            em.op("tensor", lambda e: e.matmul(pb7[:, 128:256],
                  sb_onesrowf[0:1, 0:128], sb_row[0:1, 0:128],
                  start=True, stop=True),
                  waits=[("s_ac", t_rr), ("s_v", t_gv)], inc="s_pe")
            t_rbm = C["s_pe"]
            em.op("vector", lambda e: e.tensor_copy(rb2, pb7[:, 128:256]),
                  waits=[("s_pe", t_rbm)], inc="s_v")
            t_rbv = C["s_v"]
            for h in range(NH):
                em.op("vector", lambda e, h=h, zc=zc, ych=ych:
                      e.scalar_tensor_tensor(
                          zc[:, h * 128:(h + 1) * 128],
                          ych[:, h * 128:(h + 1) * 128],
                          sb_normw[:, h:h + 1], rb2, ALU.mult, ALU.mult),
                      waits=([("s_v", t_rbv)] if h == 0 else ()), inc="s_v")
            t_ygnv = C["s_v"]
            t_ygn_spill = em.dma(
                "sync",
                d_ygn.rearrange("(h p) t -> p h t", p=128)[:, :, c * CL:(c + 1) * CL],
                zc.rearrange("p (h t) -> p h t", h=64),
                waits=[("s_v", t_ygnv)])
        t_p2 = em.cnt["s_syd"]
        ar.pop()

        BF = em.bar()
        ar.push()
        o_ygk = ar.take(32 * TPC)
        o_wo = ar.take(2 * 32 * 128)
        o_fst = ar.take(2 * TPC)
        o_fpt = ar.take(2 * 2 * TPC)
        ygk = bfv(o_ygk, (128, 32 * TPC))

        WOO = [0]
        t_ldy = 0
        t_attn_sp = [0, 0]
        t_part_sp = [0, 0]
        t_part_ld = [0, 0]
        t_fevac = {i: ("s_v", 0) for i in (2, 3, 4, 5)}
        t_kh_done = 0
        t_wslf = [None, None]
        for kh in range(2):
            wy = list(BF)
            if kh == 1:
                wy.append(("s_pe", t_kh_done))
            t_ldy = em.dma("sync", ygk.rearrange("p (a t) -> p a t", a=32),
                           d_ygn.rearrange("(a p) t -> p a t", p=128)[:, kh * 32:(kh + 1) * 32, :],
                           waits=wy)
            for dm in range(NDM):
                slot2 = dm % 2
                wsl = bfv(o_wo + slot2 * 32 * 128, (128, 32 * 128))
                w = list(BF) if t_wslf[slot2] is None else [t_wslf[slot2]]
                t_w = em.dma("gpsimd", wsl, woutp[:, WOO[0]:WOO[0] + 32 * 128],
                             waits=w)
                WOO[0] += 32 * 128
                fpt = f32v(o_fpt + slot2 * 2 * TPC, (128, TPC))
                fst = bfv(o_fst + slot2 * TPC, (128, TPC))
                if kh == 1:
                    wp = [("s_syd", t_part_sp[0]), ("s_syd", t_part_sp[1])]
                    if t_part_ld[slot2]:
                        wp.append(("s_v", t_part_ld[slot2]))
                    t_lp = em.dma("scalar", fpt, d_part[dm * 128:(dm + 1) * 128, :],
                                  waits=wp)
                toks = []
                for half in range(2):
                    bank = 2 + pe_rot4[(2 * dm + half) % 4] - 2
                    bank = pe_rot4[(2 * dm + half) % 4]
                    wv = [("s_gpd", t_w), ("s_syd", t_ldy), t_fevac[bank]]
                    for kt in range(32):
                        em.op("tensor", lambda e, ps=pb[bank],
                              lw=wsl[:, kt * 128:(kt + 1) * 128],
                              rhs=ygk[:, kt * TPC + half * 512: kt * TPC + (half + 1) * 512],
                              k0=(kt == 0), k1=(kt == 31):
                              e.matmul(ps[:, :], lw, rhs, start=k0, stop=k1),
                              waits=(wv if kt == 0 else ()),
                              inc=("s_pe" if kt == 31 else None))
                    t_pe = C["s_pe"]
                    if kh == 0:
                        w2 = [("s_pe", t_pe)]
                        if half == 0 and t_part_sp[slot2]:
                            w2.append(("s_syd", t_part_sp[slot2]))
                        em.op("vector", lambda e, o=fpt[:, half * 512:(half + 1) * 512],
                              ps=pb[bank]: e.tensor_copy(o, ps[:, :]),
                              waits=w2, inc="s_v")
                    else:
                        w2 = [("s_pe", t_pe), ("s_scd", t_lp)]
                        if half == 0 and t_attn_sp[slot2]:
                            w2.append(("s_syd", t_attn_sp[slot2]))
                        em.op("vector", lambda e, o=fst[:, half * 512:(half + 1) * 512],
                              ps=pb[bank], pr=fpt[:, half * 512:(half + 1) * 512]:
                              e.tensor_add(o, ps[:, :], pr), waits=w2, inc="s_v")
                    t_fevac[bank] = ("s_v", C["s_v"])
                    toks.append(C["s_v"])
                if kh == 0:
                    t_part_sp[slot2] = em.dma(
                        "sync", d_part[dm * 128:(dm + 1) * 128, :], fpt,
                        waits=[("s_v", toks[-1])])
                else:
                    t_part_ld[slot2] = toks[-1]
                    t_attn_sp[slot2] = em.dma(
                        "sync", d_attn[dm * 128:(dm + 1) * 128, :], fst,
                        waits=[("s_v", toks[-1])])
                t_wslf[slot2] = ("s_v", toks[-1])
            t_kh_done = C["s_pe"]
        t_f = em.cnt["s_syd"]
        ar.pop()

        BG = em.bar()
        ar.push()
        o_xf = ar.take(NKT * TPC)
        o_gst = ar.take(4 * TPC)
        o_gmb = ar.take(8 * TPC)
        xf = bfv(o_xf, (128, NKT * TPC))
        m2b = f32v(o_gmb, (128, TPC))
        r2b = f32v(o_gmb + 2 * TPC, (128, TPC))
        m3b = f32v(o_gmb + 4 * TPC, (128, TPC))
        r3b = f32v(o_gmb + 6 * TPC, (128, TPC))

        t_gpe = [0, 0]
        for dm in range(NDM):
            slot2 = dm % 2
            ast = bfv(o_gst + slot2 * TPC, (128, TPC))
            sqt = bfv(o_gst + (2 + slot2) * TPC, (128, TPC))
            w = list(BG)
            if t_gpe[slot2]:
                w.append(("s_pe", t_gpe[slot2]))
            t_la = em.dma("scalar", ast, d_attn[dm * 128:(dm + 1) * 128, :],
                          waits=w)
            em.op("scalar", lambda e, a=sqt, b=ast: e.activation(a, b, AF.Square),
                  waits=[("s_scd", t_la)], inc="s_ac")
            t_aq = C["s_ac"]
            t_gpe[slot2] = emit_stats_tile(
                ast, sqt, dm == 0, dm == NDM - 1,
                extra_wait_sum=[("s_scd", t_la)] + (list(BG) if dm == 0 else []),
                extra_wait_sq=[("s_ac", t_aq)])
        t_gstat = C["s_pe"]
        t_ln2 = finalize_stats(m2b, r2b, t_gstat, H)

        t_s2sp = [0, 0]
        for dm in range(NDM):
            slot2 = dm % 2
            ast = bfv(o_gst + slot2 * TPC, (128, TPC))
            xst = bfv(o_gst + (2 + slot2) * TPC, (128, TPC))
            w = [("s_v", t_ln2), ("s_pe", t_gpe[slot2])]
            if t_s2sp[slot2]:
                w.append(("s_syd", t_s2sp[slot2]))
            t_la = em.dma("scalar", ast, d_attn[dm * 128:(dm + 1) * 128, :],
                          waits=w)
            t_lx2 = em.dma("gpsimd", xst, xT[dm * 128:(dm + 1) * 128, :],
                           waits=w)
            em.op("vector", lambda e, a=ast: e.tensor_sub(a, a, m2b),
                  waits=[("s_scd", t_la), ("s_v", t_ln2)], inc="s_v")
            em.op("vector", lambda e, a=ast: e.tensor_mul(a, a, r2b), inc="s_v")
            em.op("vector", lambda e, a=ast, b=xst: e.tensor_add(a, a, b),
                  waits=[("s_gpd", t_lx2)], inc="s_v")
            t_s2 = C["s_v"]
            em.op("scalar", lambda e, a=xst, b=ast: e.activation(a, b, AF.Square),
                  waits=[("s_v", t_s2)], inc="s_ac")
            t_aq = C["s_ac"]
            t_gpe[slot2] = emit_stats_tile(
                ast, xst, dm == 0, dm == NDM - 1,
                extra_wait_sum=[("s_v", t_s2)],
                extra_wait_sq=[("s_ac", t_aq)])
            t_s2sp[slot2] = em.dma("sync", d_s2[dm * 128:(dm + 1) * 128, :], ast,
                                   waits=[("s_pe", t_gpe[slot2]),
                                          ("s_ac", t_aq)])
        t_s2stat = C["s_pe"]
        t_s2done = em.cnt["s_syd"]
        t_ln3i = finalize_stats(m3b, r3b, t_s2stat, H)

        t_gl2 = [0, 0]
        for dm in range(NDM):
            slot2 = dm % 2
            xft = xf[:, dm * TPC:(dm + 1) * TPC]
            w = [("s_v", t_ln3i), ("s_syd", t_s2done)]
            ast = bfv(o_gst + slot2 * TPC, (128, TPC))
            if t_gl2[slot2]:
                w.append(("s_v", t_gl2[slot2]))
            t_ls2 = em.dma("scalar", ast, d_s2[dm * 128:(dm + 1) * 128, :],
                           waits=w)
            em.op("vector", lambda e, a=xft, b=ast: e.tensor_sub(a, b, m3b),
                  waits=[("s_scd", t_ls2)], inc="s_v")
            em.op("vector", lambda e, a=xft: e.tensor_mul(a, a, r3b), inc="s_v")
            t_gl2[slot2] = C["s_v"]
        t_xf = C["s_v"]
        if debug:
            em.dma("sync", d_dbgxf.rearrange("(a p) t -> p a t", p=128),
                   xf.rearrange("p (a t) -> p a t", a=NKT),
                   waits=[("s_v", t_xf)])

        o_hf = ar.take(TPC)
        o_wbc = ar.take(TPC)
        o_ltm = ar.take(2 * 64)
        o_lfm = ar.take(2 * TPC)
        o_wtm = ar.take(64)
        o_wfm = ar.take(TPC)
        o_apk = ar.take(32 * 128)
        o_bmp = ar.take(H)
        o_wrp = ar.take(NKT * 8)
        o_wbsl = ar.take(2 * 32 * 128)
        o_fst2 = ar.take(2 * TPC)
        o_ost = ar.take(4 * TPC)
        hfm = bfv(o_hf, (128, TPC))
        wbc = bfv(o_wbc, (128, TPC))
        ltm = f32v(o_ltm, (128, 64))
        lfm = f32v(o_lfm, (8, TPC))
        wtm = bfv(o_wtm, (128, 8))
        wfm = bfv(o_wfm, (8, TPC))
        apk = bfv(o_apk, (128, 32 * 128))
        bmp = bfv(o_bmp, (128, H))
        wrp = bfv(o_wrp, (128, NKT * 8))

        t_apk = em.dma("gpsimd", apk, apack, waits=BG)
        t_bmp = em.dma("gpsimd", bmp, bmpack, waits=BG)
        t_wrp = em.dma("gpsimd", wrp, wrtp, waits=BG)

        for half in range(2):
            bank = half
            for kt in range(32):
                em.op("tensor", lambda e, ps=pb[bank],
                      lw=apk[:, kt * 128:(kt + 1) * 128],
                      rhs=xf[:, kt * TPC + half * 512: kt * TPC + (half + 1) * 512],
                      k0=(kt == 0), k1=(kt == 31):
                      e.matmul(ps[:, :], lw, rhs, start=k0, stop=k1),
                      waits=([("s_gpd", t_apk), ("s_v", t_xf),
                              ("s_v", t_ln3i)] if kt == 0 else ()),
                      inc=("s_pe" if kt == 31 else None))
            t_h = C["s_pe"]
            em.op("vector", lambda e, half=half, bank=bank: e.tensor_copy(
                hfm[:, half * 512:(half + 1) * 512], pb[bank][:, :]),
                waits=[("s_pe", t_h)], inc="s_v")
        t_hfm = C["s_v"]
        for half in range(2):
            bank = half
            for kt in range(32):
                em.op("tensor", lambda e, ps=pb[bank],
                      lw=wrp[:, kt * 8:(kt + 1) * 8],
                      rhs=xf[:, kt * TPC + half * 512: kt * TPC + (half + 1) * 512],
                      k0=(kt == 0), k1=(kt == 31):
                      e.matmul(ps[0:8, :], lw, rhs, start=k0, stop=k1),
                      waits=([("s_gpd", t_wrp), ("s_v", t_hfm)] if kt == 0 else ()),
                      inc=("s_pe" if kt == 31 else None))
            t_l = C["s_pe"]
            em.op("vector", lambda e, half=half, bank=bank: e.tensor_copy(
                lfm[0:8, half * 512:(half + 1) * 512], pb[bank][0:8, :]),
                waits=[("s_pe", t_l)], inc="s_v")
        t_lfm = C["s_v"]

        t_pb7r = t_rbv
        for tt in range(8):
            em.op("tensor", lambda e, tt=tt: e.transpose(
                pb7[:, 0:8], lfm[0:8, tt * 128:(tt + 1) * 128],
                sb_i128[0:8, 0:8]),
                waits=[("s_v", t_lfm), ("s_v", t_pb7r)], inc="s_pe")
            t_tr = C["s_pe"]
            em.op("vector", lambda e: e.tensor_copy(ltm[:, 0:8], pb7[:, 0:8]),
                  waits=[("s_pe", t_tr)], inc="s_v", serial=True)
            em.op("vector", lambda e: e.reduce_max(ltm[:, 8:9], ltm[:, 0:8],
                  axis=mybir.AxisListType.X), inc="s_v", serial=True)
            em.op("vector", lambda e: e.tensor_scalar(ltm[:, 48:56],
                  ltm[:, 0:8], ltm[:, 8:9], None, ALU.subtract), inc="s_v",
                  serial=True)
            em.op("scalar", lambda e: e.activation(ltm[:, 16:24], ltm[:, 48:56],
                  AF.Exp), waits=[("s_v", C["s_v"])], inc="s_ac", serial=True)
            t_e3 = C["s_ac"]
            em.op("vector", lambda e: e.tensor_scalar(ltm[:, 24:32],
                  ltm[:, 16:24], 1.0, None, ALU.is_ge),
                  waits=[("s_ac", t_e3)], inc="s_v", serial=True)
            em.op("vector", lambda e: e.tensor_mul(ltm[:, 32:40], ltm[:, 16:24],
                  ltm[:, 24:32]), inc="s_v", serial=True)
            em.op("vector", lambda e: e.tensor_sub(ltm[:, 32:40], ltm[:, 16:24],
                  ltm[:, 32:40]), inc="s_v", serial=True)
            em.op("vector", lambda e: e.reduce_max(ltm[:, 9:10], ltm[:, 32:40],
                  axis=mybir.AxisListType.X), inc="s_v", serial=True)
            em.op("vector", lambda e: e.tensor_scalar(ltm[:, 24:32],
                  ltm[:, 16:24], ltm[:, 9:10], None, ALU.is_ge), inc="s_v",
                  serial=True)
            em.op("vector", lambda e: e.tensor_mul(ltm[:, 32:40], ltm[:, 16:24],
                  ltm[:, 24:32]), inc="s_v", serial=True)
            em.op("vector", lambda e: e.tensor_scalar(ltm[:, 9:10], ltm[:, 9:10],
                  1.0, None, ALU.add), inc="s_v", serial=True)
            em.op("vector", lambda e: e.reciprocal(ltm[:, 10:11], ltm[:, 9:10]),
                  inc="s_v", serial=True)
            em.op("vector", lambda e: e.tensor_scalar(wtm[:, 0:8], ltm[:, 32:40],
                  ltm[:, 10:11], None, ALU.mult), inc="s_v", serial=True)
            t_w2 = C["s_v"]
            em.op("tensor", lambda e: e.transpose(pbf[0:8, 0:128], wtm[:, 0:8],
                  sb_identbf[:]), waits=[("s_v", t_w2), ("s_v", t_tr_evac)],
                  inc="s_pe")
            t_tw = C["s_pe"]
            em.op("vector", lambda e, tt=tt: e.tensor_copy(
                wfm[0:8, tt * 128:(tt + 1) * 128], pbf[0:8, 0:128]),
                waits=[("s_pe", t_tw)], inc="s_v", serial=True)
            t_pb7r = C["s_v"]
            t_tr_evac = C["s_v"]
        t_wfm = C["s_v"]
        for half in range(2):
            bank = half
            em.op("tensor", lambda e, half=half, bank=bank: e.matmul(
                pb[bank][:, :], sb_rep2[0:8, :],
                wfm[0:8, half * 512:(half + 1) * 512], start=True, stop=True),
                waits=[("s_v", t_wfm)], inc="s_pe")
            t_wb = C["s_pe"]
            em.op("scalar", lambda e, half=half, bank=bank: e.activation(
                wbc[:, half * 512:(half + 1) * 512], pb[bank][:, :], AF.Copy),
                waits=[("s_pe", t_wb)], inc="s_ac")
        t_wbc = C["s_ac"]
        em.op("vector", lambda e: e.tensor_mul(hfm, hfm, wbc),
              waits=[("s_ac", t_wbc), ("s_v", t_hfm)], inc="s_v")
        t_hw = C["s_v"]
        if debug:
            em.dma("sync", d_dbgw[:, 0:TPC], wbc, waits=[("s_v", t_hw)])
            em.dma("sync", d_dbgw[:, TPC:2 * TPC], hfm, waits=[("s_v", t_hw)])


        t_wslh = [None, None]
        t_hev = {0: ("s_v", t_hw), 1: ("s_v", t_hw)}
        t_ffn_sp = [0, 0]
        t_hpe = [0, 0]
        for dm in range(NDM):
            slot2 = dm % 2
            wsl = bfv(o_wbsl + slot2 * 32 * 128, (128, 32 * 128))
            w = [("s_v", t_xf)] if t_wslh[slot2] is None else [t_wslh[slot2]]
            t_w = em.dma("gpsimd", wsl, wbp[:, dm * 32 * 128:(dm + 1) * 32 * 128],
                         waits=w)
            fst = bfv(o_fst2 + slot2 * TPC, (128, TPC))
            sqt = bfv(o_gst + slot2 * TPC, (128, TPC))
            toks = []
            for half in range(2):
                bank = half
                wv = [("s_gpd", t_w), ("s_v", t_hw), ("s_gpd", t_bmp),
                      t_hev[bank]]
                for kt in range(32):
                    em.op("tensor", lambda e, ps=pb[bank],
                          lw=wsl[:, kt * 128:(kt + 1) * 128],
                          rhs=xf[:, kt * TPC + half * 512: kt * TPC + (half + 1) * 512],
                          k0=(kt == 0):
                          e.matmul(ps[:, :], lw, rhs, start=k0, stop=False),
                          waits=(wv if kt == 0 else ()), inc=None)
                em.op("tensor", lambda e, ps=pb[bank], dm=dm: e.matmul(
                    ps[:, :], sb_biasbf[0:1, dm * 128:(dm + 1) * 128],
                    sb_onesrowbf[0:1, 0:512], start=False, stop=False), inc=None)
                em.op("tensor", lambda e, ps=pb[bank], dm=dm, half=half: e.matmul(
                    ps[:, :], bmp[:, dm * 128:(dm + 1) * 128],
                    hfm[:, half * 512:(half + 1) * 512], start=False, stop=True),
                    inc="s_pe")
                t_pe = C["s_pe"]
                w2 = [("s_pe", t_pe)]
                if half == 0 and t_ffn_sp[slot2]:
                    w2.append(("s_syd", t_ffn_sp[slot2]))
                em.op("vector", lambda e, o=fst[:, half * 512:(half + 1) * 512],
                      ps=pb[bank]: e.tensor_copy(o, ps[:, :]), waits=w2,
                      inc="s_v")
                t_hev[bank] = ("s_v", C["s_v"])
                toks.append(C["s_v"])
            t_ev3 = toks[-1]
            em.op("scalar", lambda e, a=sqt, b=fst: e.activation(a, b, AF.Square),
                  waits=[("s_v", t_ev3), ("s_pe", t_hpe[slot2])], inc="s_ac")
            t_aq = C["s_ac"]
            t_hpe[slot2] = emit_stats_tile(
                fst, sqt, dm == 0, dm == NDM - 1,
                extra_wait_sum=[("s_v", t_ev3)] + ([("s_ac", t_wbc)] if dm == 0 else []),
                extra_wait_sq=[("s_ac", t_aq)])
            t_wslh[slot2] = ("s_v", t_ev3)
            t_ffn_sp[slot2] = em.dma("sync", d_ffn[dm * 128:(dm + 1) * 128, :],
                                     fst, waits=[("s_pe", t_hpe[slot2]),
                                                 ("s_ac", t_aq)])
        t_ffnstat = C["s_pe"]
        t_ln3 = finalize_stats(m2b, r2b, t_ffnstat, H)

        t_old = [0, 0]
        for dm in range(NDM):
            slot2 = dm % 2
            fst = bfv(o_fst2 + slot2 * TPC, (128, TPC))
            s2t = bfv(o_gst + (2 + slot2) * TPC, (128, TPC))
            ost = f32v(o_ost + slot2 * 2 * TPC, (128, TPC))
            w = [("s_v", t_ln3), ("s_syd", t_ffn_sp[0]), ("s_syd", t_ffn_sp[1])]
            if t_old[slot2]:
                w.append(("s_syd", t_old[slot2]))
            t_lf = em.dma("scalar", fst, d_ffn[dm * 128:(dm + 1) * 128, :],
                          waits=w)
            t_ls3 = em.dma("sync", s2t, d_s2[dm * 128:(dm + 1) * 128, :],
                           waits=w)
            em.op("vector", lambda e, o=ost, b=fst: e.tensor_sub(o, b, m2b),
                  waits=[("s_scd", t_lf), ("s_v", t_ln3)], inc="s_v")
            em.op("vector", lambda e, o=ost: e.tensor_mul(o, o, r2b), inc="s_v")
            em.op("vector", lambda e, o=ost, b=s2t: e.tensor_add(o, o, b),
                  waits=[("s_syd", t_ls3)], inc="s_v")
            t_o = C["s_v"]
            t_old[slot2] = em.dma("sync", out_T[dm * 128:(dm + 1) * 128, :], ost,
                                  waits=[("s_v", t_o)])
        t_final = em.cnt["s_syd"]

        if debug:
            for nm, (dst, src) in dbg.items():
                em.dma("sync", dst, src, waits=[("s_syd", t_final)])
            t_final = em.cnt["s_syd"]

        em.op("sync", lambda e: e.wait_ge(sems["s_syd"], t_final))

        em.check_deadlock()

        with nc.Block() as block:
            @block.tensor
            def _(te):
                em.replay_one("tensor", te, sems)

            @block.vector
            def _(v):
                em.replay_one("vector", v, sems)

            @block.scalar
            def _(s):
                em.replay_one("scalar", s, sems)

            @block.gpsimd
            def _(gp):
                em.replay_one("gpsimd", gp, sems)

            @block.sync
            def _(sy):
                em.replay_one("sync", sy, sems)

    return nc


_CACHE = {}


def _pack_weights(W_in, conv_w, dt_bias, A_log, D_param, norm_weight, W_out,
                  Wb, bias, Wr, A, Bm):
    f = np.float32
    W_in = np.asarray(W_in, f)

    def pack(Wsub, width=128):
        nt = Wsub.shape[1] // width
        out = np.empty((128, nt * NKT * width), f)
        for t in range(nt):
            blk = Wsub[:, t * width:(t + 1) * width].reshape(NKT, 128, width)
            out[:, t * NKT * width:(t + 1) * NKT * width] = \
                blk.transpose(1, 0, 2).reshape(128, NKT * width)
        return out

    winp = np.concatenate([
        pack(W_in[:, DSSM:DSSM + CONV]),
        pack(W_in[:, DSSM + CONV:], 64),
        pack(W_in[:, :DSSM]),
    ], axis=1)

    W_out = np.asarray(W_out, f)
    parts = []
    for kh in range(2):
        sub = W_out[kh * 4096:(kh + 1) * 4096, :]
        for dm in range(NDM):
            blk = sub[:, dm * 128:(dm + 1) * 128].reshape(32, 128, 128)
            parts.append(blk.transpose(1, 0, 2).reshape(128, 32 * 128))
    woutp = np.concatenate(parts, axis=1)

    WbT = np.ascontiguousarray(np.asarray(Wb, f).T)
    parts = []
    for dm in range(NDM):
        blk = WbT[:, dm * 128:(dm + 1) * 128].reshape(NKT, 128, 128)
        parts.append(blk.transpose(1, 0, 2).reshape(128, NKT * 128))
    wbp = np.concatenate(parts, axis=1)

    A2T = np.ascontiguousarray(np.asarray(A, f).reshape(128, H).T)
    apack = A2T.reshape(NKT, 128, 128).transpose(1, 0, 2).reshape(128, NKT * 128)
    bmpack = np.ascontiguousarray(
        np.asarray(Bm, f).transpose(0, 2, 1).reshape(128, H))
    WrT = np.ascontiguousarray(np.asarray(Wr, f).T)
    wrtp = WrT.reshape(NKT, 128, 8).transpose(1, 0, 2).reshape(128, NKT * 8)

    cw = np.asarray(conv_w, f)
    convdiag = np.zeros((128, NXT * 4 * 128), f)
    idx = np.arange(128)
    for i in range(NXT):
        for k in range(4):
            convdiag[idx, (i * 4 + k) * 128 + idx] = cw[i * 128:(i + 1) * 128, k]

    rep2m = np.zeros((8, 128), f)
    for e in range(8):
        rep2m[e, e * 16:(e + 1) * 16] = 2.0

    return dict(
        winp=np.ascontiguousarray(winp),
        woutp=np.ascontiguousarray(woutp),
        wbp=np.ascontiguousarray(wbp),
        apack=np.ascontiguousarray(apack),
        bmpack=bmpack,
        wrtp=np.ascontiguousarray(wrtp),
        convdiag=convdiag,
        i128=np.eye(128, dtype=f),
        i64=np.eye(64, dtype=f),
        trilbig=np.tril(np.full((128, 128), BIGV, f), -1),
        triu01=np.triu(np.ones((128, 128), f)),
        onescol=np.ones((128, 1), f),
        onesrow=np.ones((1, 512), f),
        dtbias=np.asarray(dt_bias, f).reshape(64, 1),
        alog=np.asarray(A_log, f).reshape(64, 1),
        drow=np.asarray(D_param, f).reshape(1, 64),
        normw=np.ascontiguousarray(
            np.asarray(norm_weight, f).reshape(64, 128).T),
        biasrow=np.asarray(bias, f).reshape(1, H),
        rep2=rep2m,
        cvals=np.ascontiguousarray(
            np.stack([np.full(128, EPS, f), np.ones(128, f)], axis=1)),
    )


def kernel(L0_attn_hidden_states, L0_attn_W_in, L0_attn_conv_w, L0_attn_dt_bias,
           L0_attn_A_log, L0_attn_D_param, L0_attn_norm_weight, L0_attn_W_out,
           L0_ffn_Wb, L0_ffn_bias, L0_ffn_Wr, L0_ffn_A, L0_ffn_Bm):
    x = np.asarray(L0_attn_hidden_states, np.float32)
    com = _pack_weights(L0_attn_W_in, L0_attn_conv_w, L0_attn_dt_bias,
                        L0_attn_A_log, L0_attn_D_param, L0_attn_norm_weight,
                        L0_attn_W_out, L0_ffn_Wb, L0_ffn_bias, L0_ffn_Wr,
                        L0_ffn_A, L0_ffn_Bm)
    if "nc" not in _CACHE:
        _CACHE["nc"] = build_program(debug=False)
    nc = _CACHE["nc"]

    in_maps = []
    for c in range(8):
        b, half = c // 2, c % 2
        m = dict(com)
        m["xT"] = np.ascontiguousarray(x[b, half * TPC:(half + 1) * TPC, :].T)
        if half == 0:
            m["xhT"] = np.zeros((H, 4), np.float32)
        else:
            m["xhT"] = np.ascontiguousarray(x[b, TPC - 4:TPC, :].T)
        m["gcol"] = np.full((128, 1), float(half), np.float32)
        in_maps.append(m)

    res = run_bass_kernel_spmd(nc, in_maps, core_ids=list(range(8)))
    out = np.empty((B, S, H), np.float32)
    for c in range(8):
        b, half = c // 2, c % 2
        out[b, half * TPC:(half + 1) * TPC, :] = res.results[c]["out_T"].T
    return out


# version 09
# speedup vs baseline: 1.1189x; 1.1189x over previous
import numpy as np
from contextlib import ExitStack

import concourse.bass as bass
import concourse.mybir as mybir
from concourse.bass_utils import run_bass_kernel_spmd

F32 = mybir.dt.float32
BF16 = mybir.dt.bfloat16
AF = mybir.ActivationFunctionType
ALU = mybir.AluOpType

B, S, H = 4, 2048, 4096
TPC = 1024
NH, P, N, K = 64, 128, 128, 4
DSSM = NH * P
CONV = DSSM + 2 * N
PROJ = DSSM + CONV + NH
NKT = H // 128
NXT = CONV // 128
NZT = DSSM // 128
NDM = H // 128
NC = 8
CL = 128
EPS = 1e-5
BIGV = 1e30

ENGINES = ("tensor", "vector", "scalar", "gpsimd", "sync")
SEMS = ("s_pe", "s_v", "s_ac", "s_gp", "s_gpd", "s_syd", "s_scd", "s_cc")


class Emitter:
    def __init__(self):
        self.q = {e: [] for e in ENGINES}
        self.cnt = {s: 0 for s in SEMS}

    ENG_SEM = {"tensor": "s_pe", "vector": "s_v", "scalar": "s_ac",
               "gpsimd": "s_gp"}

    def op(self, eng, fn, waits=(), inc=None, amt=1, serial=False):
        waits = list(waits)
        if serial:
            own = self.ENG_SEM[eng]
            waits.append((own, self.cnt[own]))
        waits = tuple((s, v) for (s, v) in waits if v)
        self.q[eng].append((fn, waits, inc, amt))
        if inc is not None:
            self.cnt[inc] += amt
            return self.cnt[inc]
        return None

    def dma(self, queue, out, in_, waits=()):
        sem = {"sync": "s_syd", "scalar": "s_scd", "gpsimd": "s_gpd"}[queue]
        return self.op(queue, lambda e: e.dma_start(out=out, in_=in_),
                       waits=waits, inc=sem, amt=16)

    def bar(self, skip=("s_cc",)):
        return [(s, v) for s, v in self.cnt.items() if s not in skip and v]

    def check_deadlock(self):
        ptr = {e: 0 for e in ENGINES}
        cnt = {s: 0 for s in SEMS}
        progress = True
        while progress:
            progress = False
            for e in ENGINES:
                while ptr[e] < len(self.q[e]):
                    fn, waits, inc, amt = self.q[e][ptr[e]]
                    if all(cnt[s] >= v for (s, v) in waits):
                        if inc is not None:
                            cnt[inc] += amt
                        ptr[e] += 1
                        progress = True
                    else:
                        break
        stuck = {e: ptr[e] for e in ENGINES if ptr[e] < len(self.q[e])}
        if stuck:
            lines = []
            for e, i in stuck.items():
                fn, waits, inc, amt = self.q[e][i]
                missing = [(s, v, cnt[s]) for (s, v) in waits if cnt[s] < v]
                lines.append(f"{e}@{i}/{len(self.q[e])} missing={missing}")
            raise RuntimeError("DEADLOCK in schedule:\n" + "\n".join(lines))

    def replay_one(self, ename, eng, sems):
        last_wait = {}
        for fn, waits, inc, amt in self.q[ename]:
            for (s, v) in waits:
                if last_wait.get(s, -1) < v:
                    eng.wait_ge(sems[s], v)
                    last_wait[s] = v
            inst = fn(eng)
            if inc is not None and inst is not None:
                if inc == "s_cc":
                    inst.then_inc(sems[inc])
                else:
                    inst.then_inc(sems[inc], amt)

    def n_ops(self):
        return {e: len(self.q[e]) for e in ENGINES}


class Arena:
    def __init__(self, nelem):
        self.n = nelem
        self.off = 0
        self.marks = []
        self.peak = 0

    def take(self, nelem, align=64):
        o = (self.off + align - 1) // align * align
        self.off = o + nelem
        self.peak = max(self.peak, self.off)
        assert self.off <= self.n, f"SBUF arena overflow: {self.off} > {self.n}"
        return o

    def push(self):
        self.marks.append(self.off)

    def pop(self):
        self.off = self.marks.pop()


def build_program(debug=False):
    nc = bass.Bass("TRN2", target_bir_lowering=False, debug=False)
    em = Emitter()
    C = em.cnt

    def din(name, shape):
        return nc.dram_tensor(name, list(shape), F32, kind="ExternalInput").ap()

    xT = din("xT", (H, TPC))
    xhT = din("xhT", (H, 4))
    winp = din("winp", (128, NKT * PROJ))
    woutp = din("woutp", (128, DSSM * H // 128))
    wbp = din("wbp", (128, NKT * H))
    apack = din("apack", (128, NKT * 128))
    bmpack = din("bmpack", (128, H))
    wrtp = din("wrtp", (128, NKT * 8))
    convdiag = din("convdiag", (128, NXT * 4 * 128))
    i128 = din("i128", (128, 128))
    i64 = din("i64", (64, 64))
    trilbig = din("trilbig", (128, 128))
    triu01 = din("triu01", (128, 128))
    onescol = din("onescol", (128, 1))
    onesrow = din("onesrow", (1, 512))
    dtbias = din("dtbias", (64, 1))
    alog = din("alog", (64, 1))
    drow = din("drow", (1, 64))
    normw = din("normw", (128, 64))
    biasrow = din("biasrow", (1, H))
    gcol = din("gcol", (128, 1))
    rep2 = din("rep2", (8, 128))
    cvals = din("cvals", (128, 2))

    out_T = nc.dram_tensor("out_T", [H, TPC], F32, kind="ExternalOutput").ap()

    d_xbc = nc.dram_tensor("d_xbc", [CONV, TPC], BF16, kind="Internal").ap()
    d_xtm = nc.dram_tensor("d_xtm", [NC * 64 * 128, 128], BF16, kind="Internal").ap()
    d_z = nc.dram_tensor("d_z", [DSSM, TPC], BF16, kind="Internal").ap()
    d_hloc = nc.dram_tensor("d_hloc", [NC * 128, DSSM], BF16, kind="Internal").ap()
    d_ygn = nc.dram_tensor("d_ygn", [DSSM, TPC], BF16, kind="Internal").ap()
    d_attn = nc.dram_tensor("d_attn", [H, TPC], BF16, kind="Internal").ap()
    d_ffn = nc.dram_tensor("d_ffn", [H, TPC], BF16, kind="Internal").ap()
    d_s2 = nc.dram_tensor("d_s2", [H, TPC], BF16, kind="Internal").ap()
    d_part = nc.dram_tensor("d_part", [H, TPC], F32, kind="Internal").ap()
    cc_hin = nc.dram_tensor("cc_hin", [128, DSSM], BF16, kind="Internal").ap()
    cc_hout = nc.dram_tensor("cc_hout", [256, DSSM], BF16, kind="Internal").ap()

    d_dbgxf = nc.dram_tensor("d_dbgxf", [H, TPC], BF16, kind="Internal").ap()
    d_dbgw = nc.dram_tensor("d_dbgw", [128, 3 * TPC], BF16, kind="Internal").ap()
    dbg = {}
    if debug:
        for nm, src in [("xbc", d_xbc), ("z", d_z), ("xtm", d_xtm),
                        ("hloc", d_hloc), ("ygn", d_ygn), ("attn", d_attn),
                        ("ffn", d_ffn), ("hout", cc_hout), ("s2", d_s2),
                        ("xf", d_dbgxf), ("w", d_dbgw)]:
            dbg[nm] = (nc.dram_tensor("dbg_" + nm, list(src.shape), BF16,
                                      kind="ExternalOutput").ap(), src)

    with ExitStack() as ctx:
        BIGN = 74000
        big = ctx.enter_context(nc.sbuf_tensor("big", [128, BIGN], BF16))
        ar = Arena(BIGN)

        def bfv(off, shape):
            v = big[:, off:off + shape[1]]
            return v if shape[0] == 128 else v[0:shape[0], :]

        def f32v(off, shape):
            v = big[:, off:off + 2 * shape[1]].bitcast(F32)
            return v if shape[0] == 128 else v[0:shape[0], :]

        def sb(name, shape, dt):
            return ctx.enter_context(nc.sbuf_tensor(name, list(shape), dt))

        sb_i128 = sb("sb_i128", (128, 128), F32)
        sb_i64 = sb("sb_i64", (64, 64), F32)
        sb_tril = sb("sb_tril", (128, 128), F32)
        sb_triu = sb("sb_triu", (128, 128), BF16)
        sb_identbf = sb("sb_identbf", (128, 128), BF16)
        sb_onescolbf = sb("sb_onescolbf", (128, 1), BF16)
        sb_onesrowf = sb("sb_onesrowf", (1, 512), F32)
        sb_onesrowbf = sb("sb_onesrowbf", (1, 512), BF16)
        sb_dtb = sb("sb_dtb", (64, 1), F32)
        sb_alog = sb("sb_alog", (64, 1), F32)
        sb_drow = sb("sb_drow", (1, 64), F32)
        sb_normw = sb("sb_normw", (128, 64), F32)
        sb_biasbf = sb("sb_biasbf", (1, H), BF16)
        sb_g = sb("sb_g", (128, 1), F32)
        sb_rep2 = sb("sb_rep2", (8, 128), BF16)
        sb_cvals = sb("sb_cvals", (128, 2), F32)
        sb_dt = sb("sb_dt", (64, TPC), F32)
        sb_na = sb("sb_na", (64, TPC), F32)
        sb_sc1 = sb("sb_sc1", (64, TPC), F32)
        sb_sc2 = sb("sb_sc2", (64, TPC), F32)
        sb_natm = sb("sb_natm", (128, NC * 64), F32)
        sb_dttm = sb("sb_dttm", (128, NC * 64), F32)
        sb_nalb = sb("sb_nalb", (128, NC * 64), F32)
        sb_gexp = sb("sb_gexp", (128, NC * 64), F32)
        sb_lstep = sb("sb_lstep", (128, NC * 64), F32)
        sb_wdt = sb("sb_wdt", (128, NC * 64), F32)
        sb_bfm = sb("sb_bfm", (128, TPC), BF16)
        sb_cfm = sb("sb_cfm", (128, TPC), BF16)
        sb_btm = sb("sb_btm", (128, TPC), BF16)
        sb_expa = sb("sb_expa", (64, 1), F32)
        sb_row = sb("sb_row", (1, 128), F32)
        sb_stat = sb("sb_stat", (1, 3 * TPC), F32)

        pb = [ctx.enter_context(nc.psum_tensor(f"pb{i}", [128, 512], F32))
              for i in range(6)]
        pbf = ctx.enter_context(nc.psum_tensor("pbf", [128, 1024], BF16))
        pb7 = ctx.enter_context(nc.psum_tensor("pb7", [128, 512], F32))

        sems = {s: ctx.enter_context(nc.semaphore(s)) for s in SEMS}

        em.dma("sync", sb_i128[:], i128)
        em.dma("sync", sb_i64[:], i64)
        em.dma("sync", sb_tril[:], trilbig)
        em.dma("sync", sb_onesrowf[:], onesrow)
        em.dma("sync", sb_dtb[:], dtbias)
        em.dma("sync", sb_alog[:], alog)
        em.dma("sync", sb_drow[:], drow)
        em.dma("sync", sb_normw[:], normw)
        em.dma("sync", sb_g[:], gcol)
        em.dma("sync", sb_cvals[:], cvals)
        em.dma("gpsimd", sb_identbf[:], i128)
        em.dma("gpsimd", sb_triu[:], triu01)
        em.dma("gpsimd", sb_onescolbf[:], onescol)
        em.dma("gpsimd", sb_onesrowbf[:], onesrow)
        em.dma("gpsimd", sb_biasbf[:], biasrow)
        em.dma("gpsimd", sb_rep2[:], rep2)
        WCONST = [("s_syd", em.cnt["s_syd"]), ("s_gpd", em.cnt["s_gpd"])]

        srow = sb_stat[0:1, 0:TPC]
        qrow = sb_stat[0:1, TPC:2 * TPC]
        trow = sb_stat[0:1, 2 * TPC:3 * TPC]

        def emit_stats_tile(tile, sqtile, first, last, extra_wait_sum=(),
                            extra_wait_sq=(), sums=(2, 3), sqs=(4, 5)):
            toks = []
            for g2 in range(2):
                em.op("tensor", lambda e, g2=g2, t=tile, k0=first, k1=last:
                      e.matmul(pb[sums[g2]][0:1, :], sb_onescolbf[:],
                               t[:, g2 * 512:(g2 + 1) * 512], start=k0, stop=k1),
                      waits=extra_wait_sum, inc=None)
                tok = em.op("tensor", lambda e, g2=g2, t=sqtile, k0=first, k1=last:
                            e.matmul(pb[sqs[g2]][0:1, :], sb_onescolbf[:],
                                     t[:, g2 * 512:(g2 + 1) * 512], start=k0,
                                     stop=k1),
                            waits=extra_wait_sq,
                            inc=("s_pe" if (g2 == 1) else None))
                if tok:
                    toks.append(tok)
            return toks[-1]

        def finalize_stats(mbv, rbv, wtok_pe, denom, sums=(2, 3), sqs=(4, 5)):
            for g2 in range(2):
                em.op("vector", lambda e, g2=g2: e.tensor_copy(
                    srow[:, g2 * 512:(g2 + 1) * 512], pb[sums[g2]][0:1, :]),
                    waits=[("s_pe", wtok_pe)], inc="s_v")
                em.op("vector", lambda e, g2=g2: e.tensor_copy(
                    qrow[:, g2 * 512:(g2 + 1) * 512], pb[sqs[g2]][0:1, :]),
                    waits=[("s_pe", wtok_pe)], inc="s_v")
            t_ev = C["s_v"]
            em.op("scalar", lambda e: e.mul(srow, srow, 1.0 / denom),
                  waits=[("s_v", t_ev)], inc="s_ac", serial=True)
            em.op("scalar", lambda e: e.activation(trow, srow, AF.Square),
                  inc="s_ac", serial=True)
            t_m = C["s_ac"]
            em.op("vector", lambda e: e.scalar_tensor_tensor(
                qrow, qrow, 1.0 / denom, trow, ALU.mult, ALU.subtract),
                waits=[("s_ac", t_m)], inc="s_v", serial=True)
            em.op("scalar", lambda e: e.activation(qrow, qrow, AF.Ln,
                  bias=sb_cvals[0:1, 0:1]),
                  waits=[("s_v", C["s_v"])], inc="s_ac", serial=True)
            em.op("scalar", lambda e: e.activation(qrow, qrow, AF.Exp, scale=-0.5),
                  inc="s_ac", serial=True)
            t_r = C["s_ac"]
            t_last = 0
            for g2 in range(2):
                em.op("tensor", lambda e, g2=g2: e.matmul(
                    pb[0][:, :], sb_onesrowf[0:1, 0:128],
                    srow[:, g2 * 512:(g2 + 1) * 512], start=True, stop=True),
                    waits=[("s_ac", t_r), ("s_v", t_last)], inc="s_pe")
                em.op("tensor", lambda e, g2=g2: e.matmul(
                    pb[1][:, :], sb_onesrowf[0:1, 0:128],
                    qrow[:, g2 * 512:(g2 + 1) * 512], start=True, stop=True),
                    inc="s_pe")
                tb = C["s_pe"]
                em.op("vector", lambda e, g2=g2, mbv=mbv: e.tensor_copy(
                    mbv[:, g2 * 512:(g2 + 1) * 512], pb[0][:, :]),
                    waits=[("s_pe", tb)], inc="s_v")
                em.op("vector", lambda e, g2=g2, rbv=rbv: e.tensor_copy(
                    rbv[:, g2 * 512:(g2 + 1) * 512], pb[1][:, :]),
                    waits=[("s_pe", tb)], inc="s_v")
                t_last = C["s_v"]
            return t_last

        ar.push()
        o_xn = ar.take(NKT * TPC)
        o_sq = ar.take(2 * TPC)
        o_mb = ar.take(2 * TPC)
        o_rb = ar.take(2 * TPC)
        o_xh = ar.take(NKT * 4)
        o_hb = ar.take(64)
        xn = bfv(o_xn, (128, NKT * TPC))
        mb = f32v(o_mb, (128, TPC))
        rb = f32v(o_rb, (128, TPC))
        xh = bfv(o_xh, (128, NKT * 4))
        mhb = f32v(o_hb, (128, 8))[:, 0:4]
        rhb = f32v(o_hb + 16, (128, 8))[:, 0:4]

        t_x = em.dma("gpsimd", xn.rearrange("p (a t) -> p a t", a=NKT),
                     xT.rearrange("(a p) t -> p a t", p=128), waits=WCONST)
        t_xh = em.dma("gpsimd", xh.rearrange("p (a t) -> p a t", a=NKT),
                      xhT.rearrange("(a p) t -> p a t", p=128), waits=WCONST)

        t_sqpe = [0, 0]
        for kt in range(NKT):
            xt = xn[:, kt * TPC:(kt + 1) * TPC]
            slot = kt % 2
            sq = bfv(o_sq + slot * TPC, (128, TPC))
            em.op("scalar", lambda e, a=sq, b=xt: e.activation(a, b, AF.Square),
                  waits=[("s_gpd", t_x), ("s_pe", t_sqpe[slot])], inc="s_ac")
            t_aq = C["s_ac"]
            t_sqpe[slot] = emit_stats_tile(
                xt, sq, kt == 0, kt == NKT - 1,
                extra_wait_sum=[("s_gpd", t_x)],
                extra_wait_sq=[("s_ac", t_aq)])
        t_stats_mm = C["s_pe"]
        t_ln1 = finalize_stats(mb, rb, t_stats_mm, H)

        for kt in range(NKT):
            xt = xn[:, kt * TPC:(kt + 1) * TPC]
            em.op("vector", lambda e, a=xt: e.tensor_sub(a, a, mb),
                  waits=[("s_v", t_ln1), ("s_pe", t_stats_mm)], inc="s_v")
            em.op("vector", lambda e, a=xt: e.tensor_mul(a, a, rb), inc="s_v")
        t_xn = C["s_v"]

        hsq = bfv(o_sq, (128, NKT * 4))
        em.op("scalar", lambda e: e.activation(hsq, xh, AF.Square),
              waits=[("s_gpd", t_xh), ("s_pe", t_stats_mm)], inc="s_ac")
        t_hq = C["s_ac"]
        for kt in range(NKT):
            em.op("tensor", lambda e, kt=kt, k0=(kt == 0), k1=(kt == NKT - 1):
                  e.matmul(pb7[0:1, 0:4], sb_onescolbf[:],
                           xh[:, kt * 4:(kt + 1) * 4], start=k0, stop=k1),
                  waits=([("s_gpd", t_xh)] if kt == 0 else ()), inc=None)
            em.op("tensor", lambda e, kt=kt, k0=(kt == 0), k1=(kt == NKT - 1):
                  e.matmul(pb7[0:1, 4:8], sb_onescolbf[:],
                           hsq[:, kt * 4:(kt + 1) * 4], start=k0, stop=k1),
                  waits=([("s_ac", t_hq)] if kt == 0 else ()),
                  inc=("s_pe" if kt == NKT - 1 else None))
        t_hstat = C["s_pe"]
        hrow = sb_stat[0:1, 2 * TPC + 64:2 * TPC + 68]
        hrow2 = sb_stat[0:1, 2 * TPC + 68:2 * TPC + 72]
        hrow3 = sb_stat[0:1, 2 * TPC + 72:2 * TPC + 76]
        em.op("vector", lambda e: e.tensor_copy(hrow, pb7[0:1, 0:4]),
              waits=[("s_pe", t_hstat)], inc="s_v")
        em.op("vector", lambda e: e.tensor_copy(hrow2, pb7[0:1, 4:8]),
              waits=[("s_pe", t_hstat)], inc="s_v")
        em.op("scalar", lambda e: e.mul(hrow, hrow, 1.0 / H),
              waits=[("s_v", C["s_v"])], inc="s_ac", serial=True)
        em.op("scalar", lambda e: e.activation(hrow3, hrow, AF.Square),
              inc="s_ac", serial=True)
        em.op("vector", lambda e: e.scalar_tensor_tensor(
              hrow2, hrow2, 1.0 / H, hrow3, ALU.mult, ALU.subtract),
              waits=[("s_ac", C["s_ac"])], inc="s_v", serial=True)
        em.op("scalar", lambda e: e.activation(hrow2, hrow2, AF.Ln,
              bias=sb_cvals[0:1, 0:1]), waits=[("s_v", C["s_v"])], inc="s_ac",
              serial=True)
        em.op("scalar", lambda e: e.activation(hrow2, hrow2, AF.Exp, scale=-0.5),
              inc="s_ac", serial=True)
        t_hr = C["s_ac"]
        em.op("tensor", lambda e: e.matmul(pb7[:, 0:4], sb_onesrowf[0:1, 0:128],
              hrow, start=True, stop=True),
              waits=[("s_ac", t_hr), ("s_v", C["s_v"])], inc="s_pe")
        em.op("tensor", lambda e: e.matmul(pb7[:, 4:8], sb_onesrowf[0:1, 0:128],
              hrow2, start=True, stop=True), inc="s_pe")
        t_hbc = C["s_pe"]
        em.op("vector", lambda e: e.tensor_copy(mhb, pb7[:, 0:4]),
              waits=[("s_pe", t_hbc)], inc="s_v")
        em.op("vector", lambda e: e.tensor_copy(rhb, pb7[:, 4:8]),
              waits=[("s_pe", t_hbc)], inc="s_v")
        t_hln = C["s_v"]
        for kt in range(NKT):
            xht = xh[:, kt * 4:(kt + 1) * 4]
            em.op("vector", lambda e, a=xht: e.tensor_sub(a, a, mhb),
                  waits=[("s_v", t_hln), ("s_pe", t_hstat)], inc="s_v",
                  serial=True)
            em.op("vector", lambda e, a=xht: e.tensor_mul(a, a, rhb), inc="s_v",
                  serial=True)
        t_xnh = C["s_v"]
        t_pb7A = ("s_v", t_hln)

        o_wsl = ar.take(2 * NKT * 128)
        o_bst = ar.take(2 * TPC)
        o_halo = ar.take(NXT * 4)
        halo_all = bfv(o_halo, (128, NXT * 4))

        WOFF = [0]
        t_wsl = [None, None]
        t_evac = {i: ("s_v", t_ln1) for i in range(6)}
        pe_rot4 = (2, 3, 4, 5)

        t_hpb7 = [t_pb7A]

        def stream_tile(ti, width, on_evac, rot=pe_rot4, halo=False):
            slot = ti % 2
            wsl = bfv(o_wsl + slot * NKT * 128, (128, NKT * width))
            w = list(WCONST)
            if t_wsl[slot] is not None:
                w.append(t_wsl[slot])
            t_w = em.dma("gpsimd", wsl, winp[:, WOFF[0]:WOFF[0] + NKT * width],
                         waits=w)
            WOFF[0] += NKT * width
            if halo:
                for kt in range(NKT):
                    em.op("tensor", lambda e, kt=kt,
                          lw=wsl[:, kt * width:(kt + 1) * width],
                          k0=(kt == 0), k1=(kt == NKT - 1):
                          e.matmul(pb7[:, 256:260], lw,
                                   xh[:, kt * 4:(kt + 1) * 4],
                                   start=k0, stop=k1),
                          waits=([("s_gpd", t_w), ("s_v", t_xnh),
                                  t_hpb7[0]] if kt == 0 else ()),
                          inc=("s_pe" if kt == NKT - 1 else None))
                t_hmm = C["s_pe"]
                em.op("vector", lambda e, ti=ti: e.tensor_copy(
                    halo_all[:, ti * 4:(ti + 1) * 4], pb7[:, 256:260]),
                    waits=[("s_pe", t_hmm)], inc="s_v")
                t_hpb7[0] = ("s_v", C["s_v"])
            last_tok = None
            for half in range(2):
                bank = rot[(2 * ti + half) % len(rot)]
                wv = [("s_gpd", t_w), ("s_v", t_xn), t_evac[bank]]
                for kt in range(NKT):
                    em.op("tensor", lambda e, ps=pb[bank], wd=width,
                          lw=wsl[:, kt * width:(kt + 1) * width],
                          rhs=xn[:, kt * TPC + half * 512: kt * TPC + (half + 1) * 512],
                          k0=(kt == 0), k1=(kt == NKT - 1):
                          e.matmul(ps[0:wd, :] if wd < 128 else ps[:, :],
                                   lw, rhs, start=k0, stop=k1),
                          waits=(wv if kt == 0 else ()),
                          inc=("s_pe" if kt == NKT - 1 else None))
                t_pe = C["s_pe"]
                tok = on_evac(half, bank, t_pe)
                t_evac[bank] = tok
                last_tok = tok
            t_wsl[slot] = last_tok
            return last_tok

        t_bsp = [0, 0]
        for ti in range(NXT):
            slot2 = ti % 2
            bst = bfv(o_bst + slot2 * TPC, (128, TPC))

            def ev(half, bank, t_pe, bst=bst, slot2=slot2):
                w = [("s_pe", t_pe)]
                if half == 0 and t_bsp[slot2]:
                    w.append(("s_syd", t_bsp[slot2]))
                em.op("vector", lambda e, half=half, bank=bank, bst=bst:
                      e.tensor_copy(bst[:, half * 512:(half + 1) * 512],
                                    pb[bank][:, :]), waits=w, inc="s_v")
                return ("s_v", C["s_v"])

            stream_tile(ti, 128, ev, halo=True)
            t_bsp[slot2] = em.dma("sync", d_xbc[ti * 128:(ti + 1) * 128, :],
                                  bst[:], waits=[("s_v", C["s_v"])])
        t_b1_spill = em.cnt["s_syd"]
        t_halo = t_hpb7[0][1]

        def ev_dt(half, bank, t_pe):
            em.op("vector", lambda e, half=half, bank=bank: e.tensor_copy(
                sb_sc1[0:64, half * 512:(half + 1) * 512], pb[bank][0:64, :]),
                waits=[("s_pe", t_pe)], inc="s_v")
            return ("s_v", C["s_v"])
        stream_tile(NXT, 64, ev_dt)
        t_dtraw = C["s_v"]


        em.op("vector", lambda e: e.tensor_scalar(sb_sc1[0:64, :], sb_sc1[0:64, :],
              sb_dtb[:, 0:1], None, ALU.add), waits=[("s_v", t_dtraw)], inc="s_v")
        t_u = C["s_v"]
        em.op("scalar", lambda e: e.activation(sb_sc2[0:64, :], sb_sc1[0:64, :],
              AF.Abs), waits=[("s_v", t_u)], inc="s_ac", serial=True)
        em.op("scalar", lambda e: e.activation(sb_sc2[0:64, :], sb_sc2[0:64, :],
              AF.Exp, scale=-1.0), inc="s_ac", serial=True)
        em.op("scalar", lambda e: e.activation(sb_sc2[0:64, :], sb_sc2[0:64, :],
              AF.Ln, bias=sb_cvals[0:64, 1:2]), inc="s_ac", serial=True)
        t_lnx = C["s_ac"]
        em.op("vector", lambda e: e.tensor_scalar(sb_dt[0:64, :], sb_sc1[0:64, :],
              0.0, None, ALU.max), waits=[("s_ac", t_lnx)], inc="s_v")
        em.op("vector", lambda e: e.tensor_add(sb_dt[0:64, :], sb_dt[0:64, :],
              sb_sc2[0:64, :]), inc="s_v")
        t_dt = C["s_v"]
        em.op("scalar", lambda e: e.activation(sb_expa[:, 0:1], sb_alog[:, 0:1],
              AF.Exp), waits=WCONST, inc="s_ac")
        t_expa = C["s_ac"]
        em.op("vector", lambda e: e.tensor_scalar(sb_sc1[0:64, :], sb_dt[0:64, :],
              sb_expa[:, 0:1], None, ALU.mult),
              waits=[("s_v", t_dt), ("s_ac", t_expa)], inc="s_v")
        em.op("vector", lambda e: e.tensor_tensor_scan(
              sb_na[0:64, :], sb_sc1[0:64, :], sb_sc2[0:64, :], 0.0,
              ALU.add, ALU.bypass), inc="s_v")
        t_na = C["s_v"]

        t_pb7 = t_hpb7[0]
        for c in range(NC):
            em.op("tensor", lambda e, c=c: e.transpose(
                pb7[:, 0:64], sb_na[0:64, c * CL:(c + 1) * CL], sb_i64[:]),
                waits=[("s_v", t_na), t_pb7], inc="s_pe")
            em.op("vector", lambda e, c=c, t=C["s_pe"]: e.tensor_copy(
                sb_natm[:, c * 64:(c + 1) * 64], pb7[:, 0:64]),
                waits=[("s_pe", C["s_pe"])], inc="s_v")
            t_pb7 = ("s_v", C["s_v"])
            em.op("tensor", lambda e, c=c: e.transpose(
                pb7[:, 0:64], sb_dt[0:64, c * CL:(c + 1) * CL], sb_i64[:]),
                waits=[("s_v", t_na), t_pb7], inc="s_pe")
            em.op("vector", lambda e, c=c: e.tensor_copy(
                sb_dttm[:, c * 64:(c + 1) * 64], pb7[:, 0:64]),
                waits=[("s_pe", C["s_pe"])], inc="s_v")
            t_pb7 = ("s_v", C["s_v"])
            em.op("tensor", lambda e, c=c: e.transpose(
                pb7[0:1, 0:64], sb_na[0:64, c * CL + 127:c * CL + 128],
                sb_i64[:]), waits=[t_pb7], inc="s_pe")
            em.op("vector", lambda e: e.tensor_copy(sb_row[0:1, 0:64],
                  pb7[0:1, 0:64]), waits=[("s_pe", C["s_pe"])], inc="s_v")
            t_pb7 = ("s_v", C["s_v"])
            em.op("tensor", lambda e: e.matmul(pb7[:, 0:64],
                  sb_onesrowf[0:1, 0:128], sb_row[0:1, 0:64],
                  start=True, stop=True), waits=[t_pb7], inc="s_pe")
            em.op("vector", lambda e, c=c: e.tensor_copy(
                sb_nalb[:, c * 64:(c + 1) * 64], pb7[:, 0:64]),
                waits=[("s_pe", C["s_pe"])], inc="s_v")
            t_pb7 = ("s_v", C["s_v"])
            if c == 0:
                em.op("vector", lambda e: e.tensor_scalar(
                    sb_gexp[:, 0:64], sb_natm[:, 0:64], 0.0, sb_g[:, 0:1],
                    ALU.mult, ALU.add), inc="s_v")
                em.op("vector", lambda e: e.tensor_scalar(
                    sb_gexp[:, 0:64], sb_gexp[:, 0:64], 0.0, None, ALU.mult),
                    inc="s_v")
                em.op("vector", lambda e: e.tensor_scalar(
                    sb_gexp[:, 0:64], sb_gexp[:, 0:64], sb_g[:, 0:1], None,
                    ALU.add), inc="s_v")
            else:
                em.op("scalar", lambda e, c=c: e.activation(
                    sb_gexp[:, c * 64:(c + 1) * 64],
                    sb_nalb[:, (c - 1) * 64:c * 64], AF.Exp, scale=-1.0),
                    waits=[t_pb7] if c == 1 else [("s_v", C["s_v"])], inc="s_ac")
                em.op("vector", lambda e, c=c: e.tensor_scalar(
                    sb_gexp[:, c * 64:(c + 1) * 64],
                    sb_gexp[:, c * 64:(c + 1) * 64], sb_g[:, 0:1], None,
                    ALU.mult), waits=[("s_ac", C["s_ac"])], inc="s_v")
            em.op("vector", lambda e, c=c: e.tensor_sub(
                sb_wdt[:, c * 64:(c + 1) * 64], sb_natm[:, c * 64:(c + 1) * 64],
                sb_nalb[:, c * 64:(c + 1) * 64]),
                waits=[("s_v", C["s_v"])], inc="s_v")
            em.op("scalar", lambda e, c=c: e.activation(
                sb_wdt[:, c * 64:(c + 1) * 64], sb_wdt[:, c * 64:(c + 1) * 64],
                AF.Exp), waits=[("s_v", C["s_v"])], inc="s_ac")
            em.op("vector", lambda e, c=c: e.tensor_mul(
                sb_wdt[:, c * 64:(c + 1) * 64], sb_wdt[:, c * 64:(c + 1) * 64],
                sb_dttm[:, c * 64:(c + 1) * 64]),
                waits=[("s_ac", C["s_ac"])], inc="s_v")
            if c > 0:
                em.op("vector", lambda e, c=c: e.tensor_sub(
                    sb_lstep[:, c * 64:(c + 1) * 64],
                    sb_nalb[:, (c - 1) * 64:c * 64],
                    sb_nalb[:, c * 64:(c + 1) * 64]), inc="s_v")
                em.op("scalar", lambda e, c=c: e.activation(
                    sb_lstep[:, c * 64:(c + 1) * 64],
                    sb_lstep[:, c * 64:(c + 1) * 64], AF.Exp),
                    waits=[("s_v", C["s_v"])], inc="s_ac")
        t_prep = C["s_v"]
        t_prep_ac = C["s_ac"]
        t_prep_pb7 = t_pb7

        t_zsp = list(t_bsp)
        NCS = 4
        o_cin = ar.take(NCS * 1056)
        o_cd = ar.take(NCS * 512)
        o_cout = ar.take(NCS * TPC)
        o_tst = ar.take(2 * TPC)

        t_cin = [None] * NCS
        t_cds = [None] * NCS
        t_cout = [("s_v", 0)] * NCS
        t_tsp = [0, 0]
        t_tr = [0]
        t_convK = {4: t_evac[4], 5: t_evac[5]}

        def z_tile(zi):
            slot2 = zi % 2
            zst = bfv(o_bst + slot2 * TPC, (128, TPC))

            def evz(half, bank, t_pe, zst=zst, slot2=slot2):
                w = [("s_pe", t_pe)]
                if half == 0 and t_zsp[slot2]:
                    w.append(("s_syd", t_zsp[slot2]))
                em.op("scalar", lambda e, half=half, bank=bank, zst=zst:
                      e.activation(zst[:, half * 512:(half + 1) * 512],
                                   pb[bank][:, :], AF.Silu), waits=w, inc="s_ac")
                return ("s_ac", C["s_ac"])

            stream_tile(NXT + 1 + zi, 128, evz, rot=(2, 3))
            t_zsp[slot2] = em.dma("sync", d_z[zi * 128:(zi + 1) * 128, :],
                                  zst[:], waits=[("s_ac", C["s_ac"])])

        conv_pre = {}

        def conv_prefetch(ti):
            slot2 = ti % NCS
            cin = bfv(o_cin + slot2 * 1056, (128, 1056))
            cds = bfv(o_cd + slot2 * 512, (128, 512))
            w = [("s_syd", t_b1_spill)]
            if t_cin[slot2] is not None:
                w.append(t_cin[slot2])
            t_rx = em.dma("scalar", cin[:, 3:1027],
                          d_xbc[ti * 128:(ti + 1) * 128, :], waits=w)
            em.op("vector", lambda e, cin=cin, ti=ti: e.tensor_copy(
                cin[:, 0:3], halo_all[:, ti * 4 + 1:(ti + 1) * 4]),
                waits=[("s_v", t_halo), ("s_scd", t_rx)], inc="s_v")
            wd = list(WCONST)
            if t_cds[slot2] is not None:
                wd.append(t_cds[slot2])
            t_cd = em.dma("gpsimd", cds, convdiag[:, ti * 512:(ti + 1) * 512],
                          waits=wd)
            conv_pre[ti] = (C["s_v"], t_rx, t_cd)

        def conv_tile(ti):
            slot2 = ti % NCS
            cin = bfv(o_cin + slot2 * 1056, (128, 1056))
            cds = bfv(o_cd + slot2 * 512, (128, 512))
            t_hcp, t_rx, t_cd = conv_pre[ti]
            if ti == 64:
                cout = sb_bfm
            elif ti == 65:
                cout = sb_cfm
            else:
                cout = bfv(o_cout + slot2 * TPC, (128, TPC))
            for half in range(2):
                bank = 4 + half
                wv = [("s_scd", t_rx), ("s_gpd", t_cd), ("s_v", t_hcp),
                      t_cout[slot2], t_convK[bank]]
                for k in range(4):
                    em.op("tensor", lambda e, ps=pb[bank],
                          lw=cds[:, k * 128:(k + 1) * 128],
                          rhs=cin[:, half * 512 + k: half * 512 + k + 512],
                          k0=(k == 0), k1=(k == 3):
                          e.matmul(ps[:, :], lw, rhs, start=k0, stop=k1),
                          waits=(wv if k == 0 else ()),
                          inc=("s_pe" if k == 3 else None))
                t_cmm = C["s_pe"]
                em.op("scalar", lambda e, o=cout[:, half * 512:(half + 1) * 512],
                      ps=pb[bank]: e.activation(o, ps[:, :], AF.Silu),
                      waits=[("s_pe", t_cmm)], inc="s_ac")
                t_convK[bank] = ("s_ac", C["s_ac"])
            t_sil = ("s_ac", C["s_ac"])
            t_cin[slot2] = ("s_pe", C["s_pe"])
            t_cds[slot2] = ("s_pe", C["s_pe"])

            if ti == 64:
                for cc in range(8):
                    em.op("tensor", lambda e, cc=cc: e.transpose(
                        pbf[:, cc * 128:(cc + 1) * 128],
                        sb_bfm[:, cc * 128:(cc + 1) * 128], sb_identbf[:]),
                        waits=([t_sil, ("s_v", t_tr[0])] if cc == 0 else ()),
                        inc=("s_pe" if cc == 7 else None))
                t_t4 = C["s_pe"]
                for cc in range(8):
                    em.op("vector", lambda e, cc=cc: e.tensor_copy(
                        sb_btm[:, cc * 128:(cc + 1) * 128],
                        pbf[:, cc * 128:(cc + 1) * 128]),
                        waits=([("s_pe", t_t4)] if cc == 0 else ()), inc="s_v")
                t_tr[0] = C["s_v"]
                t_cout[slot2] = t_sil
                return
            if ti == 65:
                t_cout[slot2] = t_sil
                return

            ts2 = ti % 2
            tst = bfv(o_tst + ts2 * TPC, (128, TPC))
            for cc in range(8):
                wv = [t_sil, ("s_v", t_tr[0])]
                if cc == 0 and t_tsp[ts2]:
                    wv.append(("s_syd", t_tsp[ts2]))
                em.op("tensor", lambda e, cc=cc, src=cout: e.transpose(
                    pbf[:, cc * 128:(cc + 1) * 128],
                    src[:, cc * 128:(cc + 1) * 128], sb_identbf[:]),
                    waits=(wv if cc == 0 else ()),
                    inc=("s_pe" if cc == 7 else None))
            t_t4 = C["s_pe"]
            for cc in range(8):
                em.op("vector", lambda e, cc=cc, tst=tst: e.tensor_copy(
                    tst[:, cc * 128:(cc + 1) * 128],
                    pbf[:, cc * 128:(cc + 1) * 128]),
                    waits=([("s_pe", t_t4)] if cc == 0 else ()), inc="s_v")
            t_tr[0] = C["s_v"]
            t_cout[slot2] = ("s_pe", C["s_pe"])
            t_tsp[ts2] = em.dma(
                "sync", d_xtm.rearrange("(c h s) p -> h s c p", c=NC, h=64)[ti],
                tst.rearrange("s (c p) -> s c p", c=NC),
                waits=[("s_v", t_tr[0])])

        LAGC = 2
        for i in range(NXT + LAGC):
            if i < NXT:
                conv_prefetch(i)
            if i < NZT:
                z_tile(i)
            if i >= LAGC:
                conv_tile(i - LAGC)
        t_zdone = em.cnt["s_syd"]
        t_tr_evac = t_tr[0]
        t_b2 = em.cnt["s_syd"]
        t_btm = ("s_v", t_tr_evac)
        t_bcfm = [t_cout[0], t_cout[1]]
        t_conv_final = [t_convK[4], t_convK[5]]

        ar.pop()
        ar.push()
        o_xtm1 = ar.take(2 * DSSM)
        o_hA = ar.take(2 * DSSM)
        o_hB = ar.take(2 * DSSM)
        o_hbs = ar.take(DSSM)
        o_xw = ar.take(2 * 128)
        hbuf = [f32v(o_hA, (128, DSSM)), f32v(o_hB, (128, DSSM))]
        hbstage = bfv(o_hbs, (128, DSSM))

        t_xtm1 = [None, None]
        t_hb_spill = 0
        t_chain_v = 0
        for c in range(NC):
            slot2 = c % 2
            xtm1 = bfv(o_xtm1 + slot2 * DSSM, (128, DSSM))
            w = [("s_syd", t_b2)]
            if c == 0:
                w += t_bcfm + [t_btm]
            if t_xtm1[slot2] is not None:
                w.append(t_xtm1[slot2])
            t_lx = em.dma("scalar", xtm1.rearrange("s (h p) -> s h p", h=64),
                          d_xtm.rearrange("(c h s) p -> c s h p", c=NC, h=64)[c],
                          waits=w)
            t_ls = t_prep_ac
            t_wd = t_prep

            hs_src = hbuf[c % 2]
            hs_dst = hbuf[(c + 1) % 2]
            t_smm = {}
            tok_xw = {}
            tok_stt = {}
            PL = 6

            def p1_front(h):
                xw = bfv(o_xw + (h % 8) * 128, (128, 128))
                wv = [("s_scd", t_lx), ("s_v", t_wd), ("s_ac", t_ls),
                      ("s_pe", t_smm.get(h - 8, 0))]
                em.op("vector", lambda e, h=h, xw=xw, xtm1=xtm1, c=c:
                      e.tensor_scalar(
                          xw, xtm1[:, h * 128:(h + 1) * 128],
                          sb_wdt[:, c * 64 + h:c * 64 + h + 1],
                          None, ALU.mult), waits=wv, inc="s_v")
                tok_xw[h] = C["s_v"]
                bank = h % 6
                wv2 = [("s_v", tok_xw[h]), t_btm,
                       ("s_v", tok_stt.get(h - 6, 0))]
                if c == 0 and h in (4, 5):
                    wv2.append(t_conv_final[h % 2])
                em.op("tensor", lambda e, h=h, xw=xw, bank=bank, c=c: e.matmul(
                    pb[bank][:, 0:128], sb_btm[:, c * 128:(c + 1) * 128], xw,
                    start=True, stop=True), waits=wv2, inc="s_pe")
                t_smm[h] = C["s_pe"]

            def p1_back(h):
                bank = h % 6
                if c == 0:
                    em.op("vector", lambda e, h=h, bank=bank, d=hs_dst:
                          e.tensor_copy(d[:, h * 128:(h + 1) * 128],
                                        pb[bank][:, 0:128]),
                          waits=[("s_pe", t_smm[h])], inc="s_v")
                else:
                    em.op("vector", lambda e, h=h, bank=bank, d=hs_dst, s=hs_src:
                          e.scalar_tensor_tensor(
                              d[:, h * 128:(h + 1) * 128],
                              s[:, h * 128:(h + 1) * 128],
                              sb_lstep[:, c * 64 + h:c * 64 + h + 1],
                              pb[bank][:, 0:128], ALU.mult, ALU.add),
                          waits=[("s_pe", t_smm[h])], inc="s_v")
                tok_stt[h] = C["s_v"]
                wv2 = [("s_v", tok_stt[h])]
                if h == 0 and t_hb_spill:
                    wv2.append(("s_syd", t_hb_spill))
                em.op("scalar", lambda e, h=h, d=hs_dst: e.copy(
                    hbstage[:, h * 128:(h + 1) * 128],
                    d[:, h * 128:(h + 1) * 128]), waits=wv2, inc="s_ac")

            for i in range(NH + PL):
                if i < NH:
                    p1_front(i)
                if i >= PL:
                    p1_back(i - PL)
            t_chain_v = C["s_v"]
            t_chain_ac = C["s_ac"]
            t_xtm1[slot2] = ("s_v", t_chain_v)
            dstp = cc_hin if c == NC - 1 else d_hloc[c * 128:(c + 1) * 128, :]
            t_hb_spill = em.dma("sync", dstp, hbstage,
                                waits=[("s_ac", t_chain_ac)])
        t_chain_final = t_chain_v
        em.op("gpsimd", lambda e: e.collective_compute(
            "AllGather", ALU.bypass,
            replica_groups=[[0, 1], [2, 3], [4, 5], [6, 7]],
            ins=[cc_hin.opt()], outs=[cc_hout.opt()]),
            waits=[("s_syd", t_hb_spill)], inc="s_cc")
        t_ag2 = C["s_cc"]
        ar.pop()

        ar.push()
        o_xtm = ar.take(2 * DSSM)
        o_hpeer = ar.take(DSSM)
        o_hloc = ar.take(DSSM)
        o_zc = ar.take(DSSM)
        o_yc = ar.take(DSSM)
        o_idsc = ar.take(DSSM)
        NSLOT = 10
        KLAG = 7
        o_g = ar.take(128)
        o_e = ar.take(2 * 128)
        o_e2 = ar.take(2 * 128)
        o_ar2 = ar.take(2 * 128)
        o_mp = ar.take(NSLOT * 128)
        o_cpp = ar.take(NSLOT * 128)
        o_hbt = ar.take(NSLOT * 128)
        o_rb2 = ar.take(256)
        o_dbc = ar.take(128)
        hpeer = bfv(o_hpeer, (128, DSSM))
        hloc = bfv(o_hloc, (128, DSSM))
        ych = bfv(o_yc, (128, DSSM))
        zc = bfv(o_zc, (128, DSSM))
        idsc = bfv(o_idsc, (128, DSSM))
        gtile = bfv(o_g, (128, 128))
        rb2 = f32v(o_rb2, (128, 128))
        dbc = f32v(o_dbc, (128, 64))

        t_hpeer = em.dma("scalar", hpeer, cc_hout[0:128, :],
                         waits=[("s_cc", t_ag2)])
        em.op("tensor", lambda e: e.matmul(pb7[:, 0:64], sb_onesrowf[0:1, 0:128],
              sb_drow[0:1, 0:64], start=True, stop=True),
              waits=WCONST + [t_prep_pb7], inc="s_pe")
        t_db = C["s_pe"]
        em.op("vector", lambda e: e.tensor_copy(dbc, pb7[:, 0:64]),
              waits=[("s_pe", t_db)], inc="s_v")
        t_dbc = C["s_v"]
        for h in range(NH):
            em.op("vector", lambda e, h=h: e.tensor_scalar(
                idsc[:, h * 128:(h + 1) * 128], sb_identbf[:], dbc[:, h:h + 1],
                None, ALU.mult),
                waits=[("s_v", t_dbc), ("s_syd", t_hb_spill)], inc="s_v")
        t_idsc = C["s_v"]

        t_xtm2 = [None, None]
        t_zld = 0
        t_hlld = 0
        t_ygn_spill = 0
        t_rmsmm = 0
        gi_pA = {}
        gi_pB = {}
        gi_AR = {}
        gi_HBT = {}
        gi_Y = {}
        gi_Yev = {}
        gi_MP = {}
        gi_CPP = {}
        t_gv = 0
        t_rbv = 0
        for c in range(NC):
            slot2 = c % 2
            xtm = bfv(o_xtm + slot2 * DSSM, (128, DSSM))
            w = [("s_syd", t_b2)]
            if c == 0:
                w.append(("s_syd", t_hb_spill))
            if t_xtm2[slot2] is not None:
                w.append(t_xtm2[slot2])
            t_lx = em.dma("scalar", xtm.rearrange("s (h p) -> s h p", h=64),
                          d_xtm.rearrange("(c h s) p -> c s h p", c=NC, h=64)[c],
                          waits=w)
            wz = [("s_syd", t_zdone)]
            if c == 0:
                wz.append(("s_syd", t_hb_spill))
            if t_ygn_spill:
                wz.append(("s_syd", t_ygn_spill))
            if t_rmsmm:
                wz.append(("s_pe", t_rmsmm))
            t_lz = em.dma("sync", zc.rearrange("p (h t) -> p h t", h=64),
                          d_z.rearrange("(h p) t -> p h t", p=128)[:, :, c * CL:(c + 1) * CL],
                          waits=wz)
            if c > 0:
                whl = [("s_syd", t_hb_spill)]
                if t_hlld:
                    whl.append(("s_v", t_hlld))
                t_lh = em.dma("scalar", hloc, d_hloc[(c - 1) * 128:c * 128, :],
                              waits=whl)
            else:
                t_lh = 0

            em.op("tensor", lambda e, c=c: e.matmul(
                pb7[:, 128:256], sb_bfm[:, c * CL:(c + 1) * CL],
                sb_cfm[:, c * CL:(c + 1) * CL], start=True, stop=True),
                waits=t_bcfm + [("s_v", t_dbc), ("s_v", t_rbv)] +
                ([("s_syd", t_hb_spill)] if c == 0 else []), inc="s_pe")
            t_g = C["s_pe"]
            em.op("vector", lambda e: e.tensor_copy(gtile, pb7[:, 128:256]),
                  waits=[("s_pe", t_g)], inc="s_v")
            t_gv = C["s_v"]

            def front(h):
                gi = c * NH + h
                par = h % 2
                bA = par
                slot = gi % NSLOT
                E = bfv(o_e + par * 128, (128, 128))
                E2 = bfv(o_e2 + par * 128, (128, 128))
                AR = bfv(o_ar2 + par * 128, (128, 128))
                MP = bfv(o_mp + slot * 128, (128, 128))
                CPP = bfv(o_cpp + slot * 128, (128, 128))
                HBT = bfv(o_hbt + slot * 128, (128, 128))

                bB = 2 + par
                wv = [("s_v", t_na), ("s_v", t_prep),
                      ("s_ac", gi_pA.get(gi - 2, 0))]
                if gi < 2:
                    wv.append(("s_v", t_chain_final))
                em.op("tensor", lambda e, h=h, c=c, bA=bA: e.matmul(
                    pb[bA][:, 0:128],
                    sb_i64[:, h:h + 1].to_broadcast([64, 128]),
                    sb_na[0:64, c * CL:(c + 1) * CL], start=True, stop=True),
                    waits=wv, inc="s_pe")
                t_pA = C["s_pe"]
                wvB = [("s_ac", gi_pB.get(gi - 2, 0))]
                if gi < 2:
                    wvB.append(("s_v", t_chain_final))
                em.op("tensor", lambda e, h=h, c=c, bB=bB: e.matmul(
                    pb[bB][:, 0:128],
                    sb_i64[:, h:h + 1].to_broadcast([64, 128]),
                    sb_na[0:64, c * CL:(c + 1) * CL], start=True, stop=False),
                    waits=wvB, inc=None)
                em.op("tensor", lambda e, bB=bB: e.matmul(
                    pb[bB][:, 0:128], sb_i128[:], sb_tril[:],
                    start=False, stop=True), inc="s_pe")
                t_pB = C["s_pe"]
                if c == 0:
                    em.op("scalar", lambda e, E2=E2, bA=bA: e.activation(
                        E2, pb[bA][:, 0:128], AF.Exp, scale=-1.0),
                        waits=[("s_pe", t_pA), ("s_v", gi_CPP.get(gi - 2, 0))],
                        inc="s_ac")
                else:
                    em.op("scalar", lambda e, E2=E2, bA=bA, c=c, h=h: e.activation(
                        E2, pb[bA][:, 0:128], AF.Exp, scale=-1.0,
                        bias=sb_nalb[:, (c - 1) * 64 + h:(c - 1) * 64 + h + 1]),
                        waits=[("s_pe", t_pA), ("s_v", gi_CPP.get(gi - 2, 0))],
                        inc="s_ac")
                gi_pA[gi] = C["s_ac"]
                em.op("scalar", lambda e, E=E, bB=bB, c=c, h=h: e.activation(
                    E, pb[bB][:, 0:128], AF.Exp, scale=-1.0,
                    bias=sb_natm[:, c * 64 + h:c * 64 + h + 1]),
                    waits=[("s_pe", t_pB), ("s_v", gi_MP.get(gi - 2, 0))],
                    inc="s_ac")
                gi_pB[gi] = C["s_ac"]
                t_e = C["s_ac"]
                em.op("vector", lambda e, MP=MP, E=E, c=c, h=h:
                      e.scalar_tensor_tensor(
                          MP, E, sb_dttm[:, c * 64 + h:c * 64 + h + 1], gtile,
                          ALU.mult, ALU.mult),
                      waits=[("s_ac", t_e), ("s_v", t_gv),
                             ("s_pe", gi_Y.get(gi - NSLOT, 0))], inc="s_v")
                gi_MP[gi] = C["s_v"]
                em.op("vector", lambda e, CPP=CPP, E2=E2, c=c: e.tensor_mul(
                    CPP, sb_cfm[:, c * CL:(c + 1) * CL], E2),
                    waits=[("s_ac", gi_pA[gi]),
                           ("s_pe", gi_Y.get(gi - NSLOT, 0))],
                    inc="s_v")
                gi_CPP[gi] = C["s_v"]
                wv = [("s_scd", t_hpeer), ("s_v", t_prep),
                      ("s_pe", gi_Y.get(gi - NSLOT, 0))]
                if c > 0:
                    wv.append(("s_scd", t_lh))
                if c == 0:
                    em.op("vector", lambda e, HBT=HBT, h=h: e.tensor_scalar(
                        HBT, hpeer[:, h * 128:(h + 1) * 128],
                        sb_gexp[:, h:h + 1], None, ALU.mult),
                        waits=wv, inc="s_v")
                else:
                    em.op("vector", lambda e, HBT=HBT, h=h, c=c:
                          e.scalar_tensor_tensor(
                              HBT, hpeer[:, h * 128:(h + 1) * 128],
                              sb_gexp[:, c * 64 + h:c * 64 + h + 1],
                              hloc[:, h * 128:(h + 1) * 128],
                              ALU.mult, ALU.add),
                          waits=wv, inc="s_v")
                gi_HBT[gi] = C["s_v"]

            def back(h):
                gi = c * NH + h
                slot = gi % NSLOT
                bY = 4 + h % 2
                MP = bfv(o_mp + slot * 128, (128, 128))
                CPP = bfv(o_cpp + slot * 128, (128, 128))
                HBT = bfv(o_hbt + slot * 128, (128, 128))
                wv = [("s_v", gi_MP[gi]), ("s_v", gi_HBT[gi]), ("s_scd", t_lx),
                      ("s_v", t_idsc), ("s_ac", gi_Yev.get(gi - 2, 0))]
                if gi < 2:
                    wv.append(("s_v", t_chain_final))
                em.op("tensor", lambda e, h=h, bY=bY, MP=MP, xtm=xtm: e.matmul(
                    pb[bY][:, 0:128], xtm[:, h * 128:(h + 1) * 128], MP,
                    start=True, stop=False), waits=wv, inc=None)
                em.op("tensor", lambda e, bY=bY, HBT=HBT, CPP=CPP: e.matmul(
                    pb[bY][:, 0:128], HBT, CPP, start=False, stop=False),
                    waits=[("s_v", gi_CPP[gi])], inc=None)
                em.op("tensor", lambda e, h=h, bY=bY, xtm=xtm: e.matmul(
                    pb[bY][:, 0:128], xtm[:, h * 128:(h + 1) * 128],
                    idsc[:, h * 128:(h + 1) * 128], start=False, stop=True),
                    inc="s_pe")
                gi_Y[gi] = C["s_pe"]
                wy = [("s_pe", gi_Y[gi])]
                if h == 0:
                    wy.append(("s_syd", t_ygn_spill))
                    wy.append(("s_pe", t_rmsmm))
                em.op("scalar", lambda e, h=h, bY=bY: e.activation(
                    ych[:, h * 128:(h + 1) * 128], pb[bY][:, 0:128], AF.Copy),
                    waits=wy, inc="s_ac")
                gi_Yev[gi] = C["s_ac"]

            for i in range(NH + KLAG):
                if i < NH:
                    front(i)
                if i >= KLAG:
                    back(i - KLAG)
            t_ally = C["s_ac"]
            t_hlld = C["s_v"]
            t_xtm2[slot2] = ("s_pe", C["s_pe"])

            em.op("vector", lambda e, zc=zc, ych=ych: e.tensor_mul(ych, ych, zc),
                  waits=[("s_ac", t_ally), ("s_syd", t_lz)], inc="s_v")
            t_yg = C["s_v"]
            em.op("scalar", lambda e, zc=zc, ych=ych: e.activation(
                zc, ych, AF.Square), waits=[("s_v", t_yg)], inc="s_ac")
            t_sq2 = C["s_ac"]
            for h in range(NH):
                em.op("tensor", lambda e, h=h, zc=zc, k0=(h == 0),
                      k1=(h == NH - 1): e.matmul(
                          pb7[0:1, 0:128], sb_onescolbf[:],
                          zc[:, h * 128:(h + 1) * 128], start=k0, stop=k1),
                      waits=([("s_ac", t_sq2), ("s_v", t_gv), ("s_v", t_rbv)]
                             if h == 0 else ()),
                      inc=("s_pe" if h == NH - 1 else None))
            t_rmsmm = C["s_pe"]
            em.op("vector", lambda e: e.tensor_copy(sb_row[0:1, 0:128],
                  pb7[0:1, 0:128]), waits=[("s_pe", t_rmsmm)], inc="s_v",
                  serial=True)
            em.op("scalar", lambda e: e.activation(sb_row[0:1, 0:128],
                  sb_row[0:1, 0:128], AF.Ln, scale=1.0 / DSSM,
                  bias=sb_cvals[0:1, 0:1]),
                  waits=[("s_v", C["s_v"])], inc="s_ac", serial=True)
            em.op("scalar", lambda e: e.activation(sb_row[0:1, 0:128],
                  sb_row[0:1, 0:128], AF.Exp, scale=-0.5), inc="s_ac",
                  serial=True)
            t_rr = C["s_ac"]
            em.op("tensor", lambda e: e.matmul(pb7[:, 128:256],
                  sb_onesrowf[0:1, 0:128], sb_row[0:1, 0:128],
                  start=True, stop=True),
                  waits=[("s_ac", t_rr), ("s_v", t_gv)], inc="s_pe")
            t_rbm = C["s_pe"]
            em.op("vector", lambda e: e.tensor_copy(rb2, pb7[:, 128:256]),
                  waits=[("s_pe", t_rbm)], inc="s_v")
            t_rbv = C["s_v"]
            for h in range(NH):
                em.op("vector", lambda e, h=h, zc=zc, ych=ych:
                      e.scalar_tensor_tensor(
                          zc[:, h * 128:(h + 1) * 128],
                          ych[:, h * 128:(h + 1) * 128],
                          sb_normw[:, h:h + 1], rb2, ALU.mult, ALU.mult),
                      waits=([("s_v", t_rbv)] if h == 0 else ()), inc="s_v")
            t_ygnv = C["s_v"]
            t_ygn_spill = em.dma(
                "sync",
                d_ygn.rearrange("(h p) t -> p h t", p=128)[:, :, c * CL:(c + 1) * CL],
                zc.rearrange("p (h t) -> p h t", h=64),
                waits=[("s_v", t_ygnv)])
        t_p2 = em.cnt["s_syd"]
        ar.pop()

        BF = em.bar()
        ar.push()
        o_ygk = ar.take(32 * TPC)
        o_wo = ar.take(2 * 32 * 128)
        o_fst = ar.take(2 * TPC)
        o_fpt = ar.take(2 * 2 * TPC)
        ygk = bfv(o_ygk, (128, 32 * TPC))

        WOO = [0]
        t_ldy = 0
        t_attn_sp = [0, 0]
        t_part_sp = [0, 0]
        t_part_ld = [0, 0]
        t_fevac = {i: ("s_v", 0) for i in (2, 3, 4, 5)}
        t_kh_done = 0
        t_wslf = [None, None]
        for kh in range(2):
            wy = list(BF)
            if kh == 1:
                wy.append(("s_pe", t_kh_done))
            t_ldy = em.dma("sync", ygk.rearrange("p (a t) -> p a t", a=32),
                           d_ygn.rearrange("(a p) t -> p a t", p=128)[:, kh * 32:(kh + 1) * 32, :],
                           waits=wy)
            for dm in range(NDM):
                slot2 = dm % 2
                wsl = bfv(o_wo + slot2 * 32 * 128, (128, 32 * 128))
                w = list(BF) if t_wslf[slot2] is None else [t_wslf[slot2]]
                t_w = em.dma("gpsimd", wsl, woutp[:, WOO[0]:WOO[0] + 32 * 128],
                             waits=w)
                WOO[0] += 32 * 128
                fpt = f32v(o_fpt + slot2 * 2 * TPC, (128, TPC))
                fst = bfv(o_fst + slot2 * TPC, (128, TPC))
                if kh == 1:
                    wp = [("s_syd", t_part_sp[0]), ("s_syd", t_part_sp[1])]
                    if t_part_ld[slot2]:
                        wp.append(("s_v", t_part_ld[slot2]))
                    t_lp = em.dma("scalar", fpt, d_part[dm * 128:(dm + 1) * 128, :],
                                  waits=wp)
                toks = []
                for half in range(2):
                    bank = 2 + pe_rot4[(2 * dm + half) % 4] - 2
                    bank = pe_rot4[(2 * dm + half) % 4]
                    wv = [("s_gpd", t_w), ("s_syd", t_ldy), t_fevac[bank]]
                    for kt in range(32):
                        em.op("tensor", lambda e, ps=pb[bank],
                              lw=wsl[:, kt * 128:(kt + 1) * 128],
                              rhs=ygk[:, kt * TPC + half * 512: kt * TPC + (half + 1) * 512],
                              k0=(kt == 0), k1=(kt == 31):
                              e.matmul(ps[:, :], lw, rhs, start=k0, stop=k1),
                              waits=(wv if kt == 0 else ()),
                              inc=("s_pe" if kt == 31 else None))
                    t_pe = C["s_pe"]
                    if kh == 0:
                        w2 = [("s_pe", t_pe)]
                        if half == 0 and t_part_sp[slot2]:
                            w2.append(("s_syd", t_part_sp[slot2]))
                        em.op("vector", lambda e, o=fpt[:, half * 512:(half + 1) * 512],
                              ps=pb[bank]: e.tensor_copy(o, ps[:, :]),
                              waits=w2, inc="s_v")
                    else:
                        w2 = [("s_pe", t_pe), ("s_scd", t_lp)]
                        if half == 0 and t_attn_sp[slot2]:
                            w2.append(("s_syd", t_attn_sp[slot2]))
                        em.op("vector", lambda e, o=fst[:, half * 512:(half + 1) * 512],
                              ps=pb[bank], pr=fpt[:, half * 512:(half + 1) * 512]:
                              e.tensor_add(o, ps[:, :], pr), waits=w2, inc="s_v")
                    t_fevac[bank] = ("s_v", C["s_v"])
                    toks.append(C["s_v"])
                if kh == 0:
                    t_part_sp[slot2] = em.dma(
                        "sync", d_part[dm * 128:(dm + 1) * 128, :], fpt,
                        waits=[("s_v", toks[-1])])
                else:
                    t_part_ld[slot2] = toks[-1]
                    t_attn_sp[slot2] = em.dma(
                        "sync", d_attn[dm * 128:(dm + 1) * 128, :], fst,
                        waits=[("s_v", toks[-1])])
                t_wslf[slot2] = ("s_v", toks[-1])
            t_kh_done = C["s_pe"]
        t_f = em.cnt["s_syd"]
        ar.pop()

        BG = em.bar()
        ar.push()
        o_xf = ar.take(NKT * TPC)
        o_gst = ar.take(4 * TPC)
        o_gmb = ar.take(8 * TPC)
        xf = bfv(o_xf, (128, NKT * TPC))
        m2b = f32v(o_gmb, (128, TPC))
        r2b = f32v(o_gmb + 2 * TPC, (128, TPC))
        m3b = f32v(o_gmb + 4 * TPC, (128, TPC))
        r3b = f32v(o_gmb + 6 * TPC, (128, TPC))

        t_gpe = [0, 0]
        for dm in range(NDM):
            slot2 = dm % 2
            ast = bfv(o_gst + slot2 * TPC, (128, TPC))
            sqt = bfv(o_gst + (2 + slot2) * TPC, (128, TPC))
            w = list(BG)
            if t_gpe[slot2]:
                w.append(("s_pe", t_gpe[slot2]))
            t_la = em.dma("scalar", ast, d_attn[dm * 128:(dm + 1) * 128, :],
                          waits=w)
            em.op("scalar", lambda e, a=sqt, b=ast: e.activation(a, b, AF.Square),
                  waits=[("s_scd", t_la)], inc="s_ac")
            t_aq = C["s_ac"]
            t_gpe[slot2] = emit_stats_tile(
                ast, sqt, dm == 0, dm == NDM - 1,
                extra_wait_sum=[("s_scd", t_la)] + (list(BG) if dm == 0 else []),
                extra_wait_sq=[("s_ac", t_aq)])
        t_gstat = C["s_pe"]
        t_ln2 = finalize_stats(m2b, r2b, t_gstat, H)

        t_s2sp = [0, 0]
        for dm in range(NDM):
            slot2 = dm % 2
            ast = bfv(o_gst + slot2 * TPC, (128, TPC))
            xst = bfv(o_gst + (2 + slot2) * TPC, (128, TPC))
            w = [("s_v", t_ln2), ("s_pe", t_gpe[slot2])]
            if t_s2sp[slot2]:
                w.append(("s_syd", t_s2sp[slot2]))
            t_la = em.dma("scalar", ast, d_attn[dm * 128:(dm + 1) * 128, :],
                          waits=w)
            t_lx2 = em.dma("gpsimd", xst, xT[dm * 128:(dm + 1) * 128, :],
                           waits=w)
            em.op("vector", lambda e, a=ast: e.tensor_sub(a, a, m2b),
                  waits=[("s_scd", t_la), ("s_v", t_ln2)], inc="s_v")
            em.op("vector", lambda e, a=ast: e.tensor_mul(a, a, r2b), inc="s_v")
            em.op("vector", lambda e, a=ast, b=xst: e.tensor_add(a, a, b),
                  waits=[("s_gpd", t_lx2)], inc="s_v")
            t_s2 = C["s_v"]
            em.op("scalar", lambda e, a=xst, b=ast: e.activation(a, b, AF.Square),
                  waits=[("s_v", t_s2)], inc="s_ac")
            t_aq = C["s_ac"]
            t_gpe[slot2] = emit_stats_tile(
                ast, xst, dm == 0, dm == NDM - 1,
                extra_wait_sum=[("s_v", t_s2)],
                extra_wait_sq=[("s_ac", t_aq)])
            t_s2sp[slot2] = em.dma("sync", d_s2[dm * 128:(dm + 1) * 128, :], ast,
                                   waits=[("s_pe", t_gpe[slot2]),
                                          ("s_ac", t_aq)])
        t_s2stat = C["s_pe"]
        t_s2done = em.cnt["s_syd"]
        t_ln3i = finalize_stats(m3b, r3b, t_s2stat, H)

        t_gl2 = [0, 0]
        for dm in range(NDM):
            slot2 = dm % 2
            xft = xf[:, dm * TPC:(dm + 1) * TPC]
            w = [("s_v", t_ln3i), ("s_syd", t_s2done)]
            ast = bfv(o_gst + slot2 * TPC, (128, TPC))
            if t_gl2[slot2]:
                w.append(("s_v", t_gl2[slot2]))
            t_ls2 = em.dma("scalar", ast, d_s2[dm * 128:(dm + 1) * 128, :],
                           waits=w)
            em.op("vector", lambda e, a=xft, b=ast: e.tensor_sub(a, b, m3b),
                  waits=[("s_scd", t_ls2)], inc="s_v")
            em.op("vector", lambda e, a=xft: e.tensor_mul(a, a, r3b), inc="s_v")
            t_gl2[slot2] = C["s_v"]
        t_xf = C["s_v"]
        if debug:
            em.dma("sync", d_dbgxf.rearrange("(a p) t -> p a t", p=128),
                   xf.rearrange("p (a t) -> p a t", a=NKT),
                   waits=[("s_v", t_xf)])

        o_hf = ar.take(TPC)
        o_wbc = ar.take(TPC)
        o_ltm = ar.take(2 * 64)
        o_lfm = ar.take(2 * TPC)
        o_wtm = ar.take(64)
        o_wfm = ar.take(TPC)
        o_apk = ar.take(32 * 128)
        o_bmp = ar.take(H)
        o_wrp = ar.take(NKT * 8)
        o_wbsl = ar.take(2 * 32 * 128)
        o_fst2 = ar.take(2 * TPC)
        o_ost = ar.take(4 * TPC)
        hfm = bfv(o_hf, (128, TPC))
        wbc = bfv(o_wbc, (128, TPC))
        ltm = f32v(o_ltm, (128, 64))
        lfm = f32v(o_lfm, (8, TPC))
        wtm = bfv(o_wtm, (128, 8))
        wfm = bfv(o_wfm, (8, TPC))
        apk = bfv(o_apk, (128, 32 * 128))
        bmp = bfv(o_bmp, (128, H))
        wrp = bfv(o_wrp, (128, NKT * 8))

        t_apk = em.dma("gpsimd", apk, apack, waits=BG)
        t_bmp = em.dma("gpsimd", bmp, bmpack, waits=BG)
        t_wrp = em.dma("gpsimd", wrp, wrtp, waits=BG)

        for half in range(2):
            bank = half
            for kt in range(32):
                em.op("tensor", lambda e, ps=pb[bank],
                      lw=apk[:, kt * 128:(kt + 1) * 128],
                      rhs=xf[:, kt * TPC + half * 512: kt * TPC + (half + 1) * 512],
                      k0=(kt == 0), k1=(kt == 31):
                      e.matmul(ps[:, :], lw, rhs, start=k0, stop=k1),
                      waits=([("s_gpd", t_apk), ("s_v", t_xf),
                              ("s_v", t_ln3i)] if kt == 0 else ()),
                      inc=("s_pe" if kt == 31 else None))
            t_h = C["s_pe"]
            em.op("vector", lambda e, half=half, bank=bank: e.tensor_copy(
                hfm[:, half * 512:(half + 1) * 512], pb[bank][:, :]),
                waits=[("s_pe", t_h)], inc="s_v")
        t_hfm = C["s_v"]
        for half in range(2):
            bank = half
            for kt in range(32):
                em.op("tensor", lambda e, ps=pb[bank],
                      lw=wrp[:, kt * 8:(kt + 1) * 8],
                      rhs=xf[:, kt * TPC + half * 512: kt * TPC + (half + 1) * 512],
                      k0=(kt == 0), k1=(kt == 31):
                      e.matmul(ps[0:8, :], lw, rhs, start=k0, stop=k1),
                      waits=([("s_gpd", t_wrp), ("s_v", t_hfm)] if kt == 0 else ()),
                      inc=("s_pe" if kt == 31 else None))
            t_l = C["s_pe"]
            em.op("vector", lambda e, half=half, bank=bank: e.tensor_copy(
                lfm[0:8, half * 512:(half + 1) * 512], pb[bank][0:8, :]),
                waits=[("s_pe", t_l)], inc="s_v")
        t_lfm = C["s_v"]

        t_pb7r = t_rbv
        for tt in range(8):
            em.op("tensor", lambda e, tt=tt: e.transpose(
                pb7[:, 0:8], lfm[0:8, tt * 128:(tt + 1) * 128],
                sb_i128[0:8, 0:8]),
                waits=[("s_v", t_lfm), ("s_v", t_pb7r)], inc="s_pe")
            t_tr = C["s_pe"]
            em.op("vector", lambda e: e.tensor_copy(ltm[:, 0:8], pb7[:, 0:8]),
                  waits=[("s_pe", t_tr)], inc="s_v", serial=True)
            em.op("vector", lambda e: e.reduce_max(ltm[:, 8:9], ltm[:, 0:8],
                  axis=mybir.AxisListType.X), inc="s_v", serial=True)
            em.op("vector", lambda e: e.tensor_scalar(ltm[:, 48:56],
                  ltm[:, 0:8], ltm[:, 8:9], None, ALU.subtract), inc="s_v",
                  serial=True)
            em.op("scalar", lambda e: e.activation(ltm[:, 16:24], ltm[:, 48:56],
                  AF.Exp), waits=[("s_v", C["s_v"])], inc="s_ac", serial=True)
            t_e3 = C["s_ac"]
            em.op("vector", lambda e: e.tensor_scalar(ltm[:, 24:32],
                  ltm[:, 16:24], 1.0, None, ALU.is_ge),
                  waits=[("s_ac", t_e3)], inc="s_v", serial=True)
            em.op("vector", lambda e: e.tensor_mul(ltm[:, 32:40], ltm[:, 16:24],
                  ltm[:, 24:32]), inc="s_v", serial=True)
            em.op("vector", lambda e: e.tensor_sub(ltm[:, 32:40], ltm[:, 16:24],
                  ltm[:, 32:40]), inc="s_v", serial=True)
            em.op("vector", lambda e: e.reduce_max(ltm[:, 9:10], ltm[:, 32:40],
                  axis=mybir.AxisListType.X), inc="s_v", serial=True)
            em.op("vector", lambda e: e.tensor_scalar(ltm[:, 24:32],
                  ltm[:, 16:24], ltm[:, 9:10], None, ALU.is_ge), inc="s_v",
                  serial=True)
            em.op("vector", lambda e: e.tensor_mul(ltm[:, 32:40], ltm[:, 16:24],
                  ltm[:, 24:32]), inc="s_v", serial=True)
            em.op("vector", lambda e: e.tensor_scalar(ltm[:, 9:10], ltm[:, 9:10],
                  1.0, None, ALU.add), inc="s_v", serial=True)
            em.op("vector", lambda e: e.reciprocal(ltm[:, 10:11], ltm[:, 9:10]),
                  inc="s_v", serial=True)
            em.op("vector", lambda e: e.tensor_scalar(wtm[:, 0:8], ltm[:, 32:40],
                  ltm[:, 10:11], None, ALU.mult), inc="s_v", serial=True)
            t_w2 = C["s_v"]
            em.op("tensor", lambda e: e.transpose(pbf[0:8, 0:128], wtm[:, 0:8],
                  sb_identbf[:]), waits=[("s_v", t_w2), ("s_v", t_tr_evac)],
                  inc="s_pe")
            t_tw = C["s_pe"]
            em.op("vector", lambda e, tt=tt: e.tensor_copy(
                wfm[0:8, tt * 128:(tt + 1) * 128], pbf[0:8, 0:128]),
                waits=[("s_pe", t_tw)], inc="s_v", serial=True)
            t_pb7r = C["s_v"]
            t_tr_evac = C["s_v"]
        t_wfm = C["s_v"]
        for half in range(2):
            bank = half
            em.op("tensor", lambda e, half=half, bank=bank: e.matmul(
                pb[bank][:, :], sb_rep2[0:8, :],
                wfm[0:8, half * 512:(half + 1) * 512], start=True, stop=True),
                waits=[("s_v", t_wfm)], inc="s_pe")
            t_wb = C["s_pe"]
            em.op("scalar", lambda e, half=half, bank=bank: e.activation(
                wbc[:, half * 512:(half + 1) * 512], pb[bank][:, :], AF.Copy),
                waits=[("s_pe", t_wb)], inc="s_ac")
        t_wbc = C["s_ac"]
        em.op("vector", lambda e: e.tensor_mul(hfm, hfm, wbc),
              waits=[("s_ac", t_wbc), ("s_v", t_hfm)], inc="s_v")
        t_hw = C["s_v"]
        if debug:
            em.dma("sync", d_dbgw[:, 0:TPC], wbc, waits=[("s_v", t_hw)])
            em.dma("sync", d_dbgw[:, TPC:2 * TPC], hfm, waits=[("s_v", t_hw)])


        t_wslh = [None, None]
        t_hev = {0: ("s_v", t_hw), 1: ("s_v", t_hw)}
        t_ffn_sp = [0, 0]
        t_hpe = [0, 0]
        for dm in range(NDM):
            slot2 = dm % 2
            wsl = bfv(o_wbsl + slot2 * 32 * 128, (128, 32 * 128))
            w = [("s_v", t_xf)] if t_wslh[slot2] is None else [t_wslh[slot2]]
            t_w = em.dma("gpsimd", wsl, wbp[:, dm * 32 * 128:(dm + 1) * 32 * 128],
                         waits=w)
            fst = bfv(o_fst2 + slot2 * TPC, (128, TPC))
            sqt = bfv(o_gst + slot2 * TPC, (128, TPC))
            toks = []
            for half in range(2):
                bank = half
                wv = [("s_gpd", t_w), ("s_v", t_hw), ("s_gpd", t_bmp),
                      t_hev[bank]]
                for kt in range(32):
                    em.op("tensor", lambda e, ps=pb[bank],
                          lw=wsl[:, kt * 128:(kt + 1) * 128],
                          rhs=xf[:, kt * TPC + half * 512: kt * TPC + (half + 1) * 512],
                          k0=(kt == 0):
                          e.matmul(ps[:, :], lw, rhs, start=k0, stop=False),
                          waits=(wv if kt == 0 else ()), inc=None)
                em.op("tensor", lambda e, ps=pb[bank], dm=dm: e.matmul(
                    ps[:, :], sb_biasbf[0:1, dm * 128:(dm + 1) * 128],
                    sb_onesrowbf[0:1, 0:512], start=False, stop=False), inc=None)
                em.op("tensor", lambda e, ps=pb[bank], dm=dm, half=half: e.matmul(
                    ps[:, :], bmp[:, dm * 128:(dm + 1) * 128],
                    hfm[:, half * 512:(half + 1) * 512], start=False, stop=True),
                    inc="s_pe")
                t_pe = C["s_pe"]
                w2 = [("s_pe", t_pe)]
                if half == 0 and t_ffn_sp[slot2]:
                    w2.append(("s_syd", t_ffn_sp[slot2]))
                em.op("vector", lambda e, o=fst[:, half * 512:(half + 1) * 512],
                      ps=pb[bank]: e.tensor_copy(o, ps[:, :]), waits=w2,
                      inc="s_v")
                t_hev[bank] = ("s_v", C["s_v"])
                toks.append(C["s_v"])
            t_ev3 = toks[-1]
            em.op("scalar", lambda e, a=sqt, b=fst: e.activation(a, b, AF.Square),
                  waits=[("s_v", t_ev3), ("s_pe", t_hpe[slot2])], inc="s_ac")
            t_aq = C["s_ac"]
            t_hpe[slot2] = emit_stats_tile(
                fst, sqt, dm == 0, dm == NDM - 1,
                extra_wait_sum=[("s_v", t_ev3)] + ([("s_ac", t_wbc)] if dm == 0 else []),
                extra_wait_sq=[("s_ac", t_aq)])
            t_wslh[slot2] = ("s_v", t_ev3)
            t_ffn_sp[slot2] = em.dma("sync", d_ffn[dm * 128:(dm + 1) * 128, :],
                                     fst, waits=[("s_pe", t_hpe[slot2]),
                                                 ("s_ac", t_aq)])
        t_ffnstat = C["s_pe"]
        t_ln3 = finalize_stats(m2b, r2b, t_ffnstat, H)

        t_old = [0, 0]
        for dm in range(NDM):
            slot2 = dm % 2
            fst = bfv(o_fst2 + slot2 * TPC, (128, TPC))
            s2t = bfv(o_gst + (2 + slot2) * TPC, (128, TPC))
            ost = f32v(o_ost + slot2 * 2 * TPC, (128, TPC))
            w = [("s_v", t_ln3), ("s_syd", t_ffn_sp[0]), ("s_syd", t_ffn_sp[1])]
            if t_old[slot2]:
                w.append(("s_syd", t_old[slot2]))
            t_lf = em.dma("scalar", fst, d_ffn[dm * 128:(dm + 1) * 128, :],
                          waits=w)
            t_ls3 = em.dma("sync", s2t, d_s2[dm * 128:(dm + 1) * 128, :],
                           waits=w)
            em.op("vector", lambda e, o=ost, b=fst: e.tensor_sub(o, b, m2b),
                  waits=[("s_scd", t_lf), ("s_v", t_ln3)], inc="s_v")
            em.op("vector", lambda e, o=ost: e.tensor_mul(o, o, r2b), inc="s_v")
            em.op("vector", lambda e, o=ost, b=s2t: e.tensor_add(o, o, b),
                  waits=[("s_syd", t_ls3)], inc="s_v")
            t_o = C["s_v"]
            t_old[slot2] = em.dma("sync", out_T[dm * 128:(dm + 1) * 128, :], ost,
                                  waits=[("s_v", t_o)])
        t_final = em.cnt["s_syd"]

        if debug:
            for nm, (dst, src) in dbg.items():
                em.dma("sync", dst, src, waits=[("s_syd", t_final)])
            t_final = em.cnt["s_syd"]

        em.op("sync", lambda e: e.wait_ge(sems["s_syd"], t_final))

        em.check_deadlock()

        with nc.Block() as block:
            @block.tensor
            def _(te):
                em.replay_one("tensor", te, sems)

            @block.vector
            def _(v):
                em.replay_one("vector", v, sems)

            @block.scalar
            def _(s):
                em.replay_one("scalar", s, sems)

            @block.gpsimd
            def _(gp):
                em.replay_one("gpsimd", gp, sems)

            @block.sync
            def _(sy):
                em.replay_one("sync", sy, sems)

    return nc


_CACHE = {}


def _pack_weights(W_in, conv_w, dt_bias, A_log, D_param, norm_weight, W_out,
                  Wb, bias, Wr, A, Bm):
    f = np.float32
    W_in = np.asarray(W_in, f)

    def pack(Wsub, width=128):
        nt = Wsub.shape[1] // width
        out = np.empty((128, nt * NKT * width), f)
        for t in range(nt):
            blk = Wsub[:, t * width:(t + 1) * width].reshape(NKT, 128, width)
            out[:, t * NKT * width:(t + 1) * NKT * width] = \
                blk.transpose(1, 0, 2).reshape(128, NKT * width)
        return out

    winp = np.concatenate([
        pack(W_in[:, DSSM:DSSM + CONV]),
        pack(W_in[:, DSSM + CONV:], 64),
        pack(W_in[:, :DSSM]),
    ], axis=1)

    W_out = np.asarray(W_out, f)
    parts = []
    for kh in range(2):
        sub = W_out[kh * 4096:(kh + 1) * 4096, :]
        for dm in range(NDM):
            blk = sub[:, dm * 128:(dm + 1) * 128].reshape(32, 128, 128)
            parts.append(blk.transpose(1, 0, 2).reshape(128, 32 * 128))
    woutp = np.concatenate(parts, axis=1)

    WbT = np.ascontiguousarray(np.asarray(Wb, f).T)
    parts = []
    for dm in range(NDM):
        blk = WbT[:, dm * 128:(dm + 1) * 128].reshape(NKT, 128, 128)
        parts.append(blk.transpose(1, 0, 2).reshape(128, NKT * 128))
    wbp = np.concatenate(parts, axis=1)

    A2T = np.ascontiguousarray(np.asarray(A, f).reshape(128, H).T)
    apack = A2T.reshape(NKT, 128, 128).transpose(1, 0, 2).reshape(128, NKT * 128)
    bmpack = np.ascontiguousarray(
        np.asarray(Bm, f).transpose(0, 2, 1).reshape(128, H))
    WrT = np.ascontiguousarray(np.asarray(Wr, f).T)
    wrtp = WrT.reshape(NKT, 128, 8).transpose(1, 0, 2).reshape(128, NKT * 8)

    cw = np.asarray(conv_w, f)
    convdiag = np.zeros((128, NXT * 4 * 128), f)
    idx = np.arange(128)
    for i in range(NXT):
        for k in range(4):
            convdiag[idx, (i * 4 + k) * 128 + idx] = cw[i * 128:(i + 1) * 128, k]

    rep2m = np.zeros((8, 128), f)
    for e in range(8):
        rep2m[e, e * 16:(e + 1) * 16] = 2.0

    return dict(
        winp=np.ascontiguousarray(winp),
        woutp=np.ascontiguousarray(woutp),
        wbp=np.ascontiguousarray(wbp),
        apack=np.ascontiguousarray(apack),
        bmpack=bmpack,
        wrtp=np.ascontiguousarray(wrtp),
        convdiag=convdiag,
        i128=np.eye(128, dtype=f),
        i64=np.eye(64, dtype=f),
        trilbig=np.tril(np.full((128, 128), BIGV, f), -1),
        triu01=np.triu(np.ones((128, 128), f)),
        onescol=np.ones((128, 1), f),
        onesrow=np.ones((1, 512), f),
        dtbias=np.asarray(dt_bias, f).reshape(64, 1),
        alog=np.asarray(A_log, f).reshape(64, 1),
        drow=np.asarray(D_param, f).reshape(1, 64),
        normw=np.ascontiguousarray(
            np.asarray(norm_weight, f).reshape(64, 128).T),
        biasrow=np.asarray(bias, f).reshape(1, H),
        rep2=rep2m,
        cvals=np.ascontiguousarray(
            np.stack([np.full(128, EPS, f), np.ones(128, f)], axis=1)),
    )


def kernel(L0_attn_hidden_states, L0_attn_W_in, L0_attn_conv_w, L0_attn_dt_bias,
           L0_attn_A_log, L0_attn_D_param, L0_attn_norm_weight, L0_attn_W_out,
           L0_ffn_Wb, L0_ffn_bias, L0_ffn_Wr, L0_ffn_A, L0_ffn_Bm):
    x = np.asarray(L0_attn_hidden_states, np.float32)
    com = _pack_weights(L0_attn_W_in, L0_attn_conv_w, L0_attn_dt_bias,
                        L0_attn_A_log, L0_attn_D_param, L0_attn_norm_weight,
                        L0_attn_W_out, L0_ffn_Wb, L0_ffn_bias, L0_ffn_Wr,
                        L0_ffn_A, L0_ffn_Bm)
    if "nc" not in _CACHE:
        _CACHE["nc"] = build_program(debug=False)
    nc = _CACHE["nc"]

    in_maps = []
    for c in range(8):
        b, half = c // 2, c % 2
        m = dict(com)
        m["xT"] = np.ascontiguousarray(x[b, half * TPC:(half + 1) * TPC, :].T)
        if half == 0:
            m["xhT"] = np.zeros((H, 4), np.float32)
        else:
            m["xhT"] = np.ascontiguousarray(x[b, TPC - 4:TPC, :].T)
        m["gcol"] = np.full((128, 1), float(half), np.float32)
        in_maps.append(m)

    res = run_bass_kernel_spmd(nc, in_maps, core_ids=list(range(8)))
    out = np.empty((B, S, H), np.float32)
    for c in range(8):
        b, half = c // 2, c % 2
        out[b, half * TPC:(half + 1) * TPC, :] = res.results[c]["out_T"].T
    return out


# version 10
# speedup vs baseline: 1.1278x; 1.0079x over previous
import numpy as np
from contextlib import ExitStack

import concourse.bass as bass
import concourse.mybir as mybir
from concourse.bass_utils import run_bass_kernel_spmd

F32 = mybir.dt.float32
BF16 = mybir.dt.bfloat16
AF = mybir.ActivationFunctionType
ALU = mybir.AluOpType

B, S, H = 4, 2048, 4096
TPC = 1024
NH, P, N, K = 64, 128, 128, 4
DSSM = NH * P
CONV = DSSM + 2 * N
PROJ = DSSM + CONV + NH
NKT = H // 128
NXT = CONV // 128
NZT = DSSM // 128
NDM = H // 128
NC = 8
CL = 128
EPS = 1e-5
BIGV = 1e30

ENGINES = ("tensor", "vector", "scalar", "gpsimd", "sync")
SEMS = ("s_pe", "s_v", "s_ac", "s_gp", "s_gpd", "s_syd", "s_scd", "s_cc")


class Emitter:
    def __init__(self):
        self.q = {e: [] for e in ENGINES}
        self.cnt = {s: 0 for s in SEMS}

    ENG_SEM = {"tensor": "s_pe", "vector": "s_v", "scalar": "s_ac",
               "gpsimd": "s_gp"}

    def op(self, eng, fn, waits=(), inc=None, amt=1, serial=False):
        waits = list(waits)
        if serial:
            own = self.ENG_SEM[eng]
            waits.append((own, self.cnt[own]))
        waits = tuple((s, v) for (s, v) in waits if v)
        self.q[eng].append((fn, waits, inc, amt))
        if inc is not None:
            self.cnt[inc] += amt
            return self.cnt[inc]
        return None

    def dma(self, queue, out, in_, waits=()):
        sem = {"sync": "s_syd", "scalar": "s_scd", "gpsimd": "s_gpd"}[queue]
        return self.op(queue, lambda e: e.dma_start(out=out, in_=in_),
                       waits=waits, inc=sem, amt=16)

    def bar(self, skip=("s_cc",)):
        return [(s, v) for s, v in self.cnt.items() if s not in skip and v]

    def check_deadlock(self):
        ptr = {e: 0 for e in ENGINES}
        cnt = {s: 0 for s in SEMS}
        progress = True
        while progress:
            progress = False
            for e in ENGINES:
                while ptr[e] < len(self.q[e]):
                    fn, waits, inc, amt = self.q[e][ptr[e]]
                    if all(cnt[s] >= v for (s, v) in waits):
                        if inc is not None:
                            cnt[inc] += amt
                        ptr[e] += 1
                        progress = True
                    else:
                        break
        stuck = {e: ptr[e] for e in ENGINES if ptr[e] < len(self.q[e])}
        if stuck:
            lines = []
            for e, i in stuck.items():
                fn, waits, inc, amt = self.q[e][i]
                missing = [(s, v, cnt[s]) for (s, v) in waits if cnt[s] < v]
                lines.append(f"{e}@{i}/{len(self.q[e])} missing={missing}")
            raise RuntimeError("DEADLOCK in schedule:\n" + "\n".join(lines))

    def replay_one(self, ename, eng, sems):
        last_wait = {}
        for fn, waits, inc, amt in self.q[ename]:
            for (s, v) in waits:
                if last_wait.get(s, -1) < v:
                    eng.wait_ge(sems[s], v)
                    last_wait[s] = v
            inst = fn(eng)
            if inc is not None and inst is not None:
                if inc == "s_cc":
                    inst.then_inc(sems[inc])
                else:
                    inst.then_inc(sems[inc], amt)

    def n_ops(self):
        return {e: len(self.q[e]) for e in ENGINES}


class Arena:
    def __init__(self, nelem):
        self.n = nelem
        self.off = 0
        self.marks = []
        self.peak = 0

    def take(self, nelem, align=64):
        o = (self.off + align - 1) // align * align
        self.off = o + nelem
        self.peak = max(self.peak, self.off)
        assert self.off <= self.n, f"SBUF arena overflow: {self.off} > {self.n}"
        return o

    def push(self):
        self.marks.append(self.off)

    def pop(self):
        self.off = self.marks.pop()


def build_program(debug=False):
    nc = bass.Bass("TRN2", target_bir_lowering=False, debug=False)
    em = Emitter()
    C = em.cnt

    def din(name, shape):
        return nc.dram_tensor(name, list(shape), F32, kind="ExternalInput").ap()

    xT = din("xT", (H, TPC))
    xhT = din("xhT", (H, 4))
    winp = din("winp", (128, NKT * PROJ))
    woutp = din("woutp", (128, DSSM * H // 128))
    wbp = din("wbp", (128, NKT * H))
    apack = din("apack", (128, NKT * 128))
    bmpack = din("bmpack", (128, H))
    wrtp = din("wrtp", (128, NKT * 8))
    convdiag = din("convdiag", (128, NXT * 4 * 128))
    i128 = din("i128", (128, 128))
    i64 = din("i64", (64, 64))
    trilbig = din("trilbig", (128, 128))
    triu01 = din("triu01", (128, 128))
    onescol = din("onescol", (128, 1))
    onesrow = din("onesrow", (1, 512))
    dtbias = din("dtbias", (64, 1))
    alog = din("alog", (64, 1))
    drow = din("drow", (1, 64))
    normw = din("normw", (128, 64))
    biasrow = din("biasrow", (1, H))
    gcol = din("gcol", (128, 1))
    rep2 = din("rep2", (8, 128))
    cvals = din("cvals", (128, 2))

    out_T = nc.dram_tensor("out_T", [H, TPC], F32, kind="ExternalOutput").ap()

    d_xbc = nc.dram_tensor("d_xbc", [CONV, TPC], BF16, kind="Internal").ap()
    d_xtm = nc.dram_tensor("d_xtm", [NC * 64 * 128, 128], BF16, kind="Internal").ap()
    d_z = nc.dram_tensor("d_z", [DSSM, TPC], BF16, kind="Internal").ap()
    d_hloc = nc.dram_tensor("d_hloc", [NC * 128, DSSM], BF16, kind="Internal").ap()
    d_ygn = nc.dram_tensor("d_ygn", [DSSM, TPC], BF16, kind="Internal").ap()
    d_attn = nc.dram_tensor("d_attn", [H, TPC], BF16, kind="Internal").ap()
    d_ffn = nc.dram_tensor("d_ffn", [H, TPC], BF16, kind="Internal").ap()
    d_s2 = nc.dram_tensor("d_s2", [H, TPC], BF16, kind="Internal").ap()
    d_part = nc.dram_tensor("d_part", [H, TPC], F32, kind="Internal").ap()
    cc_hin = nc.dram_tensor("cc_hin", [128, DSSM], BF16, kind="Internal").ap()
    cc_hout = nc.dram_tensor("cc_hout", [256, DSSM], BF16, kind="Internal").ap()

    d_dbgxf = nc.dram_tensor("d_dbgxf", [H, TPC], BF16, kind="Internal").ap()
    d_dbgw = nc.dram_tensor("d_dbgw", [128, 3 * TPC], BF16, kind="Internal").ap()
    dbg = {}
    if debug:
        for nm, src in [("xbc", d_xbc), ("z", d_z), ("xtm", d_xtm),
                        ("hloc", d_hloc), ("ygn", d_ygn), ("attn", d_attn),
                        ("ffn", d_ffn), ("hout", cc_hout), ("s2", d_s2),
                        ("xf", d_dbgxf), ("w", d_dbgw)]:
            dbg[nm] = (nc.dram_tensor("dbg_" + nm, list(src.shape), BF16,
                                      kind="ExternalOutput").ap(), src)

    with ExitStack() as ctx:
        BIGN = 74000
        big = ctx.enter_context(nc.sbuf_tensor("big", [128, BIGN], BF16))
        ar = Arena(BIGN)

        def bfv(off, shape):
            v = big[:, off:off + shape[1]]
            return v if shape[0] == 128 else v[0:shape[0], :]

        def f32v(off, shape):
            v = big[:, off:off + 2 * shape[1]].bitcast(F32)
            return v if shape[0] == 128 else v[0:shape[0], :]

        def sb(name, shape, dt):
            return ctx.enter_context(nc.sbuf_tensor(name, list(shape), dt))

        sb_i128 = sb("sb_i128", (128, 128), F32)
        sb_i64 = sb("sb_i64", (64, 64), F32)
        sb_tril = sb("sb_tril", (128, 128), F32)
        sb_triu = sb("sb_triu", (128, 128), BF16)
        sb_identbf = sb("sb_identbf", (128, 128), BF16)
        sb_onescolbf = sb("sb_onescolbf", (128, 1), BF16)
        sb_onesrowf = sb("sb_onesrowf", (1, 512), F32)
        sb_onesrowbf = sb("sb_onesrowbf", (1, 512), BF16)
        sb_dtb = sb("sb_dtb", (64, 1), F32)
        sb_alog = sb("sb_alog", (64, 1), F32)
        sb_drow = sb("sb_drow", (1, 64), F32)
        sb_normw = sb("sb_normw", (128, 64), F32)
        sb_biasbf = sb("sb_biasbf", (1, H), BF16)
        sb_g = sb("sb_g", (128, 1), F32)
        sb_rep2 = sb("sb_rep2", (8, 128), BF16)
        sb_cvals = sb("sb_cvals", (128, 2), F32)
        sb_dt = sb("sb_dt", (64, TPC), F32)
        sb_na = sb("sb_na", (64, TPC), F32)
        sb_sc1 = sb("sb_sc1", (64, TPC), F32)
        sb_sc2 = sb("sb_sc2", (64, TPC), F32)
        sb_natm = sb("sb_natm", (128, NC * 64), F32)
        sb_dttm = sb("sb_dttm", (128, NC * 64), F32)
        sb_nalb = sb("sb_nalb", (128, NC * 64), F32)
        sb_gexp = sb("sb_gexp", (128, NC * 64), F32)
        sb_lstep = sb("sb_lstep", (128, NC * 64), F32)
        sb_wdt = sb("sb_wdt", (128, NC * 64), F32)
        sb_bfm = sb("sb_bfm", (128, TPC), BF16)
        sb_cfm = sb("sb_cfm", (128, TPC), BF16)
        sb_btm = sb("sb_btm", (128, TPC), BF16)
        sb_expa = sb("sb_expa", (64, 1), F32)
        sb_row = sb("sb_row", (1, 128), F32)
        sb_stat = sb("sb_stat", (1, 3 * TPC), F32)

        pb = [ctx.enter_context(nc.psum_tensor(f"pb{i}", [128, 512], F32))
              for i in range(6)]
        pbf = ctx.enter_context(nc.psum_tensor("pbf", [128, 1024], BF16))
        pb7 = ctx.enter_context(nc.psum_tensor("pb7", [128, 512], F32))

        sems = {s: ctx.enter_context(nc.semaphore(s)) for s in SEMS}

        em.dma("sync", sb_i128[:], i128)
        em.dma("sync", sb_i64[:], i64)
        em.dma("sync", sb_tril[:], trilbig)
        em.dma("sync", sb_onesrowf[:], onesrow)
        em.dma("sync", sb_dtb[:], dtbias)
        em.dma("sync", sb_alog[:], alog)
        em.dma("sync", sb_drow[:], drow)
        em.dma("sync", sb_normw[:], normw)
        em.dma("sync", sb_g[:], gcol)
        em.dma("sync", sb_cvals[:], cvals)
        em.dma("gpsimd", sb_identbf[:], i128)
        em.dma("gpsimd", sb_triu[:], triu01)
        em.dma("gpsimd", sb_onescolbf[:], onescol)
        em.dma("gpsimd", sb_onesrowbf[:], onesrow)
        em.dma("gpsimd", sb_biasbf[:], biasrow)
        em.dma("gpsimd", sb_rep2[:], rep2)
        WCONST = [("s_syd", em.cnt["s_syd"]), ("s_gpd", em.cnt["s_gpd"])]

        srow = sb_stat[0:1, 0:TPC]
        qrow = sb_stat[0:1, TPC:2 * TPC]
        trow = sb_stat[0:1, 2 * TPC:3 * TPC]

        def emit_stats_tile(tile, sqtile, first, last, extra_wait_sum=(),
                            extra_wait_sq=(), sums=(2, 3), sqs=(4, 5)):
            toks = []
            for g2 in range(2):
                em.op("tensor", lambda e, g2=g2, t=tile, k0=first, k1=last:
                      e.matmul(pb[sums[g2]][0:1, :], sb_onescolbf[:],
                               t[:, g2 * 512:(g2 + 1) * 512], start=k0, stop=k1),
                      waits=extra_wait_sum, inc=None)
                tok = em.op("tensor", lambda e, g2=g2, t=sqtile, k0=first, k1=last:
                            e.matmul(pb[sqs[g2]][0:1, :], sb_onescolbf[:],
                                     t[:, g2 * 512:(g2 + 1) * 512], start=k0,
                                     stop=k1),
                            waits=extra_wait_sq,
                            inc=("s_pe" if (g2 == 1) else None))
                if tok:
                    toks.append(tok)
            return toks[-1]

        def finalize_stats(mbv, rbv, wtok_pe, denom, sums=(2, 3), sqs=(4, 5)):
            for g2 in range(2):
                em.op("vector", lambda e, g2=g2: e.tensor_copy(
                    srow[:, g2 * 512:(g2 + 1) * 512], pb[sums[g2]][0:1, :]),
                    waits=[("s_pe", wtok_pe)], inc="s_v")
                em.op("vector", lambda e, g2=g2: e.tensor_copy(
                    qrow[:, g2 * 512:(g2 + 1) * 512], pb[sqs[g2]][0:1, :]),
                    waits=[("s_pe", wtok_pe)], inc="s_v")
            t_ev = C["s_v"]
            em.op("scalar", lambda e: e.mul(srow, srow, 1.0 / denom),
                  waits=[("s_v", t_ev)], inc="s_ac", serial=True)
            em.op("scalar", lambda e: e.activation(trow, srow, AF.Square),
                  inc="s_ac", serial=True)
            t_m = C["s_ac"]
            em.op("vector", lambda e: e.scalar_tensor_tensor(
                qrow, qrow, 1.0 / denom, trow, ALU.mult, ALU.subtract),
                waits=[("s_ac", t_m)], inc="s_v", serial=True)
            em.op("scalar", lambda e: e.activation(qrow, qrow, AF.Ln,
                  bias=sb_cvals[0:1, 0:1]),
                  waits=[("s_v", C["s_v"])], inc="s_ac", serial=True)
            em.op("scalar", lambda e: e.activation(qrow, qrow, AF.Exp, scale=-0.5),
                  inc="s_ac", serial=True)
            t_r = C["s_ac"]
            t_last = 0
            for g2 in range(2):
                em.op("tensor", lambda e, g2=g2: e.matmul(
                    pb[0][:, :], sb_onesrowf[0:1, 0:128],
                    srow[:, g2 * 512:(g2 + 1) * 512], start=True, stop=True),
                    waits=[("s_ac", t_r), ("s_v", t_last)], inc="s_pe")
                em.op("tensor", lambda e, g2=g2: e.matmul(
                    pb[1][:, :], sb_onesrowf[0:1, 0:128],
                    qrow[:, g2 * 512:(g2 + 1) * 512], start=True, stop=True),
                    inc="s_pe")
                tb = C["s_pe"]
                em.op("vector", lambda e, g2=g2, mbv=mbv: e.tensor_copy(
                    mbv[:, g2 * 512:(g2 + 1) * 512], pb[0][:, :]),
                    waits=[("s_pe", tb)], inc="s_v")
                em.op("vector", lambda e, g2=g2, rbv=rbv: e.tensor_copy(
                    rbv[:, g2 * 512:(g2 + 1) * 512], pb[1][:, :]),
                    waits=[("s_pe", tb)], inc="s_v")
                t_last = C["s_v"]
            return t_last

        ar.push()
        o_xn = ar.take(NKT * TPC)
        o_sq = ar.take(2 * TPC)
        o_mb = ar.take(2 * TPC)
        o_rb = ar.take(2 * TPC)
        o_xh = ar.take(NKT * 4)
        o_hb = ar.take(64)
        xn = bfv(o_xn, (128, NKT * TPC))
        mb = f32v(o_mb, (128, TPC))
        rb = f32v(o_rb, (128, TPC))
        xh = bfv(o_xh, (128, NKT * 4))
        mhb = f32v(o_hb, (128, 8))[:, 0:4]
        rhb = f32v(o_hb + 16, (128, 8))[:, 0:4]

        t_x = em.dma("gpsimd", xn.rearrange("p (a t) -> p a t", a=NKT),
                     xT.rearrange("(a p) t -> p a t", p=128), waits=WCONST)
        t_xh = em.dma("gpsimd", xh.rearrange("p (a t) -> p a t", a=NKT),
                      xhT.rearrange("(a p) t -> p a t", p=128), waits=WCONST)

        t_sqpe = [0, 0]
        for kt in range(NKT):
            xt = xn[:, kt * TPC:(kt + 1) * TPC]
            slot = kt % 2
            sq = bfv(o_sq + slot * TPC, (128, TPC))
            em.op("scalar", lambda e, a=sq, b=xt: e.activation(a, b, AF.Square),
                  waits=[("s_gpd", t_x), ("s_pe", t_sqpe[slot])], inc="s_ac")
            t_aq = C["s_ac"]
            t_sqpe[slot] = emit_stats_tile(
                xt, sq, kt == 0, kt == NKT - 1,
                extra_wait_sum=[("s_gpd", t_x)],
                extra_wait_sq=[("s_ac", t_aq)])
        t_stats_mm = C["s_pe"]
        t_ln1 = finalize_stats(mb, rb, t_stats_mm, H)

        for kt in range(NKT):
            xt = xn[:, kt * TPC:(kt + 1) * TPC]
            em.op("vector", lambda e, a=xt: e.tensor_sub(a, a, mb),
                  waits=[("s_v", t_ln1), ("s_pe", t_stats_mm)], inc="s_v")
            em.op("vector", lambda e, a=xt: e.tensor_mul(a, a, rb), inc="s_v")
        t_xn = C["s_v"]

        hsq = bfv(o_sq, (128, NKT * 4))
        em.op("scalar", lambda e: e.activation(hsq, xh, AF.Square),
              waits=[("s_gpd", t_xh), ("s_pe", t_stats_mm)], inc="s_ac")
        t_hq = C["s_ac"]
        for kt in range(NKT):
            em.op("tensor", lambda e, kt=kt, k0=(kt == 0), k1=(kt == NKT - 1):
                  e.matmul(pb7[0:1, 0:4], sb_onescolbf[:],
                           xh[:, kt * 4:(kt + 1) * 4], start=k0, stop=k1),
                  waits=([("s_gpd", t_xh)] if kt == 0 else ()), inc=None)
            em.op("tensor", lambda e, kt=kt, k0=(kt == 0), k1=(kt == NKT - 1):
                  e.matmul(pb7[0:1, 4:8], sb_onescolbf[:],
                           hsq[:, kt * 4:(kt + 1) * 4], start=k0, stop=k1),
                  waits=([("s_ac", t_hq)] if kt == 0 else ()),
                  inc=("s_pe" if kt == NKT - 1 else None))
        t_hstat = C["s_pe"]
        hrow = sb_stat[0:1, 2 * TPC + 64:2 * TPC + 68]
        hrow2 = sb_stat[0:1, 2 * TPC + 68:2 * TPC + 72]
        hrow3 = sb_stat[0:1, 2 * TPC + 72:2 * TPC + 76]
        em.op("vector", lambda e: e.tensor_copy(hrow, pb7[0:1, 0:4]),
              waits=[("s_pe", t_hstat)], inc="s_v")
        em.op("vector", lambda e: e.tensor_copy(hrow2, pb7[0:1, 4:8]),
              waits=[("s_pe", t_hstat)], inc="s_v")
        em.op("scalar", lambda e: e.mul(hrow, hrow, 1.0 / H),
              waits=[("s_v", C["s_v"])], inc="s_ac", serial=True)
        em.op("scalar", lambda e: e.activation(hrow3, hrow, AF.Square),
              inc="s_ac", serial=True)
        em.op("vector", lambda e: e.scalar_tensor_tensor(
              hrow2, hrow2, 1.0 / H, hrow3, ALU.mult, ALU.subtract),
              waits=[("s_ac", C["s_ac"])], inc="s_v", serial=True)
        em.op("scalar", lambda e: e.activation(hrow2, hrow2, AF.Ln,
              bias=sb_cvals[0:1, 0:1]), waits=[("s_v", C["s_v"])], inc="s_ac",
              serial=True)
        em.op("scalar", lambda e: e.activation(hrow2, hrow2, AF.Exp, scale=-0.5),
              inc="s_ac", serial=True)
        t_hr = C["s_ac"]
        em.op("tensor", lambda e: e.matmul(pb7[:, 0:4], sb_onesrowf[0:1, 0:128],
              hrow, start=True, stop=True),
              waits=[("s_ac", t_hr), ("s_v", C["s_v"])], inc="s_pe")
        em.op("tensor", lambda e: e.matmul(pb7[:, 4:8], sb_onesrowf[0:1, 0:128],
              hrow2, start=True, stop=True), inc="s_pe")
        t_hbc = C["s_pe"]
        em.op("vector", lambda e: e.tensor_copy(mhb, pb7[:, 0:4]),
              waits=[("s_pe", t_hbc)], inc="s_v")
        em.op("vector", lambda e: e.tensor_copy(rhb, pb7[:, 4:8]),
              waits=[("s_pe", t_hbc)], inc="s_v")
        t_hln = C["s_v"]
        for kt in range(NKT):
            xht = xh[:, kt * 4:(kt + 1) * 4]
            em.op("vector", lambda e, a=xht: e.tensor_sub(a, a, mhb),
                  waits=[("s_v", t_hln), ("s_pe", t_hstat)], inc="s_v",
                  serial=True)
            em.op("vector", lambda e, a=xht: e.tensor_mul(a, a, rhb), inc="s_v",
                  serial=True)
        t_xnh = C["s_v"]
        t_pb7A = ("s_v", t_hln)

        o_wsl = ar.take(2 * NKT * 128)
        o_bst = ar.take(2 * TPC)
        o_halo = ar.take(NXT * 4)
        halo_all = bfv(o_halo, (128, NXT * 4))

        WOFF = [0]
        t_wsl = [None, None]
        t_evac = {i: ("s_v", t_ln1) for i in range(6)}
        pe_rot4 = (2, 3, 4, 5)

        t_hpb7 = [t_pb7A]

        def stream_tile(ti, width, on_evac, rot=pe_rot4, halo=False):
            slot = ti % 2
            wsl = bfv(o_wsl + slot * NKT * 128, (128, NKT * width))
            w = list(WCONST)
            if t_wsl[slot] is not None:
                w.append(t_wsl[slot])
            t_w = em.dma("gpsimd", wsl, winp[:, WOFF[0]:WOFF[0] + NKT * width],
                         waits=w)
            WOFF[0] += NKT * width
            if halo:
                for kt in range(NKT):
                    em.op("tensor", lambda e, kt=kt,
                          lw=wsl[:, kt * width:(kt + 1) * width],
                          k0=(kt == 0), k1=(kt == NKT - 1):
                          e.matmul(pb7[:, 256:260], lw,
                                   xh[:, kt * 4:(kt + 1) * 4],
                                   start=k0, stop=k1),
                          waits=([("s_gpd", t_w), ("s_v", t_xnh),
                                  t_hpb7[0]] if kt == 0 else ()),
                          inc=("s_pe" if kt == NKT - 1 else None))
                t_hmm = C["s_pe"]
                em.op("vector", lambda e, ti=ti: e.tensor_copy(
                    halo_all[:, ti * 4:(ti + 1) * 4], pb7[:, 256:260]),
                    waits=[("s_pe", t_hmm)], inc="s_v")
                t_hpb7[0] = ("s_v", C["s_v"])
            last_tok = None
            for half in range(2):
                bank = rot[(2 * ti + half) % len(rot)]
                wv = [("s_gpd", t_w), ("s_v", t_xn), t_evac[bank]]
                for kt in range(NKT):
                    em.op("tensor", lambda e, ps=pb[bank], wd=width,
                          lw=wsl[:, kt * width:(kt + 1) * width],
                          rhs=xn[:, kt * TPC + half * 512: kt * TPC + (half + 1) * 512],
                          k0=(kt == 0), k1=(kt == NKT - 1):
                          e.matmul(ps[0:wd, :] if wd < 128 else ps[:, :],
                                   lw, rhs, start=k0, stop=k1),
                          waits=(wv if kt == 0 else ()),
                          inc=("s_pe" if kt == NKT - 1 else None))
                t_pe = C["s_pe"]
                tok = on_evac(half, bank, t_pe)
                t_evac[bank] = tok
                last_tok = tok
            t_wsl[slot] = last_tok
            return last_tok

        t_bsp = [0, 0]
        for ti in range(NXT):
            slot2 = ti % 2
            bst = bfv(o_bst + slot2 * TPC, (128, TPC))

            def ev(half, bank, t_pe, bst=bst, slot2=slot2):
                w = [("s_pe", t_pe)]
                if half == 0 and t_bsp[slot2]:
                    w.append(("s_syd", t_bsp[slot2]))
                em.op("vector", lambda e, half=half, bank=bank, bst=bst:
                      e.tensor_copy(bst[:, half * 512:(half + 1) * 512],
                                    pb[bank][:, :]), waits=w, inc="s_v")
                return ("s_v", C["s_v"])

            stream_tile(ti, 128, ev, halo=True)
            t_bsp[slot2] = em.dma("sync", d_xbc[ti * 128:(ti + 1) * 128, :],
                                  bst[:], waits=[("s_v", C["s_v"])])
        t_b1_spill = em.cnt["s_syd"]
        t_halo = t_hpb7[0][1]

        def ev_dt(half, bank, t_pe):
            em.op("vector", lambda e, half=half, bank=bank: e.tensor_copy(
                sb_sc1[0:64, half * 512:(half + 1) * 512], pb[bank][0:64, :]),
                waits=[("s_pe", t_pe)], inc="s_v")
            return ("s_v", C["s_v"])
        stream_tile(NXT, 64, ev_dt)
        t_dtraw = C["s_v"]


        em.op("vector", lambda e: e.tensor_scalar(sb_sc1[0:64, :], sb_sc1[0:64, :],
              sb_dtb[:, 0:1], None, ALU.add), waits=[("s_v", t_dtraw)], inc="s_v")
        t_u = C["s_v"]
        em.op("scalar", lambda e: e.activation(sb_sc2[0:64, :], sb_sc1[0:64, :],
              AF.Abs), waits=[("s_v", t_u)], inc="s_ac", serial=True)
        em.op("scalar", lambda e: e.activation(sb_sc2[0:64, :], sb_sc2[0:64, :],
              AF.Exp, scale=-1.0), inc="s_ac", serial=True)
        em.op("scalar", lambda e: e.activation(sb_sc2[0:64, :], sb_sc2[0:64, :],
              AF.Ln, bias=sb_cvals[0:64, 1:2]), inc="s_ac", serial=True)
        t_lnx = C["s_ac"]
        em.op("vector", lambda e: e.tensor_scalar(sb_dt[0:64, :], sb_sc1[0:64, :],
              0.0, None, ALU.max), waits=[("s_ac", t_lnx)], inc="s_v")
        em.op("vector", lambda e: e.tensor_add(sb_dt[0:64, :], sb_dt[0:64, :],
              sb_sc2[0:64, :]), inc="s_v")
        t_dt = C["s_v"]
        em.op("scalar", lambda e: e.activation(sb_expa[:, 0:1], sb_alog[:, 0:1],
              AF.Exp), waits=WCONST, inc="s_ac")
        t_expa = C["s_ac"]
        em.op("vector", lambda e: e.tensor_scalar(sb_sc1[0:64, :], sb_dt[0:64, :],
              sb_expa[:, 0:1], None, ALU.mult),
              waits=[("s_v", t_dt), ("s_ac", t_expa)], inc="s_v")
        em.op("vector", lambda e: e.tensor_tensor_scan(
              sb_na[0:64, :], sb_sc1[0:64, :], sb_sc2[0:64, :], 0.0,
              ALU.add, ALU.bypass), inc="s_v")
        t_na = C["s_v"]

        t_pb7 = t_hpb7[0]
        for c in range(NC):
            em.op("tensor", lambda e, c=c: e.transpose(
                pb7[:, 0:64], sb_na[0:64, c * CL:(c + 1) * CL], sb_i64[:]),
                waits=[("s_v", t_na), t_pb7], inc="s_pe")
            em.op("vector", lambda e, c=c, t=C["s_pe"]: e.tensor_copy(
                sb_natm[:, c * 64:(c + 1) * 64], pb7[:, 0:64]),
                waits=[("s_pe", C["s_pe"])], inc="s_v")
            t_pb7 = ("s_v", C["s_v"])
            em.op("tensor", lambda e, c=c: e.transpose(
                pb7[:, 0:64], sb_dt[0:64, c * CL:(c + 1) * CL], sb_i64[:]),
                waits=[("s_v", t_na), t_pb7], inc="s_pe")
            em.op("vector", lambda e, c=c: e.tensor_copy(
                sb_dttm[:, c * 64:(c + 1) * 64], pb7[:, 0:64]),
                waits=[("s_pe", C["s_pe"])], inc="s_v")
            t_pb7 = ("s_v", C["s_v"])
            em.op("tensor", lambda e, c=c: e.transpose(
                pb7[0:1, 0:64], sb_na[0:64, c * CL + 127:c * CL + 128],
                sb_i64[:]), waits=[t_pb7], inc="s_pe")
            em.op("vector", lambda e: e.tensor_copy(sb_row[0:1, 0:64],
                  pb7[0:1, 0:64]), waits=[("s_pe", C["s_pe"])], inc="s_v")
            t_pb7 = ("s_v", C["s_v"])
            em.op("tensor", lambda e: e.matmul(pb7[:, 0:64],
                  sb_onesrowf[0:1, 0:128], sb_row[0:1, 0:64],
                  start=True, stop=True), waits=[t_pb7], inc="s_pe")
            em.op("vector", lambda e, c=c: e.tensor_copy(
                sb_nalb[:, c * 64:(c + 1) * 64], pb7[:, 0:64]),
                waits=[("s_pe", C["s_pe"])], inc="s_v")
            t_pb7 = ("s_v", C["s_v"])
            if c == 0:
                em.op("vector", lambda e: e.tensor_scalar(
                    sb_gexp[:, 0:64], sb_natm[:, 0:64], 0.0, sb_g[:, 0:1],
                    ALU.mult, ALU.add), inc="s_v")
                em.op("vector", lambda e: e.tensor_scalar(
                    sb_gexp[:, 0:64], sb_gexp[:, 0:64], 0.0, None, ALU.mult),
                    inc="s_v")
                em.op("vector", lambda e: e.tensor_scalar(
                    sb_gexp[:, 0:64], sb_gexp[:, 0:64], sb_g[:, 0:1], None,
                    ALU.add), inc="s_v")
            else:
                em.op("scalar", lambda e, c=c: e.activation(
                    sb_gexp[:, c * 64:(c + 1) * 64],
                    sb_nalb[:, (c - 1) * 64:c * 64], AF.Exp, scale=-1.0),
                    waits=[t_pb7] if c == 1 else [("s_v", C["s_v"])], inc="s_ac")
                em.op("vector", lambda e, c=c: e.tensor_scalar(
                    sb_gexp[:, c * 64:(c + 1) * 64],
                    sb_gexp[:, c * 64:(c + 1) * 64], sb_g[:, 0:1], None,
                    ALU.mult), waits=[("s_ac", C["s_ac"])], inc="s_v")
            em.op("vector", lambda e, c=c: e.tensor_sub(
                sb_wdt[:, c * 64:(c + 1) * 64], sb_natm[:, c * 64:(c + 1) * 64],
                sb_nalb[:, c * 64:(c + 1) * 64]),
                waits=[("s_v", C["s_v"])], inc="s_v")
            em.op("scalar", lambda e, c=c: e.activation(
                sb_wdt[:, c * 64:(c + 1) * 64], sb_wdt[:, c * 64:(c + 1) * 64],
                AF.Exp), waits=[("s_v", C["s_v"])], inc="s_ac")
            em.op("vector", lambda e, c=c: e.tensor_mul(
                sb_wdt[:, c * 64:(c + 1) * 64], sb_wdt[:, c * 64:(c + 1) * 64],
                sb_dttm[:, c * 64:(c + 1) * 64]),
                waits=[("s_ac", C["s_ac"])], inc="s_v")
            if c > 0:
                em.op("vector", lambda e, c=c: e.tensor_sub(
                    sb_lstep[:, c * 64:(c + 1) * 64],
                    sb_nalb[:, (c - 1) * 64:c * 64],
                    sb_nalb[:, c * 64:(c + 1) * 64]), inc="s_v")
                em.op("scalar", lambda e, c=c: e.activation(
                    sb_lstep[:, c * 64:(c + 1) * 64],
                    sb_lstep[:, c * 64:(c + 1) * 64], AF.Exp),
                    waits=[("s_v", C["s_v"])], inc="s_ac")
        t_prep = C["s_v"]
        t_prep_ac = C["s_ac"]
        t_prep_pb7 = t_pb7

        t_zsp = list(t_bsp)
        NCS = 4
        o_cin = ar.take(NCS * 1056)
        o_cd = ar.take(NCS * 512)
        o_cout = ar.take(NCS * TPC)
        o_tst = ar.take(2 * TPC)

        t_cin = [None] * NCS
        t_cds = [None] * NCS
        t_cout = [("s_v", 0)] * NCS
        t_tsp = [0, 0]
        t_tr = [0]
        t_convK = {4: t_evac[4], 5: t_evac[5]}

        def z_tile(zi):
            slot2 = zi % 2
            zst = bfv(o_bst + slot2 * TPC, (128, TPC))

            def evz(half, bank, t_pe, zst=zst, slot2=slot2):
                w = [("s_pe", t_pe)]
                if half == 0 and t_zsp[slot2]:
                    w.append(("s_syd", t_zsp[slot2]))
                em.op("scalar", lambda e, half=half, bank=bank, zst=zst:
                      e.activation(zst[:, half * 512:(half + 1) * 512],
                                   pb[bank][:, :], AF.Silu), waits=w, inc="s_ac")
                return ("s_ac", C["s_ac"])

            stream_tile(NXT + 1 + zi, 128, evz, rot=(2, 3))
            t_zsp[slot2] = em.dma("sync", d_z[zi * 128:(zi + 1) * 128, :],
                                  zst[:], waits=[("s_ac", C["s_ac"])])

        conv_pre = {}

        def conv_prefetch(ti):
            slot2 = ti % NCS
            cin = bfv(o_cin + slot2 * 1056, (128, 1056))
            cds = bfv(o_cd + slot2 * 512, (128, 512))
            w = [("s_syd", t_b1_spill)]
            if t_cin[slot2] is not None:
                w.append(t_cin[slot2])
            t_rx = em.dma("scalar", cin[:, 3:1027],
                          d_xbc[ti * 128:(ti + 1) * 128, :], waits=w)
            em.op("vector", lambda e, cin=cin, ti=ti: e.tensor_copy(
                cin[:, 0:3], halo_all[:, ti * 4 + 1:(ti + 1) * 4]),
                waits=[("s_v", t_halo), ("s_scd", t_rx)], inc="s_v")
            wd = list(WCONST)
            if t_cds[slot2] is not None:
                wd.append(t_cds[slot2])
            t_cd = em.dma("gpsimd", cds, convdiag[:, ti * 512:(ti + 1) * 512],
                          waits=wd)
            conv_pre[ti] = (C["s_v"], t_rx, t_cd)

        def conv_tile(ti):
            slot2 = ti % NCS
            cin = bfv(o_cin + slot2 * 1056, (128, 1056))
            cds = bfv(o_cd + slot2 * 512, (128, 512))
            t_hcp, t_rx, t_cd = conv_pre[ti]
            if ti == 64:
                cout = sb_bfm
            elif ti == 65:
                cout = sb_cfm
            else:
                cout = bfv(o_cout + slot2 * TPC, (128, TPC))
            for half in range(2):
                bank = 4 + half
                wv = [("s_scd", t_rx), ("s_gpd", t_cd), ("s_v", t_hcp),
                      t_cout[slot2], t_convK[bank]]
                for k in range(4):
                    em.op("tensor", lambda e, ps=pb[bank],
                          lw=cds[:, k * 128:(k + 1) * 128],
                          rhs=cin[:, half * 512 + k: half * 512 + k + 512],
                          k0=(k == 0), k1=(k == 3):
                          e.matmul(ps[:, :], lw, rhs, start=k0, stop=k1),
                          waits=(wv if k == 0 else ()),
                          inc=("s_pe" if k == 3 else None))
                t_cmm = C["s_pe"]
                em.op("scalar", lambda e, o=cout[:, half * 512:(half + 1) * 512],
                      ps=pb[bank]: e.activation(o, ps[:, :], AF.Silu),
                      waits=[("s_pe", t_cmm)], inc="s_ac")
                t_convK[bank] = ("s_ac", C["s_ac"])
            t_sil = ("s_ac", C["s_ac"])
            t_cin[slot2] = ("s_pe", C["s_pe"])
            t_cds[slot2] = ("s_pe", C["s_pe"])

            if ti == 64:
                for cc in range(8):
                    em.op("tensor", lambda e, cc=cc: e.transpose(
                        pbf[:, cc * 128:(cc + 1) * 128],
                        sb_bfm[:, cc * 128:(cc + 1) * 128], sb_identbf[:]),
                        waits=([t_sil, ("s_v", t_tr[0])] if cc == 0 else ()),
                        inc=("s_pe" if cc == 7 else None))
                t_t4 = C["s_pe"]
                for cc in range(8):
                    em.op("vector", lambda e, cc=cc: e.tensor_copy(
                        sb_btm[:, cc * 128:(cc + 1) * 128],
                        pbf[:, cc * 128:(cc + 1) * 128]),
                        waits=([("s_pe", t_t4)] if cc == 0 else ()), inc="s_v")
                t_tr[0] = C["s_v"]
                t_cout[slot2] = t_sil
                return
            if ti == 65:
                t_cout[slot2] = t_sil
                return

            ts2 = ti % 2
            tst = bfv(o_tst + ts2 * TPC, (128, TPC))
            for cc in range(8):
                wv = [t_sil, ("s_v", t_tr[0])]
                if cc == 0 and t_tsp[ts2]:
                    wv.append(("s_syd", t_tsp[ts2]))
                em.op("tensor", lambda e, cc=cc, src=cout: e.transpose(
                    pbf[:, cc * 128:(cc + 1) * 128],
                    src[:, cc * 128:(cc + 1) * 128], sb_identbf[:]),
                    waits=(wv if cc == 0 else ()),
                    inc=("s_pe" if cc == 7 else None))
            t_t4 = C["s_pe"]
            for cc in range(8):
                em.op("vector", lambda e, cc=cc, tst=tst: e.tensor_copy(
                    tst[:, cc * 128:(cc + 1) * 128],
                    pbf[:, cc * 128:(cc + 1) * 128]),
                    waits=([("s_pe", t_t4)] if cc == 0 else ()), inc="s_v")
            t_tr[0] = C["s_v"]
            t_cout[slot2] = ("s_pe", C["s_pe"])
            t_tsp[ts2] = em.dma(
                "sync", d_xtm.rearrange("(c h s) p -> h s c p", c=NC, h=64)[ti],
                tst.rearrange("s (c p) -> s c p", c=NC),
                waits=[("s_v", t_tr[0])])

        LAGC = 2
        for i in range(NXT + LAGC):
            if i < NXT:
                conv_prefetch(i)
            if i < NZT:
                z_tile(i)
            if i >= LAGC:
                conv_tile(i - LAGC)
        t_zdone = em.cnt["s_syd"]
        t_tr_evac = t_tr[0]
        t_b2 = em.cnt["s_syd"]
        t_btm = ("s_v", t_tr_evac)
        t_bcfm = [t_cout[0], t_cout[1]]
        t_conv_final = [t_convK[4], t_convK[5]]

        ar.pop()
        ar.push()
        o_xtm1 = ar.take(2 * DSSM)
        o_hA = ar.take(2 * DSSM)
        o_hB = ar.take(2 * DSSM)
        o_hbs = ar.take(DSSM)
        o_xw = ar.take(2 * 128)
        hbuf = [f32v(o_hA, (128, DSSM)), f32v(o_hB, (128, DSSM))]
        hbstage = bfv(o_hbs, (128, DSSM))

        t_xtm1 = [None, None]
        t_hb_spill = 0
        t_chain_v = 0
        t_lx_arr = {}

        def p1_load(cc):
            slot2 = cc % 2
            xtm1 = bfv(o_xtm1 + slot2 * DSSM, (128, DSSM))
            w = [("s_syd", t_b2)]
            if cc == 0:
                w += t_bcfm + [t_btm]
            if t_xtm1[slot2] is not None:
                w.append(t_xtm1[slot2])
            t_lx_arr[cc] = em.dma(
                "scalar", xtm1.rearrange("s (h p) -> s h p", h=64),
                d_xtm.rearrange("(c h s) p -> c s h p", c=NC, h=64)[cc],
                waits=w)

        p1_load(0)
        for c in range(NC):
            slot2 = c % 2
            xtm1 = bfv(o_xtm1 + slot2 * DSSM, (128, DSSM))
            if c + 1 < NC:
                p1_load(c + 1)
            t_lx = t_lx_arr[c]
            t_ls = t_prep_ac
            t_wd = t_prep

            hs_src = hbuf[c % 2]
            hs_dst = hbuf[(c + 1) % 2]
            t_smm = {}
            tok_xw = {}
            tok_stt = {}
            PL = 6

            def p1_front(h):
                xw = bfv(o_xw + (h % 8) * 128, (128, 128))
                wv = [("s_scd", t_lx), ("s_v", t_wd), ("s_ac", t_ls),
                      ("s_pe", t_smm.get(h - 8, 0))]
                em.op("vector", lambda e, h=h, xw=xw, xtm1=xtm1, c=c:
                      e.tensor_scalar(
                          xw, xtm1[:, h * 128:(h + 1) * 128],
                          sb_wdt[:, c * 64 + h:c * 64 + h + 1],
                          None, ALU.mult), waits=wv, inc="s_v")
                tok_xw[h] = C["s_v"]
                bank = h % 6
                wv2 = [("s_v", tok_xw[h]), t_btm,
                       ("s_v", tok_stt.get(h - 6, 0))]
                if c == 0 and h in (4, 5):
                    wv2.append(t_conv_final[h % 2])
                em.op("tensor", lambda e, h=h, xw=xw, bank=bank, c=c: e.matmul(
                    pb[bank][:, 0:128], sb_btm[:, c * 128:(c + 1) * 128], xw,
                    start=True, stop=True), waits=wv2, inc="s_pe")
                t_smm[h] = C["s_pe"]

            def p1_back(h):
                bank = h % 6
                if c == 0:
                    em.op("vector", lambda e, h=h, bank=bank, d=hs_dst:
                          e.tensor_copy(d[:, h * 128:(h + 1) * 128],
                                        pb[bank][:, 0:128]),
                          waits=[("s_pe", t_smm[h])], inc="s_v")
                else:
                    em.op("vector", lambda e, h=h, bank=bank, d=hs_dst, s=hs_src:
                          e.scalar_tensor_tensor(
                              d[:, h * 128:(h + 1) * 128],
                              s[:, h * 128:(h + 1) * 128],
                              sb_lstep[:, c * 64 + h:c * 64 + h + 1],
                              pb[bank][:, 0:128], ALU.mult, ALU.add),
                          waits=[("s_pe", t_smm[h])], inc="s_v")
                tok_stt[h] = C["s_v"]
                wv2 = [("s_v", tok_stt[h])]
                if h == 0 and t_hb_spill:
                    wv2.append(("s_syd", t_hb_spill))
                em.op("scalar", lambda e, h=h, d=hs_dst: e.copy(
                    hbstage[:, h * 128:(h + 1) * 128],
                    d[:, h * 128:(h + 1) * 128]), waits=wv2, inc="s_ac")

            for i in range(NH + PL):
                if i < NH:
                    p1_front(i)
                if i >= PL:
                    p1_back(i - PL)
            t_chain_v = C["s_v"]
            t_chain_ac = C["s_ac"]
            t_xtm1[slot2] = ("s_v", t_chain_v)
            dstp = cc_hin if c == NC - 1 else d_hloc[c * 128:(c + 1) * 128, :]
            t_hb_spill = em.dma("sync", dstp, hbstage,
                                waits=[("s_ac", t_chain_ac)])
        t_chain_final = t_chain_v
        em.op("gpsimd", lambda e: e.collective_compute(
            "AllGather", ALU.bypass,
            replica_groups=[[0, 1], [2, 3], [4, 5], [6, 7]],
            ins=[cc_hin.opt()], outs=[cc_hout.opt()]),
            waits=[("s_syd", t_hb_spill)], inc="s_cc")
        t_ag2 = C["s_cc"]
        ar.pop()

        ar.push()
        o_xtm = ar.take(2 * DSSM)
        o_hpeer = ar.take(DSSM)
        o_hloc = ar.take(DSSM)
        o_zc = ar.take(DSSM)
        o_yc = ar.take(DSSM)
        o_idsc = ar.take(DSSM)
        NSLOT = 10
        KLAG = 7
        o_g = ar.take(128)
        o_e = ar.take(2 * 128)
        o_e2 = ar.take(2 * 128)
        o_ar2 = ar.take(2 * 128)
        o_mp = ar.take(NSLOT * 128)
        o_cpp = ar.take(NSLOT * 128)
        o_hbt = ar.take(NSLOT * 128)
        o_rb2 = ar.take(256)
        o_dbc = ar.take(128)
        hpeer = bfv(o_hpeer, (128, DSSM))
        hloc = bfv(o_hloc, (128, DSSM))
        ych = bfv(o_yc, (128, DSSM))
        zc = bfv(o_zc, (128, DSSM))
        idsc = bfv(o_idsc, (128, DSSM))
        gtile = bfv(o_g, (128, 128))
        rb2 = f32v(o_rb2, (128, 128))
        dbc = f32v(o_dbc, (128, 64))

        t_hpeer = em.dma("scalar", hpeer, cc_hout[0:128, :],
                         waits=[("s_cc", t_ag2)])
        em.op("tensor", lambda e: e.matmul(pb7[:, 0:64], sb_onesrowf[0:1, 0:128],
              sb_drow[0:1, 0:64], start=True, stop=True),
              waits=WCONST + [t_prep_pb7], inc="s_pe")
        t_db = C["s_pe"]
        em.op("vector", lambda e: e.tensor_copy(dbc, pb7[:, 0:64]),
              waits=[("s_pe", t_db)], inc="s_v")
        t_dbc = C["s_v"]
        for h in range(NH):
            em.op("vector", lambda e, h=h: e.tensor_scalar(
                idsc[:, h * 128:(h + 1) * 128], sb_identbf[:], dbc[:, h:h + 1],
                None, ALU.mult),
                waits=[("s_v", t_dbc), ("s_syd", t_hb_spill)], inc="s_v")
        t_idsc = C["s_v"]

        t_xtm2 = [None, None]
        t_zld = 0
        t_hlld = 0
        t_ygn_spill = 0
        t_rmsmm = 0
        gi_pA = {}
        gi_pB = {}
        gi_AR = {}
        gi_HBT = {}
        gi_Y = {}
        gi_Yev = {}
        gi_MP = {}
        gi_CPP = {}
        t_gv = 0
        t_rbv = 0
        for c in range(NC):
            slot2 = c % 2
            xtm = bfv(o_xtm + slot2 * DSSM, (128, DSSM))
            w = [("s_syd", t_b2)]
            if c == 0:
                w.append(("s_syd", t_hb_spill))
            if t_xtm2[slot2] is not None:
                w.append(t_xtm2[slot2])
            t_lx = em.dma("scalar", xtm.rearrange("s (h p) -> s h p", h=64),
                          d_xtm.rearrange("(c h s) p -> c s h p", c=NC, h=64)[c],
                          waits=w)
            wz = [("s_syd", t_zdone)]
            if c == 0:
                wz.append(("s_syd", t_hb_spill))
            if t_ygn_spill:
                wz.append(("s_syd", t_ygn_spill))
            if t_rmsmm:
                wz.append(("s_pe", t_rmsmm))
            t_lz = em.dma("sync", zc.rearrange("p (h t) -> p h t", h=64),
                          d_z.rearrange("(h p) t -> p h t", p=128)[:, :, c * CL:(c + 1) * CL],
                          waits=wz)
            if c > 0:
                whl = [("s_syd", t_hb_spill)]
                if t_hlld:
                    whl.append(("s_v", t_hlld))
                t_lh = em.dma("scalar", hloc, d_hloc[(c - 1) * 128:c * 128, :],
                              waits=whl)
            else:
                t_lh = 0

            em.op("tensor", lambda e, c=c: e.matmul(
                pb7[:, 128:256], sb_bfm[:, c * CL:(c + 1) * CL],
                sb_cfm[:, c * CL:(c + 1) * CL], start=True, stop=True),
                waits=t_bcfm + [("s_v", t_dbc), ("s_v", t_rbv)] +
                ([("s_syd", t_hb_spill)] if c == 0 else []), inc="s_pe")
            t_g = C["s_pe"]
            em.op("vector", lambda e: e.tensor_copy(gtile, pb7[:, 128:256]),
                  waits=[("s_pe", t_g)], inc="s_v")
            t_gv = C["s_v"]

            def front(h):
                gi = c * NH + h
                par = h % 2
                bA = par
                slot = gi % NSLOT
                E = bfv(o_e + par * 128, (128, 128))
                E2 = bfv(o_e2 + par * 128, (128, 128))
                AR = bfv(o_ar2 + par * 128, (128, 128))
                MP = bfv(o_mp + slot * 128, (128, 128))
                CPP = bfv(o_cpp + slot * 128, (128, 128))
                HBT = bfv(o_hbt + slot * 128, (128, 128))

                bB = 2 + par
                wv = [("s_v", t_na), ("s_v", t_prep),
                      ("s_ac", gi_pA.get(gi - 2, 0))]
                if gi < 2:
                    wv.append(("s_v", t_chain_final))
                em.op("tensor", lambda e, h=h, c=c, bA=bA: e.matmul(
                    pb[bA][:, 0:128],
                    sb_i64[:, h:h + 1].to_broadcast([64, 128]),
                    sb_na[0:64, c * CL:(c + 1) * CL], start=True, stop=True),
                    waits=wv, inc="s_pe")
                t_pA = C["s_pe"]
                wvB = [("s_ac", gi_pB.get(gi - 2, 0))]
                if gi < 2:
                    wvB.append(("s_v", t_chain_final))
                em.op("tensor", lambda e, h=h, c=c, bB=bB: e.matmul(
                    pb[bB][:, 0:128],
                    sb_i64[:, h:h + 1].to_broadcast([64, 128]),
                    sb_na[0:64, c * CL:(c + 1) * CL], start=True, stop=False),
                    waits=wvB, inc=None)
                em.op("tensor", lambda e, bB=bB: e.matmul(
                    pb[bB][:, 0:128], sb_i128[:], sb_tril[:],
                    start=False, stop=True), inc="s_pe")
                t_pB = C["s_pe"]
                if c == 0:
                    em.op("scalar", lambda e, E2=E2, bA=bA: e.activation(
                        E2, pb[bA][:, 0:128], AF.Exp, scale=-1.0),
                        waits=[("s_pe", t_pA), ("s_v", gi_CPP.get(gi - 2, 0))],
                        inc="s_ac")
                else:
                    em.op("scalar", lambda e, E2=E2, bA=bA, c=c, h=h: e.activation(
                        E2, pb[bA][:, 0:128], AF.Exp, scale=-1.0,
                        bias=sb_nalb[:, (c - 1) * 64 + h:(c - 1) * 64 + h + 1]),
                        waits=[("s_pe", t_pA), ("s_v", gi_CPP.get(gi - 2, 0))],
                        inc="s_ac")
                gi_pA[gi] = C["s_ac"]
                em.op("scalar", lambda e, E=E, bB=bB, c=c, h=h: e.activation(
                    E, pb[bB][:, 0:128], AF.Exp, scale=-1.0,
                    bias=sb_natm[:, c * 64 + h:c * 64 + h + 1]),
                    waits=[("s_pe", t_pB), ("s_v", gi_MP.get(gi - 2, 0))],
                    inc="s_ac")
                gi_pB[gi] = C["s_ac"]
                t_e = C["s_ac"]
                em.op("vector", lambda e, MP=MP, E=E, c=c, h=h:
                      e.scalar_tensor_tensor(
                          MP, E, sb_dttm[:, c * 64 + h:c * 64 + h + 1], gtile,
                          ALU.mult, ALU.mult),
                      waits=[("s_ac", t_e), ("s_v", t_gv),
                             ("s_pe", gi_Y.get(gi - NSLOT, 0))], inc="s_v")
                gi_MP[gi] = C["s_v"]
                em.op("vector", lambda e, CPP=CPP, E2=E2, c=c: e.tensor_mul(
                    CPP, sb_cfm[:, c * CL:(c + 1) * CL], E2),
                    waits=[("s_ac", gi_pA[gi]),
                           ("s_pe", gi_Y.get(gi - NSLOT, 0))],
                    inc="s_v")
                gi_CPP[gi] = C["s_v"]
                wv = [("s_scd", t_hpeer), ("s_v", t_prep),
                      ("s_pe", gi_Y.get(gi - NSLOT, 0))]
                if c > 0:
                    wv.append(("s_scd", t_lh))
                if c == 0:
                    em.op("vector", lambda e, HBT=HBT, h=h: e.tensor_scalar(
                        HBT, hpeer[:, h * 128:(h + 1) * 128],
                        sb_gexp[:, h:h + 1], None, ALU.mult),
                        waits=wv, inc="s_v")
                else:
                    em.op("vector", lambda e, HBT=HBT, h=h, c=c:
                          e.scalar_tensor_tensor(
                              HBT, hpeer[:, h * 128:(h + 1) * 128],
                              sb_gexp[:, c * 64 + h:c * 64 + h + 1],
                              hloc[:, h * 128:(h + 1) * 128],
                              ALU.mult, ALU.add),
                          waits=wv, inc="s_v")
                gi_HBT[gi] = C["s_v"]

            def back(h):
                gi = c * NH + h
                slot = gi % NSLOT
                bY = 4 + h % 2
                MP = bfv(o_mp + slot * 128, (128, 128))
                CPP = bfv(o_cpp + slot * 128, (128, 128))
                HBT = bfv(o_hbt + slot * 128, (128, 128))
                wv = [("s_v", gi_MP[gi]), ("s_v", gi_HBT[gi]), ("s_scd", t_lx),
                      ("s_v", t_idsc), ("s_ac", gi_Yev.get(gi - 2, 0))]
                if gi < 2:
                    wv.append(("s_v", t_chain_final))
                em.op("tensor", lambda e, h=h, bY=bY, MP=MP, xtm=xtm: e.matmul(
                    pb[bY][:, 0:128], xtm[:, h * 128:(h + 1) * 128], MP,
                    start=True, stop=False), waits=wv, inc=None)
                em.op("tensor", lambda e, bY=bY, HBT=HBT, CPP=CPP: e.matmul(
                    pb[bY][:, 0:128], HBT, CPP, start=False, stop=False),
                    waits=[("s_v", gi_CPP[gi])], inc=None)
                em.op("tensor", lambda e, h=h, bY=bY, xtm=xtm: e.matmul(
                    pb[bY][:, 0:128], xtm[:, h * 128:(h + 1) * 128],
                    idsc[:, h * 128:(h + 1) * 128], start=False, stop=True),
                    inc="s_pe")
                gi_Y[gi] = C["s_pe"]
                wy = [("s_pe", gi_Y[gi])]
                if h == 0:
                    wy.append(("s_syd", t_ygn_spill))
                    wy.append(("s_pe", t_rmsmm))
                em.op("scalar", lambda e, h=h, bY=bY: e.activation(
                    ych[:, h * 128:(h + 1) * 128], pb[bY][:, 0:128], AF.Copy),
                    waits=wy, inc="s_ac")
                gi_Yev[gi] = C["s_ac"]

            for i in range(NH + KLAG):
                if i < NH:
                    front(i)
                if i >= KLAG:
                    back(i - KLAG)
            t_ally = C["s_ac"]
            t_hlld = C["s_v"]
            t_xtm2[slot2] = ("s_pe", C["s_pe"])

            em.op("vector", lambda e, zc=zc, ych=ych: e.tensor_mul(ych, ych, zc),
                  waits=[("s_ac", t_ally), ("s_syd", t_lz)], inc="s_v")
            t_yg = C["s_v"]
            em.op("scalar", lambda e, zc=zc, ych=ych: e.activation(
                zc, ych, AF.Square), waits=[("s_v", t_yg)], inc="s_ac")
            t_sq2 = C["s_ac"]
            for h in range(NH):
                em.op("tensor", lambda e, h=h, zc=zc, k0=(h == 0),
                      k1=(h == NH - 1): e.matmul(
                          pb7[0:1, 0:128], sb_onescolbf[:],
                          zc[:, h * 128:(h + 1) * 128], start=k0, stop=k1),
                      waits=([("s_ac", t_sq2), ("s_v", t_gv), ("s_v", t_rbv)]
                             if h == 0 else ()),
                      inc=("s_pe" if h == NH - 1 else None))
            t_rmsmm = C["s_pe"]
            em.op("vector", lambda e: e.tensor_copy(sb_row[0:1, 0:128],
                  pb7[0:1, 0:128]), waits=[("s_pe", t_rmsmm)], inc="s_v",
                  serial=True)
            em.op("scalar", lambda e: e.activation(sb_row[0:1, 0:128],
                  sb_row[0:1, 0:128], AF.Ln, scale=1.0 / DSSM,
                  bias=sb_cvals[0:1, 0:1]),
                  waits=[("s_v", C["s_v"])], inc="s_ac", serial=True)
            em.op("scalar", lambda e: e.activation(sb_row[0:1, 0:128],
                  sb_row[0:1, 0:128], AF.Exp, scale=-0.5), inc="s_ac",
                  serial=True)
            t_rr = C["s_ac"]
            em.op("tensor", lambda e: e.matmul(pb7[:, 128:256],
                  sb_onesrowf[0:1, 0:128], sb_row[0:1, 0:128],
                  start=True, stop=True),
                  waits=[("s_ac", t_rr), ("s_v", t_gv)], inc="s_pe")
            t_rbm = C["s_pe"]
            em.op("vector", lambda e: e.tensor_copy(rb2, pb7[:, 128:256]),
                  waits=[("s_pe", t_rbm)], inc="s_v")
            t_rbv = C["s_v"]
            for h in range(NH):
                em.op("vector", lambda e, h=h, zc=zc, ych=ych:
                      e.scalar_tensor_tensor(
                          zc[:, h * 128:(h + 1) * 128],
                          ych[:, h * 128:(h + 1) * 128],
                          sb_normw[:, h:h + 1], rb2, ALU.mult, ALU.mult),
                      waits=([("s_v", t_rbv)] if h == 0 else ()), inc="s_v")
            t_ygnv = C["s_v"]
            t_ygn_spill = em.dma(
                "sync",
                d_ygn.rearrange("(h p) t -> p h t", p=128)[:, :, c * CL:(c + 1) * CL],
                zc.rearrange("p (h t) -> p h t", h=64),
                waits=[("s_v", t_ygnv)])
        t_p2 = em.cnt["s_syd"]
        ar.pop()

        BF = em.bar()
        ar.push()
        o_ygk = ar.take(32 * TPC)
        o_wo = ar.take(2 * 32 * 128)
        o_fst = ar.take(2 * TPC)
        o_fpt = ar.take(2 * 2 * TPC)
        o_fsq = ar.take(2 * TPC)
        ygk = bfv(o_ygk, (128, 32 * TPC))

        WOO = [0]
        t_ldy = 0
        t_attn_sp = [0, 0]
        t_part_sp = [0, 0]
        t_part_ld = [0, 0]
        t_fevac = {i: ("s_v", 0) for i in (2, 3, 4, 5)}
        t_kh_done = 0
        t_fpe = [0, 0]
        t_wslf = [None, None]
        for kh in range(2):
            wy = list(BF)
            if kh == 1:
                wy.append(("s_pe", t_kh_done))
            t_ldy = em.dma("sync", ygk.rearrange("p (a t) -> p a t", a=32),
                           d_ygn.rearrange("(a p) t -> p a t", p=128)[:, kh * 32:(kh + 1) * 32, :],
                           waits=wy)
            for dm in range(NDM):
                slot2 = dm % 2
                wsl = bfv(o_wo + slot2 * 32 * 128, (128, 32 * 128))
                w = list(BF) if t_wslf[slot2] is None else [t_wslf[slot2]]
                t_w = em.dma("gpsimd", wsl, woutp[:, WOO[0]:WOO[0] + 32 * 128],
                             waits=w)
                WOO[0] += 32 * 128
                fpt = f32v(o_fpt + slot2 * 2 * TPC, (128, TPC))
                fst = bfv(o_fst + slot2 * TPC, (128, TPC))
                if kh == 1:
                    wp = [("s_syd", t_part_sp[0]), ("s_syd", t_part_sp[1])]
                    if t_part_ld[slot2]:
                        wp.append(("s_v", t_part_ld[slot2]))
                    t_lp = em.dma("scalar", fpt, d_part[dm * 128:(dm + 1) * 128, :],
                                  waits=wp)
                toks = []
                for half in range(2):
                    if kh == 0:
                        bank = pe_rot4[(2 * dm + half) % 4]
                    else:
                        bank = 2 + (2 * dm + half) % 2
                    wv = [("s_gpd", t_w), ("s_syd", t_ldy), t_fevac[bank]]
                    for kt in range(32):
                        em.op("tensor", lambda e, ps=pb[bank],
                              lw=wsl[:, kt * 128:(kt + 1) * 128],
                              rhs=ygk[:, kt * TPC + half * 512: kt * TPC + (half + 1) * 512],
                              k0=(kt == 0), k1=(kt == 31):
                              e.matmul(ps[:, :], lw, rhs, start=k0, stop=k1),
                              waits=(wv if kt == 0 else ()),
                              inc=("s_pe" if kt == 31 else None))
                    t_pe = C["s_pe"]
                    if kh == 0:
                        w2 = [("s_pe", t_pe)]
                        if half == 0 and t_part_sp[slot2]:
                            w2.append(("s_syd", t_part_sp[slot2]))
                        em.op("vector", lambda e, o=fpt[:, half * 512:(half + 1) * 512],
                              ps=pb[bank]: e.tensor_copy(o, ps[:, :]),
                              waits=w2, inc="s_v")
                    else:
                        w2 = [("s_pe", t_pe), ("s_scd", t_lp)]
                        if half == 0 and t_attn_sp[slot2]:
                            w2.append(("s_syd", t_attn_sp[slot2]))
                        em.op("vector", lambda e, o=fst[:, half * 512:(half + 1) * 512],
                              ps=pb[bank], pr=fpt[:, half * 512:(half + 1) * 512]:
                              e.tensor_add(o, ps[:, :], pr), waits=w2, inc="s_v")
                    t_fevac[bank] = ("s_v", C["s_v"])
                    toks.append(C["s_v"])
                if kh == 0:
                    t_part_sp[slot2] = em.dma(
                        "sync", d_part[dm * 128:(dm + 1) * 128, :], fpt,
                        waits=[("s_v", toks[-1])])
                else:
                    t_part_ld[slot2] = toks[-1]
                    sqt = bfv(o_fsq + slot2 * TPC, (128, TPC))
                    em.op("scalar", lambda e, a=sqt, b=fst: e.activation(
                        a, b, AF.Square),
                        waits=[("s_v", toks[-1]),
                               ("s_pe", t_fpe[slot2])], inc="s_ac")
                    t_aq = C["s_ac"]
                    ws = [("s_v", toks[-1])]
                    if dm == 0:
                        ws += [t_fevac[4], t_fevac[5]]
                    t_fpe[slot2] = emit_stats_tile(
                        fst, sqt, dm == 0, dm == NDM - 1,
                        extra_wait_sum=ws, extra_wait_sq=[("s_ac", t_aq)],
                        sums=(4, 5), sqs=(0, 1))
                    t_attn_sp[slot2] = em.dma(
                        "sync", d_attn[dm * 128:(dm + 1) * 128, :], fst,
                        waits=[("s_v", toks[-1]), ("s_pe", t_fpe[slot2])])
                t_wslf[slot2] = ("s_v", toks[-1])
            t_kh_done = C["s_pe"]
        t_fstat = C["s_pe"]
        t_f = em.cnt["s_syd"]
        ar.pop()

        BG = em.bar()
        ar.push()
        o_xf = ar.take(NKT * TPC)
        o_gst = ar.take(4 * TPC)
        o_gmb = ar.take(8 * TPC)
        xf = bfv(o_xf, (128, NKT * TPC))
        m2b = f32v(o_gmb, (128, TPC))
        r2b = f32v(o_gmb + 2 * TPC, (128, TPC))
        m3b = f32v(o_gmb + 4 * TPC, (128, TPC))
        r3b = f32v(o_gmb + 6 * TPC, (128, TPC))

        t_gpe = [0, 0]
        t_ln2 = finalize_stats(m2b, r2b, t_fstat, H, sums=(4, 5), sqs=(0, 1))

        t_s2sp = [0, 0]
        for dm in range(NDM):
            slot2 = dm % 2
            ast = bfv(o_gst + slot2 * TPC, (128, TPC))
            xst = bfv(o_gst + (2 + slot2) * TPC, (128, TPC))
            w = [("s_v", t_ln2)]
            if t_gpe[slot2]:
                w.append(("s_pe", t_gpe[slot2]))
            if t_s2sp[slot2]:
                w.append(("s_syd", t_s2sp[slot2]))
            t_la = em.dma("scalar", ast, d_attn[dm * 128:(dm + 1) * 128, :],
                          waits=w)
            t_lx2 = em.dma("gpsimd", xst, xT[dm * 128:(dm + 1) * 128, :],
                           waits=w)
            em.op("vector", lambda e, a=ast: e.tensor_sub(a, a, m2b),
                  waits=[("s_scd", t_la), ("s_v", t_ln2)], inc="s_v")
            em.op("vector", lambda e, a=ast: e.tensor_mul(a, a, r2b), inc="s_v")
            em.op("vector", lambda e, a=ast, b=xst: e.tensor_add(a, a, b),
                  waits=[("s_gpd", t_lx2)], inc="s_v")
            t_s2 = C["s_v"]
            em.op("scalar", lambda e, a=xst, b=ast: e.activation(a, b, AF.Square),
                  waits=[("s_v", t_s2)], inc="s_ac")
            t_aq = C["s_ac"]
            t_gpe[slot2] = emit_stats_tile(
                ast, xst, dm == 0, dm == NDM - 1,
                extra_wait_sum=[("s_v", t_s2)],
                extra_wait_sq=[("s_ac", t_aq)])
            t_s2sp[slot2] = em.dma("sync", d_s2[dm * 128:(dm + 1) * 128, :], ast,
                                   waits=[("s_pe", t_gpe[slot2]),
                                          ("s_ac", t_aq)])
        t_s2stat = C["s_pe"]
        t_s2done = em.cnt["s_syd"]
        t_ln3i = finalize_stats(m3b, r3b, t_s2stat, H)

        t_gl2 = [0, 0]
        for dm in range(NDM):
            slot2 = dm % 2
            xft = xf[:, dm * TPC:(dm + 1) * TPC]
            w = [("s_v", t_ln3i), ("s_syd", t_s2done)]
            ast = bfv(o_gst + slot2 * TPC, (128, TPC))
            if t_gl2[slot2]:
                w.append(("s_v", t_gl2[slot2]))
            t_ls2 = em.dma("scalar", ast, d_s2[dm * 128:(dm + 1) * 128, :],
                           waits=w)
            em.op("vector", lambda e, a=xft, b=ast: e.tensor_sub(a, b, m3b),
                  waits=[("s_scd", t_ls2)], inc="s_v")
            em.op("vector", lambda e, a=xft: e.tensor_mul(a, a, r3b), inc="s_v")
            t_gl2[slot2] = C["s_v"]
        t_xf = C["s_v"]
        if debug:
            em.dma("sync", d_dbgxf.rearrange("(a p) t -> p a t", p=128),
                   xf.rearrange("p (a t) -> p a t", a=NKT),
                   waits=[("s_v", t_xf)])

        o_hf = ar.take(TPC)
        o_wbc = ar.take(TPC)
        o_ltm = ar.take(2 * 64)
        o_lfm = ar.take(2 * TPC)
        o_wtm = ar.take(64)
        o_wfm = ar.take(TPC)
        o_apk = ar.take(32 * 128)
        o_bmp = ar.take(H)
        o_wrp = ar.take(NKT * 8)
        o_wbsl = ar.take(2 * 32 * 128)
        o_fst2 = ar.take(2 * TPC)
        o_ost = ar.take(4 * TPC)
        hfm = bfv(o_hf, (128, TPC))
        wbc = bfv(o_wbc, (128, TPC))
        ltm = f32v(o_ltm, (128, 64))
        lfm = f32v(o_lfm, (8, TPC))
        wtm = bfv(o_wtm, (128, 8))
        wfm = bfv(o_wfm, (8, TPC))
        apk = bfv(o_apk, (128, 32 * 128))
        bmp = bfv(o_bmp, (128, H))
        wrp = bfv(o_wrp, (128, NKT * 8))

        t_apk = em.dma("gpsimd", apk, apack, waits=BG)
        t_bmp = em.dma("gpsimd", bmp, bmpack, waits=BG)
        t_wrp = em.dma("gpsimd", wrp, wrtp, waits=BG)

        for half in range(2):
            bank = half
            for kt in range(32):
                em.op("tensor", lambda e, ps=pb[bank],
                      lw=apk[:, kt * 128:(kt + 1) * 128],
                      rhs=xf[:, kt * TPC + half * 512: kt * TPC + (half + 1) * 512],
                      k0=(kt == 0), k1=(kt == 31):
                      e.matmul(ps[:, :], lw, rhs, start=k0, stop=k1),
                      waits=([("s_gpd", t_apk), ("s_v", t_xf),
                              ("s_v", t_ln3i)] if kt == 0 else ()),
                      inc=("s_pe" if kt == 31 else None))
            t_h = C["s_pe"]
            em.op("vector", lambda e, half=half, bank=bank: e.tensor_copy(
                hfm[:, half * 512:(half + 1) * 512], pb[bank][:, :]),
                waits=[("s_pe", t_h)], inc="s_v")
        t_hfm = C["s_v"]
        for half in range(2):
            bank = half
            for kt in range(32):
                em.op("tensor", lambda e, ps=pb[bank],
                      lw=wrp[:, kt * 8:(kt + 1) * 8],
                      rhs=xf[:, kt * TPC + half * 512: kt * TPC + (half + 1) * 512],
                      k0=(kt == 0), k1=(kt == 31):
                      e.matmul(ps[0:8, :], lw, rhs, start=k0, stop=k1),
                      waits=([("s_gpd", t_wrp), ("s_v", t_hfm)] if kt == 0 else ()),
                      inc=("s_pe" if kt == 31 else None))
            t_l = C["s_pe"]
            em.op("vector", lambda e, half=half, bank=bank: e.tensor_copy(
                lfm[0:8, half * 512:(half + 1) * 512], pb[bank][0:8, :]),
                waits=[("s_pe", t_l)], inc="s_v")
        t_lfm = C["s_v"]

        t_pb7r = t_rbv
        for tt in range(8):
            em.op("tensor", lambda e, tt=tt: e.transpose(
                pb7[:, 0:8], lfm[0:8, tt * 128:(tt + 1) * 128],
                sb_i128[0:8, 0:8]),
                waits=[("s_v", t_lfm), ("s_v", t_pb7r)], inc="s_pe")
            t_tr = C["s_pe"]
            em.op("vector", lambda e: e.tensor_copy(ltm[:, 0:8], pb7[:, 0:8]),
                  waits=[("s_pe", t_tr)], inc="s_v", serial=True)
            em.op("vector", lambda e: e.reduce_max(ltm[:, 8:9], ltm[:, 0:8],
                  axis=mybir.AxisListType.X), inc="s_v", serial=True)
            em.op("vector", lambda e: e.tensor_scalar(ltm[:, 48:56],
                  ltm[:, 0:8], ltm[:, 8:9], None, ALU.subtract), inc="s_v",
                  serial=True)
            em.op("scalar", lambda e: e.activation(ltm[:, 16:24], ltm[:, 48:56],
                  AF.Exp), waits=[("s_v", C["s_v"])], inc="s_ac", serial=True)
            t_e3 = C["s_ac"]
            em.op("vector", lambda e: e.tensor_scalar(ltm[:, 24:32],
                  ltm[:, 16:24], 1.0, None, ALU.is_ge),
                  waits=[("s_ac", t_e3)], inc="s_v", serial=True)
            em.op("vector", lambda e: e.tensor_mul(ltm[:, 32:40], ltm[:, 16:24],
                  ltm[:, 24:32]), inc="s_v", serial=True)
            em.op("vector", lambda e: e.tensor_sub(ltm[:, 32:40], ltm[:, 16:24],
                  ltm[:, 32:40]), inc="s_v", serial=True)
            em.op("vector", lambda e: e.reduce_max(ltm[:, 9:10], ltm[:, 32:40],
                  axis=mybir.AxisListType.X), inc="s_v", serial=True)
            em.op("vector", lambda e: e.tensor_scalar(ltm[:, 24:32],
                  ltm[:, 16:24], ltm[:, 9:10], None, ALU.is_ge), inc="s_v",
                  serial=True)
            em.op("vector", lambda e: e.tensor_mul(ltm[:, 32:40], ltm[:, 16:24],
                  ltm[:, 24:32]), inc="s_v", serial=True)
            em.op("vector", lambda e: e.tensor_scalar(ltm[:, 9:10], ltm[:, 9:10],
                  1.0, None, ALU.add), inc="s_v", serial=True)
            em.op("vector", lambda e: e.reciprocal(ltm[:, 10:11], ltm[:, 9:10]),
                  inc="s_v", serial=True)
            em.op("vector", lambda e: e.tensor_scalar(wtm[:, 0:8], ltm[:, 32:40],
                  ltm[:, 10:11], None, ALU.mult), inc="s_v", serial=True)
            t_w2 = C["s_v"]
            em.op("tensor", lambda e: e.transpose(pbf[0:8, 0:128], wtm[:, 0:8],
                  sb_identbf[:]), waits=[("s_v", t_w2), ("s_v", t_tr_evac)],
                  inc="s_pe")
            t_tw = C["s_pe"]
            em.op("vector", lambda e, tt=tt: e.tensor_copy(
                wfm[0:8, tt * 128:(tt + 1) * 128], pbf[0:8, 0:128]),
                waits=[("s_pe", t_tw)], inc="s_v", serial=True)
            t_pb7r = C["s_v"]
            t_tr_evac = C["s_v"]
        t_wfm = C["s_v"]
        for half in range(2):
            bank = half
            em.op("tensor", lambda e, half=half, bank=bank: e.matmul(
                pb[bank][:, :], sb_rep2[0:8, :],
                wfm[0:8, half * 512:(half + 1) * 512], start=True, stop=True),
                waits=[("s_v", t_wfm)], inc="s_pe")
            t_wb = C["s_pe"]
            em.op("scalar", lambda e, half=half, bank=bank: e.activation(
                wbc[:, half * 512:(half + 1) * 512], pb[bank][:, :], AF.Copy),
                waits=[("s_pe", t_wb)], inc="s_ac")
        t_wbc = C["s_ac"]
        em.op("vector", lambda e: e.tensor_mul(hfm, hfm, wbc),
              waits=[("s_ac", t_wbc), ("s_v", t_hfm)], inc="s_v")
        t_hw = C["s_v"]
        if debug:
            em.dma("sync", d_dbgw[:, 0:TPC], wbc, waits=[("s_v", t_hw)])
            em.dma("sync", d_dbgw[:, TPC:2 * TPC], hfm, waits=[("s_v", t_hw)])


        t_wslh = [None, None]
        t_hev = {0: ("s_v", t_hw), 1: ("s_v", t_hw)}
        t_ffn_sp = [0, 0]
        t_hpe = [0, 0]
        for dm in range(NDM):
            slot2 = dm % 2
            wsl = bfv(o_wbsl + slot2 * 32 * 128, (128, 32 * 128))
            w = [("s_v", t_xf)] if t_wslh[slot2] is None else [t_wslh[slot2]]
            t_w = em.dma("gpsimd", wsl, wbp[:, dm * 32 * 128:(dm + 1) * 32 * 128],
                         waits=w)
            fst = bfv(o_fst2 + slot2 * TPC, (128, TPC))
            sqt = bfv(o_gst + slot2 * TPC, (128, TPC))
            toks = []
            for half in range(2):
                bank = half
                wv = [("s_gpd", t_w), ("s_v", t_hw), ("s_gpd", t_bmp),
                      t_hev[bank]]
                for kt in range(32):
                    em.op("tensor", lambda e, ps=pb[bank],
                          lw=wsl[:, kt * 128:(kt + 1) * 128],
                          rhs=xf[:, kt * TPC + half * 512: kt * TPC + (half + 1) * 512],
                          k0=(kt == 0):
                          e.matmul(ps[:, :], lw, rhs, start=k0, stop=False),
                          waits=(wv if kt == 0 else ()), inc=None)
                em.op("tensor", lambda e, ps=pb[bank], dm=dm: e.matmul(
                    ps[:, :], sb_biasbf[0:1, dm * 128:(dm + 1) * 128],
                    sb_onesrowbf[0:1, 0:512], start=False, stop=False), inc=None)
                em.op("tensor", lambda e, ps=pb[bank], dm=dm, half=half: e.matmul(
                    ps[:, :], bmp[:, dm * 128:(dm + 1) * 128],
                    hfm[:, half * 512:(half + 1) * 512], start=False, stop=True),
                    inc="s_pe")
                t_pe = C["s_pe"]
                w2 = [("s_pe", t_pe)]
                if half == 0 and t_ffn_sp[slot2]:
                    w2.append(("s_syd", t_ffn_sp[slot2]))
                em.op("vector", lambda e, o=fst[:, half * 512:(half + 1) * 512],
                      ps=pb[bank]: e.tensor_copy(o, ps[:, :]), waits=w2,
                      inc="s_v")
                t_hev[bank] = ("s_v", C["s_v"])
                toks.append(C["s_v"])
            t_ev3 = toks[-1]
            em.op("scalar", lambda e, a=sqt, b=fst: e.activation(a, b, AF.Square),
                  waits=[("s_v", t_ev3), ("s_pe", t_hpe[slot2])], inc="s_ac")
            t_aq = C["s_ac"]
            t_hpe[slot2] = emit_stats_tile(
                fst, sqt, dm == 0, dm == NDM - 1,
                extra_wait_sum=[("s_v", t_ev3)] + ([("s_ac", t_wbc)] if dm == 0 else []),
                extra_wait_sq=[("s_ac", t_aq)])
            t_wslh[slot2] = ("s_v", t_ev3)
            t_ffn_sp[slot2] = em.dma("sync", d_ffn[dm * 128:(dm + 1) * 128, :],
                                     fst, waits=[("s_pe", t_hpe[slot2]),
                                                 ("s_ac", t_aq)])
        t_ffnstat = C["s_pe"]
        t_ln3 = finalize_stats(m2b, r2b, t_ffnstat, H)

        t_old = [0, 0]
        for dm in range(NDM):
            slot2 = dm % 2
            fst = bfv(o_fst2 + slot2 * TPC, (128, TPC))
            s2t = bfv(o_gst + (2 + slot2) * TPC, (128, TPC))
            ost = f32v(o_ost + slot2 * 2 * TPC, (128, TPC))
            w = [("s_v", t_ln3), ("s_syd", t_ffn_sp[0]), ("s_syd", t_ffn_sp[1])]
            if t_old[slot2]:
                w.append(("s_syd", t_old[slot2]))
            t_lf = em.dma("scalar", fst, d_ffn[dm * 128:(dm + 1) * 128, :],
                          waits=w)
            t_ls3 = em.dma("sync", s2t, d_s2[dm * 128:(dm + 1) * 128, :],
                           waits=w)
            em.op("vector", lambda e, o=ost, b=fst: e.tensor_sub(o, b, m2b),
                  waits=[("s_scd", t_lf), ("s_v", t_ln3)], inc="s_v")
            em.op("vector", lambda e, o=ost: e.tensor_mul(o, o, r2b), inc="s_v")
            em.op("vector", lambda e, o=ost, b=s2t: e.tensor_add(o, o, b),
                  waits=[("s_syd", t_ls3)], inc="s_v")
            t_o = C["s_v"]
            t_old[slot2] = em.dma("sync", out_T[dm * 128:(dm + 1) * 128, :], ost,
                                  waits=[("s_v", t_o)])
        t_final = em.cnt["s_syd"]

        if debug:
            for nm, (dst, src) in dbg.items():
                em.dma("sync", dst, src, waits=[("s_syd", t_final)])
            t_final = em.cnt["s_syd"]

        em.op("sync", lambda e: e.wait_ge(sems["s_syd"], t_final))

        em.check_deadlock()

        with nc.Block() as block:
            @block.tensor
            def _(te):
                em.replay_one("tensor", te, sems)

            @block.vector
            def _(v):
                em.replay_one("vector", v, sems)

            @block.scalar
            def _(s):
                em.replay_one("scalar", s, sems)

            @block.gpsimd
            def _(gp):
                em.replay_one("gpsimd", gp, sems)

            @block.sync
            def _(sy):
                em.replay_one("sync", sy, sems)

    return nc


_CACHE = {}


def _pack_weights(W_in, conv_w, dt_bias, A_log, D_param, norm_weight, W_out,
                  Wb, bias, Wr, A, Bm):
    f = np.float32
    W_in = np.asarray(W_in, f)

    def pack(Wsub, width=128):
        nt = Wsub.shape[1] // width
        out = np.empty((128, nt * NKT * width), f)
        for t in range(nt):
            blk = Wsub[:, t * width:(t + 1) * width].reshape(NKT, 128, width)
            out[:, t * NKT * width:(t + 1) * NKT * width] = \
                blk.transpose(1, 0, 2).reshape(128, NKT * width)
        return out

    winp = np.concatenate([
        pack(W_in[:, DSSM:DSSM + CONV]),
        pack(W_in[:, DSSM + CONV:], 64),
        pack(W_in[:, :DSSM]),
    ], axis=1)

    W_out = np.asarray(W_out, f)
    parts = []
    for kh in range(2):
        sub = W_out[kh * 4096:(kh + 1) * 4096, :]
        for dm in range(NDM):
            blk = sub[:, dm * 128:(dm + 1) * 128].reshape(32, 128, 128)
            parts.append(blk.transpose(1, 0, 2).reshape(128, 32 * 128))
    woutp = np.concatenate(parts, axis=1)

    WbT = np.ascontiguousarray(np.asarray(Wb, f).T)
    parts = []
    for dm in range(NDM):
        blk = WbT[:, dm * 128:(dm + 1) * 128].reshape(NKT, 128, 128)
        parts.append(blk.transpose(1, 0, 2).reshape(128, NKT * 128))
    wbp = np.concatenate(parts, axis=1)

    A2T = np.ascontiguousarray(np.asarray(A, f).reshape(128, H).T)
    apack = A2T.reshape(NKT, 128, 128).transpose(1, 0, 2).reshape(128, NKT * 128)
    bmpack = np.ascontiguousarray(
        np.asarray(Bm, f).transpose(0, 2, 1).reshape(128, H))
    WrT = np.ascontiguousarray(np.asarray(Wr, f).T)
    wrtp = WrT.reshape(NKT, 128, 8).transpose(1, 0, 2).reshape(128, NKT * 8)

    cw = np.asarray(conv_w, f)
    convdiag = np.zeros((128, NXT * 4 * 128), f)
    idx = np.arange(128)
    for i in range(NXT):
        for k in range(4):
            convdiag[idx, (i * 4 + k) * 128 + idx] = cw[i * 128:(i + 1) * 128, k]

    rep2m = np.zeros((8, 128), f)
    for e in range(8):
        rep2m[e, e * 16:(e + 1) * 16] = 2.0

    return dict(
        winp=np.ascontiguousarray(winp),
        woutp=np.ascontiguousarray(woutp),
        wbp=np.ascontiguousarray(wbp),
        apack=np.ascontiguousarray(apack),
        bmpack=bmpack,
        wrtp=np.ascontiguousarray(wrtp),
        convdiag=convdiag,
        i128=np.eye(128, dtype=f),
        i64=np.eye(64, dtype=f),
        trilbig=np.tril(np.full((128, 128), BIGV, f), -1),
        triu01=np.triu(np.ones((128, 128), f)),
        onescol=np.ones((128, 1), f),
        onesrow=np.ones((1, 512), f),
        dtbias=np.asarray(dt_bias, f).reshape(64, 1),
        alog=np.asarray(A_log, f).reshape(64, 1),
        drow=np.asarray(D_param, f).reshape(1, 64),
        normw=np.ascontiguousarray(
            np.asarray(norm_weight, f).reshape(64, 128).T),
        biasrow=np.asarray(bias, f).reshape(1, H),
        rep2=rep2m,
        cvals=np.ascontiguousarray(
            np.stack([np.full(128, EPS, f), np.ones(128, f)], axis=1)),
    )


def kernel(L0_attn_hidden_states, L0_attn_W_in, L0_attn_conv_w, L0_attn_dt_bias,
           L0_attn_A_log, L0_attn_D_param, L0_attn_norm_weight, L0_attn_W_out,
           L0_ffn_Wb, L0_ffn_bias, L0_ffn_Wr, L0_ffn_A, L0_ffn_Bm):
    x = np.asarray(L0_attn_hidden_states, np.float32)
    com = _pack_weights(L0_attn_W_in, L0_attn_conv_w, L0_attn_dt_bias,
                        L0_attn_A_log, L0_attn_D_param, L0_attn_norm_weight,
                        L0_attn_W_out, L0_ffn_Wb, L0_ffn_bias, L0_ffn_Wr,
                        L0_ffn_A, L0_ffn_Bm)
    if "nc" not in _CACHE:
        _CACHE["nc"] = build_program(debug=False)
    nc = _CACHE["nc"]

    in_maps = []
    for c in range(8):
        b, half = c // 2, c % 2
        m = dict(com)
        m["xT"] = np.ascontiguousarray(x[b, half * TPC:(half + 1) * TPC, :].T)
        if half == 0:
            m["xhT"] = np.zeros((H, 4), np.float32)
        else:
            m["xhT"] = np.ascontiguousarray(x[b, TPC - 4:TPC, :].T)
        m["gcol"] = np.full((128, 1), float(half), np.float32)
        in_maps.append(m)

    res = run_bass_kernel_spmd(nc, in_maps, core_ids=list(range(8)))
    out = np.empty((B, S, H), np.float32)
    for c in range(8):
        b, half = c // 2, c % 2
        out[b, half * TPC:(half + 1) * TPC, :] = res.results[c]["out_T"].T
    return out


# version 11
# speedup vs baseline: 1.2040x; 1.0677x over previous
import numpy as np
from contextlib import ExitStack

import concourse.bass as bass
import concourse.mybir as mybir
from concourse.bass_utils import run_bass_kernel_spmd

F32 = mybir.dt.float32
BF16 = mybir.dt.bfloat16
AF = mybir.ActivationFunctionType
ALU = mybir.AluOpType

B, S, H = 4, 2048, 4096
TPC = 1024
NH, P, N, K = 64, 128, 128, 4
DSSM = NH * P
CONV = DSSM + 2 * N
PROJ = DSSM + CONV + NH
NKT = H // 128
NXT = CONV // 128
NZT = DSSM // 128
NDM = H // 128
NC = 8
CL = 128
EPS = 1e-5
BIGV = 1e30

ENGINES = ("tensor", "vector", "scalar", "gpsimd", "sync")
SEMS = ("s_pe", "s_v", "s_ac", "s_gp", "s_gpd", "s_syd", "s_scd", "s_cc")


class Emitter:
    def __init__(self):
        self.q = {e: [] for e in ENGINES}
        self.cnt = {s: 0 for s in SEMS}

    ENG_SEM = {"tensor": "s_pe", "vector": "s_v", "scalar": "s_ac",
               "gpsimd": "s_gp"}

    def op(self, eng, fn, waits=(), inc=None, amt=1, serial=False):
        waits = list(waits)
        if serial:
            own = self.ENG_SEM[eng]
            waits.append((own, self.cnt[own]))
        waits = tuple((s, v) for (s, v) in waits if v)
        self.q[eng].append((fn, waits, inc, amt))
        if inc is not None:
            self.cnt[inc] += amt
            return self.cnt[inc]
        return None

    def dma(self, queue, out, in_, waits=()):
        sem = {"sync": "s_syd", "scalar": "s_scd", "gpsimd": "s_gpd"}[queue]
        return self.op(queue, lambda e: e.dma_start(out=out, in_=in_),
                       waits=waits, inc=sem, amt=16)

    def bar(self, skip=("s_cc",)):
        return [(s, v) for s, v in self.cnt.items() if s not in skip and v]

    def check_deadlock(self):
        ptr = {e: 0 for e in ENGINES}
        cnt = {s: 0 for s in SEMS}
        progress = True
        while progress:
            progress = False
            for e in ENGINES:
                while ptr[e] < len(self.q[e]):
                    fn, waits, inc, amt = self.q[e][ptr[e]]
                    if all(cnt[s] >= v for (s, v) in waits):
                        if inc is not None:
                            cnt[inc] += amt
                        ptr[e] += 1
                        progress = True
                    else:
                        break
        stuck = {e: ptr[e] for e in ENGINES if ptr[e] < len(self.q[e])}
        if stuck:
            lines = []
            for e, i in stuck.items():
                fn, waits, inc, amt = self.q[e][i]
                missing = [(s, v, cnt[s]) for (s, v) in waits if cnt[s] < v]
                lines.append(f"{e}@{i}/{len(self.q[e])} missing={missing}")
            raise RuntimeError("DEADLOCK in schedule:\n" + "\n".join(lines))

    def replay_one(self, ename, eng, sems):
        last_wait = {}
        for fn, waits, inc, amt in self.q[ename]:
            for (s, v) in waits:
                if last_wait.get(s, -1) < v:
                    eng.wait_ge(sems[s], v)
                    last_wait[s] = v
            inst = fn(eng)
            if inc is not None and inst is not None:
                if inc == "s_cc":
                    inst.then_inc(sems[inc])
                else:
                    inst.then_inc(sems[inc], amt)

    def n_ops(self):
        return {e: len(self.q[e]) for e in ENGINES}


class Arena:
    def __init__(self, nelem):
        self.n = nelem
        self.off = 0
        self.marks = []
        self.peak = 0

    def take(self, nelem, align=64):
        o = (self.off + align - 1) // align * align
        self.off = o + nelem
        self.peak = max(self.peak, self.off)
        assert self.off <= self.n, f"SBUF arena overflow: {self.off} > {self.n}"
        return o

    def push(self):
        self.marks.append(self.off)

    def pop(self):
        self.off = self.marks.pop()


def build_program(debug=False):
    nc = bass.Bass("TRN2", target_bir_lowering=False, debug=False)
    em = Emitter()
    C = em.cnt

    def din(name, shape):
        return nc.dram_tensor(name, list(shape), F32, kind="ExternalInput").ap()

    xT = din("xT", (H, TPC))
    xhT = din("xhT", (H, 4))
    winp = din("winp", (128, NKT * PROJ))
    woutp = din("woutp", (128, DSSM * H // 128))
    wbp = din("wbp", (128, NKT * H))
    apack = din("apack", (128, NKT * 128))
    bmpack = din("bmpack", (128, H))
    wrtp = din("wrtp", (128, NKT * 8))
    convdiag = din("convdiag", (128, NXT * 4 * 128))
    i128 = din("i128", (128, 128))
    i64 = din("i64", (64, 64))
    trilbig = din("trilbig", (128, 128))
    triu01 = din("triu01", (128, 128))
    onescol = din("onescol", (128, 1))
    onesrow = din("onesrow", (1, 512))
    dtbias = din("dtbias", (64, 1))
    alog = din("alog", (64, 1))
    drow = din("drow", (1, 64))
    normw = din("normw", (128, 64))
    biasrow = din("biasrow", (1, H))
    gcol = din("gcol", (128, 1))
    rep2 = din("rep2", (8, 128))
    cvals = din("cvals", (128, 2))

    out_T = nc.dram_tensor("out_T", [H, TPC], F32, kind="ExternalOutput").ap()

    d_xbc = nc.dram_tensor("d_xbc", [CONV, TPC], BF16, kind="Internal").ap()
    d_xtm = nc.dram_tensor("d_xtm", [NC * 64 * 128, 128], BF16, kind="Internal").ap()
    d_z = nc.dram_tensor("d_z", [DSSM, TPC], BF16, kind="Internal").ap()
    d_hloc = nc.dram_tensor("d_hloc", [NC * 128, DSSM], BF16, kind="Internal").ap()
    d_ygn = nc.dram_tensor("d_ygn", [DSSM, TPC], BF16, kind="Internal").ap()
    d_attn = nc.dram_tensor("d_attn", [H, TPC], BF16, kind="Internal").ap()
    d_ffn = nc.dram_tensor("d_ffn", [H, TPC], BF16, kind="Internal").ap()
    d_s2 = nc.dram_tensor("d_s2", [H, TPC], BF16, kind="Internal").ap()
    d_part = nc.dram_tensor("d_part", [H, TPC], F32, kind="Internal").ap()
    cc_hin = nc.dram_tensor("cc_hin", [128, DSSM], BF16, kind="Internal").ap()
    cc_hout = nc.dram_tensor("cc_hout", [256, DSSM], BF16, kind="Internal").ap()

    d_dbgxf = nc.dram_tensor("d_dbgxf", [H, TPC], BF16, kind="Internal").ap()
    d_dbgw = nc.dram_tensor("d_dbgw", [128, 3 * TPC], BF16, kind="Internal").ap()
    dbg = {}
    if debug:
        for nm, src in [("xbc", d_xbc), ("z", d_z), ("xtm", d_xtm),
                        ("hloc", d_hloc), ("ygn", d_ygn), ("attn", d_attn),
                        ("ffn", d_ffn), ("hout", cc_hout), ("s2", d_s2),
                        ("xf", d_dbgxf), ("w", d_dbgw)]:
            dbg[nm] = (nc.dram_tensor("dbg_" + nm, list(src.shape), BF16,
                                      kind="ExternalOutput").ap(), src)

    with ExitStack() as ctx:
        BIGN = 74000
        big = ctx.enter_context(nc.sbuf_tensor("big", [128, BIGN], BF16))
        ar = Arena(BIGN)

        def bfv(off, shape):
            v = big[:, off:off + shape[1]]
            return v if shape[0] == 128 else v[0:shape[0], :]

        def f32v(off, shape):
            v = big[:, off:off + 2 * shape[1]].bitcast(F32)
            return v if shape[0] == 128 else v[0:shape[0], :]

        def sb(name, shape, dt):
            return ctx.enter_context(nc.sbuf_tensor(name, list(shape), dt))

        sb_i128 = sb("sb_i128", (128, 128), F32)
        sb_i64 = sb("sb_i64", (64, 64), F32)
        sb_tril = sb("sb_tril", (128, 128), F32)
        sb_triu = sb("sb_triu", (128, 128), BF16)
        sb_identbf = sb("sb_identbf", (128, 128), BF16)
        sb_onescolbf = sb("sb_onescolbf", (128, 1), BF16)
        sb_onesrowf = sb("sb_onesrowf", (1, 512), F32)
        sb_onesrowbf = sb("sb_onesrowbf", (1, 512), BF16)
        sb_dtb = sb("sb_dtb", (64, 1), F32)
        sb_alog = sb("sb_alog", (64, 1), F32)
        sb_drow = sb("sb_drow", (1, 64), F32)
        sb_normw = sb("sb_normw", (128, 64), F32)
        sb_biasbf = sb("sb_biasbf", (1, H), BF16)
        sb_g = sb("sb_g", (128, 1), F32)
        sb_rep2 = sb("sb_rep2", (8, 128), BF16)
        sb_cvals = sb("sb_cvals", (128, 2), F32)
        sb_dt = sb("sb_dt", (64, TPC), F32)
        sb_na = sb("sb_na", (64, TPC), F32)
        sb_sc1 = sb("sb_sc1", (64, TPC), F32)
        sb_sc2 = sb("sb_sc2", (64, TPC), F32)
        sb_natm = sb("sb_natm", (128, NC * 64), F32)
        sb_natmn = sb("sb_natmn", (128, NC * 64), F32)
        sb_dttm = sb("sb_dttm", (128, NC * 64), F32)
        sb_nalb = sb("sb_nalb", (128, NC * 64), F32)
        sb_gexp = sb("sb_gexp", (128, NC * 64), F32)
        sb_lstep = sb("sb_lstep", (128, NC * 64), F32)
        sb_wdt = sb("sb_wdt", (128, NC * 64), F32)
        sb_bfm = sb("sb_bfm", (128, TPC), BF16)
        sb_cfm = sb("sb_cfm", (128, TPC), BF16)
        sb_btm = sb("sb_btm", (128, TPC), BF16)
        sb_expa = sb("sb_expa", (64, 1), F32)
        sb_row = sb("sb_row", (1, 128), F32)
        sb_stat = sb("sb_stat", (1, 3 * TPC), F32)

        pb = [ctx.enter_context(nc.psum_tensor(f"pb{i}", [128, 512], F32))
              for i in range(6)]
        pbf = ctx.enter_context(nc.psum_tensor("pbf", [128, 1024], BF16))
        pb7 = ctx.enter_context(nc.psum_tensor("pb7", [128, 512], F32))

        sems = {s: ctx.enter_context(nc.semaphore(s)) for s in SEMS}

        em.dma("sync", sb_i128[:], i128)
        em.dma("sync", sb_i64[:], i64)
        em.dma("sync", sb_tril[:], trilbig)
        em.dma("sync", sb_onesrowf[:], onesrow)
        em.dma("sync", sb_dtb[:], dtbias)
        em.dma("sync", sb_alog[:], alog)
        em.dma("sync", sb_drow[:], drow)
        em.dma("sync", sb_normw[:], normw)
        em.dma("sync", sb_g[:], gcol)
        em.dma("sync", sb_cvals[:], cvals)
        em.dma("gpsimd", sb_identbf[:], i128)
        em.dma("gpsimd", sb_triu[:], triu01)
        em.dma("gpsimd", sb_onescolbf[:], onescol)
        em.dma("gpsimd", sb_onesrowbf[:], onesrow)
        em.dma("gpsimd", sb_biasbf[:], biasrow)
        em.dma("gpsimd", sb_rep2[:], rep2)
        WCONST = [("s_syd", em.cnt["s_syd"]), ("s_gpd", em.cnt["s_gpd"])]

        srow = sb_stat[0:1, 0:TPC]
        qrow = sb_stat[0:1, TPC:2 * TPC]
        trow = sb_stat[0:1, 2 * TPC:3 * TPC]

        def emit_stats_tile(tile, sqtile, first, last, extra_wait_sum=(),
                            extra_wait_sq=(), sums=(2, 3), sqs=(4, 5)):
            toks = []
            for g2 in range(2):
                em.op("tensor", lambda e, g2=g2, t=tile, k0=first, k1=last:
                      e.matmul(pb[sums[g2]][0:1, :], sb_onescolbf[:],
                               t[:, g2 * 512:(g2 + 1) * 512], start=k0, stop=k1),
                      waits=extra_wait_sum, inc=None)
                tok = em.op("tensor", lambda e, g2=g2, t=sqtile, k0=first, k1=last:
                            e.matmul(pb[sqs[g2]][0:1, :], sb_onescolbf[:],
                                     t[:, g2 * 512:(g2 + 1) * 512], start=k0,
                                     stop=k1),
                            waits=extra_wait_sq,
                            inc=("s_pe" if (g2 == 1) else None))
                if tok:
                    toks.append(tok)
            return toks[-1]

        def finalize_stats(mbv, rbv, wtok_pe, denom, sums=(2, 3), sqs=(4, 5)):
            for g2 in range(2):
                em.op("vector", lambda e, g2=g2: e.tensor_copy(
                    srow[:, g2 * 512:(g2 + 1) * 512], pb[sums[g2]][0:1, :]),
                    waits=[("s_pe", wtok_pe)], inc="s_v")
                em.op("vector", lambda e, g2=g2: e.tensor_copy(
                    qrow[:, g2 * 512:(g2 + 1) * 512], pb[sqs[g2]][0:1, :]),
                    waits=[("s_pe", wtok_pe)], inc="s_v")
            t_ev = C["s_v"]
            em.op("scalar", lambda e: e.mul(srow, srow, 1.0 / denom),
                  waits=[("s_v", t_ev)], inc="s_ac", serial=True)
            em.op("scalar", lambda e: e.activation(trow, srow, AF.Square),
                  inc="s_ac", serial=True)
            t_m = C["s_ac"]
            em.op("vector", lambda e: e.scalar_tensor_tensor(
                qrow, qrow, 1.0 / denom, trow, ALU.mult, ALU.subtract),
                waits=[("s_ac", t_m)], inc="s_v", serial=True)
            em.op("scalar", lambda e: e.activation(qrow, qrow, AF.Ln,
                  bias=sb_cvals[0:1, 0:1]),
                  waits=[("s_v", C["s_v"])], inc="s_ac", serial=True)
            em.op("scalar", lambda e: e.activation(qrow, qrow, AF.Exp, scale=-0.5),
                  inc="s_ac", serial=True)
            t_r = C["s_ac"]
            t_last = 0
            for g2 in range(2):
                em.op("tensor", lambda e, g2=g2: e.matmul(
                    pb[0][:, :], sb_onesrowf[0:1, 0:128],
                    srow[:, g2 * 512:(g2 + 1) * 512], start=True, stop=True),
                    waits=[("s_ac", t_r), ("s_v", t_last)], inc="s_pe")
                em.op("tensor", lambda e, g2=g2: e.matmul(
                    pb[1][:, :], sb_onesrowf[0:1, 0:128],
                    qrow[:, g2 * 512:(g2 + 1) * 512], start=True, stop=True),
                    inc="s_pe")
                tb = C["s_pe"]
                em.op("vector", lambda e, g2=g2, mbv=mbv: e.tensor_copy(
                    mbv[:, g2 * 512:(g2 + 1) * 512], pb[0][:, :]),
                    waits=[("s_pe", tb)], inc="s_v")
                em.op("vector", lambda e, g2=g2, rbv=rbv: e.tensor_copy(
                    rbv[:, g2 * 512:(g2 + 1) * 512], pb[1][:, :]),
                    waits=[("s_pe", tb)], inc="s_v")
                t_last = C["s_v"]
            return t_last

        ar.push()
        o_xn = ar.take(NKT * TPC)
        o_sq = ar.take(2 * TPC)
        o_mb = ar.take(2 * TPC)
        o_rb = ar.take(2 * TPC)
        o_xh = ar.take(NKT * 4)
        o_hb = ar.take(64)
        xn = bfv(o_xn, (128, NKT * TPC))
        mb = f32v(o_mb, (128, TPC))
        rb = f32v(o_rb, (128, TPC))
        xh = bfv(o_xh, (128, NKT * 4))
        mhb = f32v(o_hb, (128, 8))[:, 0:4]
        rhb = f32v(o_hb + 16, (128, 8))[:, 0:4]

        t_x = em.dma("gpsimd", xn.rearrange("p (a t) -> p a t", a=NKT),
                     xT.rearrange("(a p) t -> p a t", p=128), waits=WCONST)
        t_xh = em.dma("gpsimd", xh.rearrange("p (a t) -> p a t", a=NKT),
                      xhT.rearrange("(a p) t -> p a t", p=128), waits=WCONST)

        t_sqpe = [0, 0]
        for kt in range(NKT):
            xt = xn[:, kt * TPC:(kt + 1) * TPC]
            slot = kt % 2
            sq = bfv(o_sq + slot * TPC, (128, TPC))
            em.op("scalar", lambda e, a=sq, b=xt: e.activation(a, b, AF.Square),
                  waits=[("s_gpd", t_x), ("s_pe", t_sqpe[slot])], inc="s_ac")
            t_aq = C["s_ac"]
            t_sqpe[slot] = emit_stats_tile(
                xt, sq, kt == 0, kt == NKT - 1,
                extra_wait_sum=[("s_gpd", t_x)],
                extra_wait_sq=[("s_ac", t_aq)])
        t_stats_mm = C["s_pe"]
        t_ln1 = finalize_stats(mb, rb, t_stats_mm, H)

        for kt in range(NKT):
            xt = xn[:, kt * TPC:(kt + 1) * TPC]
            em.op("vector", lambda e, a=xt: e.tensor_sub(a, a, mb),
                  waits=[("s_v", t_ln1), ("s_pe", t_stats_mm)], inc="s_v")
            em.op("vector", lambda e, a=xt: e.tensor_mul(a, a, rb), inc="s_v")
        t_xn = C["s_v"]

        hsq = bfv(o_sq, (128, NKT * 4))
        em.op("scalar", lambda e: e.activation(hsq, xh, AF.Square),
              waits=[("s_gpd", t_xh), ("s_pe", t_stats_mm)], inc="s_ac")
        t_hq = C["s_ac"]
        for kt in range(NKT):
            em.op("tensor", lambda e, kt=kt, k0=(kt == 0), k1=(kt == NKT - 1):
                  e.matmul(pb7[0:1, 0:4], sb_onescolbf[:],
                           xh[:, kt * 4:(kt + 1) * 4], start=k0, stop=k1),
                  waits=([("s_gpd", t_xh)] if kt == 0 else ()), inc=None)
            em.op("tensor", lambda e, kt=kt, k0=(kt == 0), k1=(kt == NKT - 1):
                  e.matmul(pb7[0:1, 4:8], sb_onescolbf[:],
                           hsq[:, kt * 4:(kt + 1) * 4], start=k0, stop=k1),
                  waits=([("s_ac", t_hq)] if kt == 0 else ()),
                  inc=("s_pe" if kt == NKT - 1 else None))
        t_hstat = C["s_pe"]
        hrow = sb_stat[0:1, 2 * TPC + 64:2 * TPC + 68]
        hrow2 = sb_stat[0:1, 2 * TPC + 68:2 * TPC + 72]
        hrow3 = sb_stat[0:1, 2 * TPC + 72:2 * TPC + 76]
        em.op("vector", lambda e: e.tensor_copy(hrow, pb7[0:1, 0:4]),
              waits=[("s_pe", t_hstat)], inc="s_v")
        em.op("vector", lambda e: e.tensor_copy(hrow2, pb7[0:1, 4:8]),
              waits=[("s_pe", t_hstat)], inc="s_v")
        em.op("scalar", lambda e: e.mul(hrow, hrow, 1.0 / H),
              waits=[("s_v", C["s_v"])], inc="s_ac", serial=True)
        em.op("scalar", lambda e: e.activation(hrow3, hrow, AF.Square),
              inc="s_ac", serial=True)
        em.op("vector", lambda e: e.scalar_tensor_tensor(
              hrow2, hrow2, 1.0 / H, hrow3, ALU.mult, ALU.subtract),
              waits=[("s_ac", C["s_ac"])], inc="s_v", serial=True)
        em.op("scalar", lambda e: e.activation(hrow2, hrow2, AF.Ln,
              bias=sb_cvals[0:1, 0:1]), waits=[("s_v", C["s_v"])], inc="s_ac",
              serial=True)
        em.op("scalar", lambda e: e.activation(hrow2, hrow2, AF.Exp, scale=-0.5),
              inc="s_ac", serial=True)
        t_hr = C["s_ac"]
        em.op("tensor", lambda e: e.matmul(pb7[:, 0:4], sb_onesrowf[0:1, 0:128],
              hrow, start=True, stop=True),
              waits=[("s_ac", t_hr), ("s_v", C["s_v"])], inc="s_pe")
        em.op("tensor", lambda e: e.matmul(pb7[:, 4:8], sb_onesrowf[0:1, 0:128],
              hrow2, start=True, stop=True), inc="s_pe")
        t_hbc = C["s_pe"]
        em.op("vector", lambda e: e.tensor_copy(mhb, pb7[:, 0:4]),
              waits=[("s_pe", t_hbc)], inc="s_v")
        em.op("vector", lambda e: e.tensor_copy(rhb, pb7[:, 4:8]),
              waits=[("s_pe", t_hbc)], inc="s_v")
        t_hln = C["s_v"]
        for kt in range(NKT):
            xht = xh[:, kt * 4:(kt + 1) * 4]
            em.op("vector", lambda e, a=xht: e.tensor_sub(a, a, mhb),
                  waits=[("s_v", t_hln), ("s_pe", t_hstat)], inc="s_v",
                  serial=True)
            em.op("vector", lambda e, a=xht: e.tensor_mul(a, a, rhb), inc="s_v",
                  serial=True)
        t_xnh = C["s_v"]
        t_pb7A = ("s_v", t_hln)

        o_wsl = ar.take(2 * NKT * 128)
        o_bst = ar.take(2 * TPC)
        o_halo = ar.take(NXT * 4)
        halo_all = bfv(o_halo, (128, NXT * 4))

        WOFF = [0]
        t_wsl = [None, None]
        t_evac = {i: ("s_v", t_ln1) for i in range(6)}
        pe_rot4 = (2, 3, 4, 5)

        t_hpb7 = [t_pb7A]

        def stream_tile(ti, width, on_evac, rot=pe_rot4, halo=False):
            slot = ti % 2
            wsl = bfv(o_wsl + slot * NKT * 128, (128, NKT * width))
            w = list(WCONST)
            if t_wsl[slot] is not None:
                w.append(t_wsl[slot])
            t_w = em.dma("gpsimd", wsl, winp[:, WOFF[0]:WOFF[0] + NKT * width],
                         waits=w)
            WOFF[0] += NKT * width
            if halo:
                for kt in range(NKT):
                    em.op("tensor", lambda e, kt=kt,
                          lw=wsl[:, kt * width:(kt + 1) * width],
                          k0=(kt == 0), k1=(kt == NKT - 1):
                          e.matmul(pb7[:, 256:260], lw,
                                   xh[:, kt * 4:(kt + 1) * 4],
                                   start=k0, stop=k1),
                          waits=([("s_gpd", t_w), ("s_v", t_xnh),
                                  t_hpb7[0]] if kt == 0 else ()),
                          inc=("s_pe" if kt == NKT - 1 else None))
                t_hmm = C["s_pe"]
                em.op("vector", lambda e, ti=ti: e.tensor_copy(
                    halo_all[:, ti * 4:(ti + 1) * 4], pb7[:, 256:260]),
                    waits=[("s_pe", t_hmm)], inc="s_v")
                t_hpb7[0] = ("s_v", C["s_v"])
            last_tok = None
            for half in range(2):
                bank = rot[(2 * ti + half) % len(rot)]
                wv = [("s_gpd", t_w), ("s_v", t_xn), t_evac[bank]]
                for kt in range(NKT):
                    em.op("tensor", lambda e, ps=pb[bank], wd=width,
                          lw=wsl[:, kt * width:(kt + 1) * width],
                          rhs=xn[:, kt * TPC + half * 512: kt * TPC + (half + 1) * 512],
                          k0=(kt == 0), k1=(kt == NKT - 1):
                          e.matmul(ps[0:wd, :] if wd < 128 else ps[:, :],
                                   lw, rhs, start=k0, stop=k1),
                          waits=(wv if kt == 0 else ()),
                          inc=("s_pe" if kt == NKT - 1 else None))
                t_pe = C["s_pe"]
                tok = on_evac(half, bank, t_pe)
                t_evac[bank] = tok
                last_tok = tok
            t_wsl[slot] = last_tok
            return last_tok

        t_bsp = [0, 0]
        for ti in range(NXT):
            slot2 = ti % 2
            bst = bfv(o_bst + slot2 * TPC, (128, TPC))

            def ev(half, bank, t_pe, bst=bst, slot2=slot2):
                w = [("s_pe", t_pe)]
                if half == 0 and t_bsp[slot2]:
                    w.append(("s_syd", t_bsp[slot2]))
                em.op("vector", lambda e, half=half, bank=bank, bst=bst:
                      e.tensor_copy(bst[:, half * 512:(half + 1) * 512],
                                    pb[bank][:, :]), waits=w, inc="s_v")
                return ("s_v", C["s_v"])

            stream_tile(ti, 128, ev, halo=True)
            t_bsp[slot2] = em.dma("sync", d_xbc[ti * 128:(ti + 1) * 128, :],
                                  bst[:], waits=[("s_v", C["s_v"])])
        t_b1_spill = em.cnt["s_syd"]
        t_halo = t_hpb7[0][1]

        def ev_dt(half, bank, t_pe):
            em.op("vector", lambda e, half=half, bank=bank: e.tensor_copy(
                sb_sc1[0:64, half * 512:(half + 1) * 512], pb[bank][0:64, :]),
                waits=[("s_pe", t_pe)], inc="s_v")
            return ("s_v", C["s_v"])
        stream_tile(NXT, 64, ev_dt)
        t_dtraw = C["s_v"]


        em.op("vector", lambda e: e.tensor_scalar(sb_sc1[0:64, :], sb_sc1[0:64, :],
              sb_dtb[:, 0:1], None, ALU.add), waits=[("s_v", t_dtraw)], inc="s_v")
        t_u = C["s_v"]
        em.op("scalar", lambda e: e.activation(sb_sc2[0:64, :], sb_sc1[0:64, :],
              AF.Abs), waits=[("s_v", t_u)], inc="s_ac", serial=True)
        em.op("scalar", lambda e: e.activation(sb_sc2[0:64, :], sb_sc2[0:64, :],
              AF.Exp, scale=-1.0), inc="s_ac", serial=True)
        em.op("scalar", lambda e: e.activation(sb_sc2[0:64, :], sb_sc2[0:64, :],
              AF.Ln, bias=sb_cvals[0:64, 1:2]), inc="s_ac", serial=True)
        t_lnx = C["s_ac"]
        em.op("vector", lambda e: e.tensor_scalar(sb_dt[0:64, :], sb_sc1[0:64, :],
              0.0, None, ALU.max), waits=[("s_ac", t_lnx)], inc="s_v")
        em.op("vector", lambda e: e.tensor_add(sb_dt[0:64, :], sb_dt[0:64, :],
              sb_sc2[0:64, :]), inc="s_v")
        t_dt = C["s_v"]
        em.op("scalar", lambda e: e.activation(sb_expa[:, 0:1], sb_alog[:, 0:1],
              AF.Exp), waits=WCONST, inc="s_ac")
        t_expa = C["s_ac"]
        em.op("vector", lambda e: e.tensor_scalar(sb_sc1[0:64, :], sb_dt[0:64, :],
              sb_expa[:, 0:1], None, ALU.mult),
              waits=[("s_v", t_dt), ("s_ac", t_expa)], inc="s_v")
        em.op("vector", lambda e: e.tensor_tensor_scan(
              sb_na[0:64, :], sb_sc1[0:64, :], sb_sc2[0:64, :], 0.0,
              ALU.add, ALU.bypass), inc="s_v")
        t_na = C["s_v"]

        t_pb7 = t_hpb7[0]
        for c in range(NC):
            em.op("tensor", lambda e, c=c: e.transpose(
                pb7[:, 0:64], sb_na[0:64, c * CL:(c + 1) * CL], sb_i64[:]),
                waits=[("s_v", t_na), t_pb7], inc="s_pe")
            em.op("vector", lambda e, c=c, t=C["s_pe"]: e.tensor_copy(
                sb_natm[:, c * 64:(c + 1) * 64], pb7[:, 0:64]),
                waits=[("s_pe", C["s_pe"])], inc="s_v")
            t_pb7 = ("s_v", C["s_v"])
            em.op("vector", lambda e, c=c: e.tensor_scalar(
                sb_natmn[:, c * 64:(c + 1) * 64],
                sb_natm[:, c * 64:(c + 1) * 64], -1.0, None, ALU.mult),
                inc="s_v")
            em.op("tensor", lambda e, c=c: e.transpose(
                pb7[:, 0:64], sb_dt[0:64, c * CL:(c + 1) * CL], sb_i64[:]),
                waits=[("s_v", t_na), t_pb7], inc="s_pe")
            em.op("vector", lambda e, c=c: e.tensor_copy(
                sb_dttm[:, c * 64:(c + 1) * 64], pb7[:, 0:64]),
                waits=[("s_pe", C["s_pe"])], inc="s_v")
            t_pb7 = ("s_v", C["s_v"])
            em.op("tensor", lambda e, c=c: e.transpose(
                pb7[0:1, 0:64], sb_na[0:64, c * CL + 127:c * CL + 128],
                sb_i64[:]), waits=[t_pb7], inc="s_pe")
            em.op("vector", lambda e: e.tensor_copy(sb_row[0:1, 0:64],
                  pb7[0:1, 0:64]), waits=[("s_pe", C["s_pe"])], inc="s_v")
            t_pb7 = ("s_v", C["s_v"])
            em.op("tensor", lambda e: e.matmul(pb7[:, 0:64],
                  sb_onesrowf[0:1, 0:128], sb_row[0:1, 0:64],
                  start=True, stop=True), waits=[t_pb7], inc="s_pe")
            em.op("vector", lambda e, c=c: e.tensor_copy(
                sb_nalb[:, c * 64:(c + 1) * 64], pb7[:, 0:64]),
                waits=[("s_pe", C["s_pe"])], inc="s_v")
            t_pb7 = ("s_v", C["s_v"])
            if c == 0:
                em.op("vector", lambda e: e.tensor_scalar(
                    sb_gexp[:, 0:64], sb_natm[:, 0:64], 0.0, sb_g[:, 0:1],
                    ALU.mult, ALU.add), inc="s_v")
                em.op("vector", lambda e: e.tensor_scalar(
                    sb_gexp[:, 0:64], sb_gexp[:, 0:64], 0.0, None, ALU.mult),
                    inc="s_v")
                em.op("vector", lambda e: e.tensor_scalar(
                    sb_gexp[:, 0:64], sb_gexp[:, 0:64], sb_g[:, 0:1], None,
                    ALU.add), inc="s_v")
            else:
                em.op("scalar", lambda e, c=c: e.activation(
                    sb_gexp[:, c * 64:(c + 1) * 64],
                    sb_nalb[:, (c - 1) * 64:c * 64], AF.Exp, scale=-1.0),
                    waits=[t_pb7] if c == 1 else [("s_v", C["s_v"])], inc="s_ac")
                em.op("vector", lambda e, c=c: e.tensor_scalar(
                    sb_gexp[:, c * 64:(c + 1) * 64],
                    sb_gexp[:, c * 64:(c + 1) * 64], sb_g[:, 0:1], None,
                    ALU.mult), waits=[("s_ac", C["s_ac"])], inc="s_v")
            em.op("vector", lambda e, c=c: e.tensor_sub(
                sb_wdt[:, c * 64:(c + 1) * 64], sb_natm[:, c * 64:(c + 1) * 64],
                sb_nalb[:, c * 64:(c + 1) * 64]),
                waits=[("s_v", C["s_v"])], inc="s_v")
            em.op("scalar", lambda e, c=c: e.activation(
                sb_wdt[:, c * 64:(c + 1) * 64], sb_wdt[:, c * 64:(c + 1) * 64],
                AF.Exp), waits=[("s_v", C["s_v"])], inc="s_ac")
            em.op("vector", lambda e, c=c: e.tensor_mul(
                sb_wdt[:, c * 64:(c + 1) * 64], sb_wdt[:, c * 64:(c + 1) * 64],
                sb_dttm[:, c * 64:(c + 1) * 64]),
                waits=[("s_ac", C["s_ac"])], inc="s_v")
            if c > 0:
                em.op("vector", lambda e, c=c: e.tensor_sub(
                    sb_lstep[:, c * 64:(c + 1) * 64],
                    sb_nalb[:, (c - 1) * 64:c * 64],
                    sb_nalb[:, c * 64:(c + 1) * 64]), inc="s_v")
                em.op("scalar", lambda e, c=c: e.activation(
                    sb_lstep[:, c * 64:(c + 1) * 64],
                    sb_lstep[:, c * 64:(c + 1) * 64], AF.Exp),
                    waits=[("s_v", C["s_v"])], inc="s_ac")
        t_prep = C["s_v"]
        t_prep_ac = C["s_ac"]
        t_prep_pb7 = t_pb7

        t_zsp = list(t_bsp)
        NCS = 4
        o_cin = ar.take(NCS * 1056)
        o_cd = ar.take(NCS * 512)
        o_cout = ar.take(NCS * TPC)
        o_tst = ar.take(2 * TPC)

        t_cin = [None] * NCS
        t_cds = [None] * NCS
        t_cout = [("s_v", 0)] * NCS
        t_tsp = [0, 0]
        t_tr = [0]
        t_convK = {4: t_evac[4], 5: t_evac[5]}

        def z_tile(zi):
            slot2 = zi % 2
            zst = bfv(o_bst + slot2 * TPC, (128, TPC))

            def evz(half, bank, t_pe, zst=zst, slot2=slot2):
                w = [("s_pe", t_pe)]
                if half == 0 and t_zsp[slot2]:
                    w.append(("s_syd", t_zsp[slot2]))
                em.op("scalar", lambda e, half=half, bank=bank, zst=zst:
                      e.activation(zst[:, half * 512:(half + 1) * 512],
                                   pb[bank][:, :], AF.Silu), waits=w, inc="s_ac")
                return ("s_ac", C["s_ac"])

            stream_tile(NXT + 1 + zi, 128, evz, rot=(2, 3))
            t_zsp[slot2] = em.dma("sync", d_z[zi * 128:(zi + 1) * 128, :],
                                  zst[:], waits=[("s_ac", C["s_ac"])])

        conv_pre = {}

        def conv_prefetch(ti):
            slot2 = ti % NCS
            cin = bfv(o_cin + slot2 * 1056, (128, 1056))
            cds = bfv(o_cd + slot2 * 512, (128, 512))
            w = [("s_syd", t_b1_spill)]
            if t_cin[slot2] is not None:
                w.append(t_cin[slot2])
            t_rx = em.dma("scalar", cin[:, 3:1027],
                          d_xbc[ti * 128:(ti + 1) * 128, :], waits=w)
            em.op("vector", lambda e, cin=cin, ti=ti: e.tensor_copy(
                cin[:, 0:3], halo_all[:, ti * 4 + 1:(ti + 1) * 4]),
                waits=[("s_v", t_halo), ("s_scd", t_rx)], inc="s_v")
            wd = list(WCONST)
            if t_cds[slot2] is not None:
                wd.append(t_cds[slot2])
            t_cd = em.dma("gpsimd", cds, convdiag[:, ti * 512:(ti + 1) * 512],
                          waits=wd)
            conv_pre[ti] = (C["s_v"], t_rx, t_cd)

        def conv_tile(ti):
            slot2 = ti % NCS
            cin = bfv(o_cin + slot2 * 1056, (128, 1056))
            cds = bfv(o_cd + slot2 * 512, (128, 512))
            t_hcp, t_rx, t_cd = conv_pre[ti]
            if ti == 64:
                cout = sb_bfm
            elif ti == 65:
                cout = sb_cfm
            else:
                cout = bfv(o_cout + slot2 * TPC, (128, TPC))
            for half in range(2):
                bank = 4 + half
                wv = [("s_scd", t_rx), ("s_gpd", t_cd), ("s_v", t_hcp),
                      t_cout[slot2], t_convK[bank]]
                for k in range(4):
                    em.op("tensor", lambda e, ps=pb[bank],
                          lw=cds[:, k * 128:(k + 1) * 128],
                          rhs=cin[:, half * 512 + k: half * 512 + k + 512],
                          k0=(k == 0), k1=(k == 3):
                          e.matmul(ps[:, :], lw, rhs, start=k0, stop=k1),
                          waits=(wv if k == 0 else ()),
                          inc=("s_pe" if k == 3 else None))
                t_cmm = C["s_pe"]
                em.op("scalar", lambda e, o=cout[:, half * 512:(half + 1) * 512],
                      ps=pb[bank]: e.activation(o, ps[:, :], AF.Silu),
                      waits=[("s_pe", t_cmm)], inc="s_ac")
                t_convK[bank] = ("s_ac", C["s_ac"])
            t_sil = ("s_ac", C["s_ac"])
            t_cin[slot2] = ("s_pe", C["s_pe"])
            t_cds[slot2] = ("s_pe", C["s_pe"])

            if ti == 64:
                for cc in range(8):
                    em.op("tensor", lambda e, cc=cc: e.transpose(
                        pbf[:, cc * 128:(cc + 1) * 128],
                        sb_bfm[:, cc * 128:(cc + 1) * 128], sb_identbf[:]),
                        waits=([t_sil, ("s_v", t_tr[0])] if cc == 0 else ()),
                        inc=("s_pe" if cc == 7 else None))
                t_t4 = C["s_pe"]
                for cc in range(8):
                    em.op("vector", lambda e, cc=cc: e.tensor_copy(
                        sb_btm[:, cc * 128:(cc + 1) * 128],
                        pbf[:, cc * 128:(cc + 1) * 128]),
                        waits=([("s_pe", t_t4)] if cc == 0 else ()), inc="s_v")
                t_tr[0] = C["s_v"]
                t_cout[slot2] = t_sil
                return
            if ti == 65:
                t_cout[slot2] = t_sil
                return

            ts2 = ti % 2
            tst = bfv(o_tst + ts2 * TPC, (128, TPC))
            for cc in range(8):
                wv = [t_sil, ("s_v", t_tr[0])]
                if cc == 0 and t_tsp[ts2]:
                    wv.append(("s_syd", t_tsp[ts2]))
                em.op("tensor", lambda e, cc=cc, src=cout: e.transpose(
                    pbf[:, cc * 128:(cc + 1) * 128],
                    src[:, cc * 128:(cc + 1) * 128], sb_identbf[:]),
                    waits=(wv if cc == 0 else ()),
                    inc=("s_pe" if cc == 7 else None))
            t_t4 = C["s_pe"]
            for cc in range(8):
                em.op("vector", lambda e, cc=cc, tst=tst: e.tensor_copy(
                    tst[:, cc * 128:(cc + 1) * 128],
                    pbf[:, cc * 128:(cc + 1) * 128]),
                    waits=([("s_pe", t_t4)] if cc == 0 else ()), inc="s_v")
            t_tr[0] = C["s_v"]
            t_cout[slot2] = ("s_pe", C["s_pe"])
            t_tsp[ts2] = em.dma(
                "sync", d_xtm.rearrange("(c h s) p -> h s c p", c=NC, h=64)[ti],
                tst.rearrange("s (c p) -> s c p", c=NC),
                waits=[("s_v", t_tr[0])])

        LAGC = 2
        for i in range(NXT + LAGC):
            if i < NXT:
                conv_prefetch(i)
            if i < NZT:
                z_tile(i)
            if i >= LAGC:
                conv_tile(i - LAGC)
        t_zdone = em.cnt["s_syd"]
        t_tr_evac = t_tr[0]
        t_b2 = em.cnt["s_syd"]
        t_btm = ("s_v", t_tr_evac)
        t_bcfm = [t_cout[0], t_cout[1]]
        t_conv_final = [t_convK[4], t_convK[5]]

        ar.pop()
        ar.push()
        o_xtm1 = ar.take(2 * DSSM)
        o_hA = ar.take(2 * DSSM)
        o_hB = ar.take(2 * DSSM)
        o_hbs = ar.take(DSSM)
        o_xw = ar.take(2 * 128)
        hbuf = [f32v(o_hA, (128, DSSM)), f32v(o_hB, (128, DSSM))]
        hbstage = bfv(o_hbs, (128, DSSM))

        t_xtm1 = [None, None]
        t_hb_spill = 0
        t_chain_v = 0
        t_lx_arr = {}

        def p1_load(cc):
            slot2 = cc % 2
            xtm1 = bfv(o_xtm1 + slot2 * DSSM, (128, DSSM))
            w = [("s_syd", t_b2)]
            if cc == 0:
                w += t_bcfm + [t_btm]
            if t_xtm1[slot2] is not None:
                w.append(t_xtm1[slot2])
            t_lx_arr[cc] = em.dma(
                "scalar", xtm1.rearrange("s (h p) -> s h p", h=64),
                d_xtm.rearrange("(c h s) p -> c s h p", c=NC, h=64)[cc],
                waits=w)

        p1_load(0)
        for c in range(NC):
            slot2 = c % 2
            xtm1 = bfv(o_xtm1 + slot2 * DSSM, (128, DSSM))
            if c + 1 < NC:
                p1_load(c + 1)
            t_lx = t_lx_arr[c]
            t_ls = t_prep_ac
            t_wd = t_prep

            hs_src = hbuf[c % 2]
            hs_dst = hbuf[(c + 1) % 2]
            t_smm = {}
            tok_xw = {}
            tok_stt = {}
            PL = 6

            def p1_front(h):
                xw = bfv(o_xw + (h % 8) * 128, (128, 128))
                wv = [("s_scd", t_lx), ("s_v", t_wd), ("s_ac", t_ls),
                      ("s_pe", t_smm.get(h - 8, 0))]
                em.op("vector", lambda e, h=h, xw=xw, xtm1=xtm1, c=c:
                      e.tensor_scalar(
                          xw, xtm1[:, h * 128:(h + 1) * 128],
                          sb_wdt[:, c * 64 + h:c * 64 + h + 1],
                          None, ALU.mult), waits=wv, inc="s_v")
                tok_xw[h] = C["s_v"]
                bank = h % 6
                wv2 = [("s_v", tok_xw[h]), t_btm,
                       ("s_v", tok_stt.get(h - 6, 0))]
                if c == 0 and h in (4, 5):
                    wv2.append(t_conv_final[h % 2])
                em.op("tensor", lambda e, h=h, xw=xw, bank=bank, c=c: e.matmul(
                    pb[bank][:, 0:128], sb_btm[:, c * 128:(c + 1) * 128], xw,
                    start=True, stop=True), waits=wv2, inc="s_pe")
                t_smm[h] = C["s_pe"]

            def p1_back(h):
                bank = h % 6
                if c == 0:
                    em.op("vector", lambda e, h=h, bank=bank, d=hs_dst:
                          e.tensor_copy(d[:, h * 128:(h + 1) * 128],
                                        pb[bank][:, 0:128]),
                          waits=[("s_pe", t_smm[h])], inc="s_v")
                else:
                    em.op("vector", lambda e, h=h, bank=bank, d=hs_dst, s=hs_src:
                          e.scalar_tensor_tensor(
                              d[:, h * 128:(h + 1) * 128],
                              s[:, h * 128:(h + 1) * 128],
                              sb_lstep[:, c * 64 + h:c * 64 + h + 1],
                              pb[bank][:, 0:128], ALU.mult, ALU.add),
                          waits=[("s_pe", t_smm[h])], inc="s_v")
                tok_stt[h] = C["s_v"]
                wv2 = [("s_v", tok_stt[h])]
                if h == 0 and t_hb_spill:
                    wv2.append(("s_syd", t_hb_spill))
                em.op("scalar", lambda e, h=h, d=hs_dst: e.copy(
                    hbstage[:, h * 128:(h + 1) * 128],
                    d[:, h * 128:(h + 1) * 128]), waits=wv2, inc="s_ac")

            for i in range(NH + PL):
                if i < NH:
                    p1_front(i)
                if i >= PL:
                    p1_back(i - PL)
            t_chain_v = C["s_v"]
            t_chain_ac = C["s_ac"]
            t_xtm1[slot2] = ("s_v", t_chain_v)
            dstp = cc_hin if c == NC - 1 else d_hloc[c * 128:(c + 1) * 128, :]
            t_hb_spill = em.dma("sync", dstp, hbstage,
                                waits=[("s_ac", t_chain_ac)])
        t_chain_final = t_chain_v
        em.op("gpsimd", lambda e: e.collective_compute(
            "AllGather", ALU.bypass,
            replica_groups=[[0, 1], [2, 3], [4, 5], [6, 7]],
            ins=[cc_hin.opt()], outs=[cc_hout.opt()]),
            waits=[("s_syd", t_hb_spill)], inc="s_cc")
        t_ag2 = C["s_cc"]
        ar.pop()

        ar.push()
        o_xtm = ar.take(2 * DSSM)
        o_hpeer = ar.take(DSSM)
        o_hloc = ar.take(DSSM)
        o_zc = ar.take(DSSM)
        o_yc = ar.take(DSSM)
        o_idsc = ar.take(DSSM)
        NSLOT = 10
        KLAG = 7
        o_g = ar.take(128)
        o_e = ar.take(2 * 128)
        o_e2 = ar.take(2 * 128)
        o_ar2 = ar.take(2 * 128)
        o_mp = ar.take(NSLOT * 128)
        o_cpp = ar.take(NSLOT * 128)
        o_hbt = ar.take(NSLOT * 128)
        o_rb2 = ar.take(256)
        o_dbc = ar.take(128)
        hpeer = bfv(o_hpeer, (128, DSSM))
        hloc = bfv(o_hloc, (128, DSSM))
        ych = bfv(o_yc, (128, DSSM))
        zc = bfv(o_zc, (128, DSSM))
        idsc = bfv(o_idsc, (128, DSSM))
        gtile = bfv(o_g, (128, 128))
        rb2 = f32v(o_rb2, (128, 128))
        dbc = f32v(o_dbc, (128, 64))

        t_hpeer = em.dma("scalar", hpeer, cc_hout[0:128, :],
                         waits=[("s_cc", t_ag2)])
        em.op("tensor", lambda e: e.matmul(pb7[:, 0:64], sb_onesrowf[0:1, 0:128],
              sb_drow[0:1, 0:64], start=True, stop=True),
              waits=WCONST + [t_prep_pb7], inc="s_pe")
        t_db = C["s_pe"]
        em.op("vector", lambda e: e.tensor_copy(dbc, pb7[:, 0:64]),
              waits=[("s_pe", t_db)], inc="s_v")
        t_dbc = C["s_v"]
        for h in range(NH):
            em.op("vector", lambda e, h=h: e.tensor_scalar(
                idsc[:, h * 128:(h + 1) * 128], sb_identbf[:], dbc[:, h:h + 1],
                None, ALU.mult),
                waits=[("s_v", t_dbc), ("s_syd", t_hb_spill)], inc="s_v")
        t_idsc = C["s_v"]

        t_xtm2 = [None, None]
        t_zld = 0
        t_hlld = 0
        t_ygn_spill = 0
        t_rmsmm = 0
        gi_pA = {}
        gi_pB = {}
        gi_AR = {}
        gi_HBT = {}
        gi_Y = {}
        gi_Yev = {}
        gi_MP = {}
        gi_CPP = {}
        t_gv = 0
        t_rbv = 0
        for c in range(NC):
            slot2 = c % 2
            xtm = bfv(o_xtm + slot2 * DSSM, (128, DSSM))
            w = [("s_syd", t_b2)]
            if c == 0:
                w.append(("s_syd", t_hb_spill))
            if t_xtm2[slot2] is not None:
                w.append(t_xtm2[slot2])
            t_lx = em.dma("scalar", xtm.rearrange("s (h p) -> s h p", h=64),
                          d_xtm.rearrange("(c h s) p -> c s h p", c=NC, h=64)[c],
                          waits=w)
            wz = [("s_syd", t_zdone)]
            if c == 0:
                wz.append(("s_syd", t_hb_spill))
            if t_ygn_spill:
                wz.append(("s_syd", t_ygn_spill))
            if t_rmsmm:
                wz.append(("s_pe", t_rmsmm))
            t_lz = em.dma("sync", zc.rearrange("p (h t) -> p h t", h=64),
                          d_z.rearrange("(h p) t -> p h t", p=128)[:, :, c * CL:(c + 1) * CL],
                          waits=wz)
            if c > 0:
                whl = [("s_syd", t_hb_spill)]
                if t_hlld:
                    whl.append(("s_v", t_hlld))
                t_lh = em.dma("scalar", hloc, d_hloc[(c - 1) * 128:c * 128, :],
                              waits=whl)
            else:
                t_lh = 0

            em.op("tensor", lambda e, c=c: e.matmul(
                pb7[:, 128:256], sb_bfm[:, c * CL:(c + 1) * CL],
                sb_cfm[:, c * CL:(c + 1) * CL], start=True, stop=True),
                waits=t_bcfm + [("s_v", t_dbc), ("s_v", t_rbv)] +
                ([("s_syd", t_hb_spill)] if c == 0 else []), inc="s_pe")
            t_g = C["s_pe"]
            em.op("vector", lambda e: e.tensor_mul(gtile, pb7[:, 128:256],
                  sb_triu[:, :]), waits=[("s_pe", t_g)], inc="s_v")
            t_gv = C["s_v"]

            def front(h):
                gi = c * NH + h
                par = h % 2
                bA = par
                slot = gi % NSLOT
                E = bfv(o_e + par * 128, (128, 128))
                E2 = bfv(o_e2 + par * 128, (128, 128))
                AR = bfv(o_ar2 + par * 128, (128, 128))
                MP = bfv(o_mp + slot * 128, (128, 128))
                CPP = bfv(o_cpp + slot * 128, (128, 128))
                HBT = bfv(o_hbt + slot * 128, (128, 128))

                wv = [("s_v", t_na), ("s_v", t_prep),
                      ("s_ac", gi_pA.get(gi - 2, 0))]
                if gi < 2:
                    wv.append(("s_v", t_chain_final))
                em.op("tensor", lambda e, h=h, c=c, bA=bA: e.matmul(
                    pb[bA][:, 0:128],
                    sb_i64[:, h:h + 1].to_broadcast([64, 128]),
                    sb_na[0:64, c * CL:(c + 1) * CL], start=True, stop=True),
                    waits=wv, inc="s_pe")
                t_pA = C["s_pe"]
                em.op("scalar", lambda e, AR=AR, bA=bA, c=c, h=h: e.activation(
                    AR, pb[bA][:, 0:128], AF.Relu,
                    bias=sb_natmn[:, c * 64 + h:c * 64 + h + 1]),
                    waits=[("s_pe", t_pA), ("s_ac", gi_pB.get(gi - 2, 0))],
                    inc="s_ac")
                t_ar = C["s_ac"]
                if c == 0:
                    em.op("scalar", lambda e, E2=E2, bA=bA: e.activation(
                        E2, pb[bA][:, 0:128], AF.Exp, scale=-1.0),
                        waits=[("s_pe", t_pA), ("s_v", gi_CPP.get(gi - 2, 0))],
                        inc="s_ac")
                else:
                    em.op("scalar", lambda e, E2=E2, bA=bA, c=c, h=h: e.activation(
                        E2, pb[bA][:, 0:128], AF.Exp, scale=-1.0,
                        bias=sb_nalb[:, (c - 1) * 64 + h:(c - 1) * 64 + h + 1]),
                        waits=[("s_pe", t_pA), ("s_v", gi_CPP.get(gi - 2, 0))],
                        inc="s_ac")
                gi_pA[gi] = C["s_ac"]
                em.op("scalar", lambda e, E=E, AR=AR: e.activation(
                    E, AR, AF.Exp, scale=-1.0),
                    waits=[("s_v", gi_MP.get(gi - 2, 0))],
                    inc="s_ac", serial=True)
                gi_pB[gi] = C["s_ac"]
                t_e = C["s_ac"]
                em.op("vector", lambda e, MP=MP, E=E, c=c, h=h:
                      e.scalar_tensor_tensor(
                          MP, E, sb_dttm[:, c * 64 + h:c * 64 + h + 1], gtile,
                          ALU.mult, ALU.mult),
                      waits=[("s_ac", t_e), ("s_v", t_gv),
                             ("s_pe", gi_Y.get(gi - NSLOT, 0))], inc="s_v")
                gi_MP[gi] = C["s_v"]
                em.op("vector", lambda e, CPP=CPP, E2=E2, c=c: e.tensor_mul(
                    CPP, sb_cfm[:, c * CL:(c + 1) * CL], E2),
                    waits=[("s_ac", gi_pA[gi]),
                           ("s_pe", gi_Y.get(gi - NSLOT, 0))],
                    inc="s_v")
                gi_CPP[gi] = C["s_v"]
                wv = [("s_scd", t_hpeer), ("s_v", t_prep),
                      ("s_pe", gi_Y.get(gi - NSLOT, 0))]
                if c > 0:
                    wv.append(("s_scd", t_lh))
                if c == 0:
                    em.op("vector", lambda e, HBT=HBT, h=h: e.tensor_scalar(
                        HBT, hpeer[:, h * 128:(h + 1) * 128],
                        sb_gexp[:, h:h + 1], None, ALU.mult),
                        waits=wv, inc="s_v")
                else:
                    em.op("vector", lambda e, HBT=HBT, h=h, c=c:
                          e.scalar_tensor_tensor(
                              HBT, hpeer[:, h * 128:(h + 1) * 128],
                              sb_gexp[:, c * 64 + h:c * 64 + h + 1],
                              hloc[:, h * 128:(h + 1) * 128],
                              ALU.mult, ALU.add),
                          waits=wv, inc="s_v")
                gi_HBT[gi] = C["s_v"]

            def back(h):
                gi = c * NH + h
                slot = gi % NSLOT
                bY = 4 + h % 2
                MP = bfv(o_mp + slot * 128, (128, 128))
                CPP = bfv(o_cpp + slot * 128, (128, 128))
                HBT = bfv(o_hbt + slot * 128, (128, 128))
                wv = [("s_v", gi_MP[gi]), ("s_v", gi_HBT[gi]), ("s_scd", t_lx),
                      ("s_v", t_idsc), ("s_ac", gi_Yev.get(gi - 2, 0))]
                if gi < 2:
                    wv.append(("s_v", t_chain_final))
                em.op("tensor", lambda e, h=h, bY=bY, MP=MP, xtm=xtm: e.matmul(
                    pb[bY][:, 0:128], xtm[:, h * 128:(h + 1) * 128], MP,
                    start=True, stop=False), waits=wv, inc=None)
                em.op("tensor", lambda e, bY=bY, HBT=HBT, CPP=CPP: e.matmul(
                    pb[bY][:, 0:128], HBT, CPP, start=False, stop=False),
                    waits=[("s_v", gi_CPP[gi])], inc=None)
                em.op("tensor", lambda e, h=h, bY=bY, xtm=xtm: e.matmul(
                    pb[bY][:, 0:128], xtm[:, h * 128:(h + 1) * 128],
                    idsc[:, h * 128:(h + 1) * 128], start=False, stop=True),
                    inc="s_pe")
                gi_Y[gi] = C["s_pe"]
                wy = [("s_pe", gi_Y[gi])]
                if h == 0:
                    wy.append(("s_syd", t_ygn_spill))
                    wy.append(("s_pe", t_rmsmm))
                em.op("scalar", lambda e, h=h, bY=bY: e.activation(
                    ych[:, h * 128:(h + 1) * 128], pb[bY][:, 0:128], AF.Copy),
                    waits=wy, inc="s_ac")
                gi_Yev[gi] = C["s_ac"]

            for i in range(NH + KLAG):
                if i < NH:
                    front(i)
                if i >= KLAG:
                    back(i - KLAG)
            t_ally = C["s_ac"]
            t_hlld = C["s_v"]
            t_xtm2[slot2] = ("s_pe", C["s_pe"])

            em.op("vector", lambda e, zc=zc, ych=ych: e.tensor_mul(ych, ych, zc),
                  waits=[("s_ac", t_ally), ("s_syd", t_lz)], inc="s_v")
            t_yg = C["s_v"]
            em.op("scalar", lambda e, zc=zc, ych=ych: e.activation(
                zc, ych, AF.Square), waits=[("s_v", t_yg)], inc="s_ac")
            t_sq2 = C["s_ac"]
            for h in range(NH):
                em.op("tensor", lambda e, h=h, zc=zc, k0=(h == 0),
                      k1=(h == NH - 1): e.matmul(
                          pb7[0:1, 0:128], sb_onescolbf[:],
                          zc[:, h * 128:(h + 1) * 128], start=k0, stop=k1),
                      waits=([("s_ac", t_sq2), ("s_v", t_gv), ("s_v", t_rbv)]
                             if h == 0 else ()),
                      inc=("s_pe" if h == NH - 1 else None))
            t_rmsmm = C["s_pe"]
            em.op("vector", lambda e: e.tensor_copy(sb_row[0:1, 0:128],
                  pb7[0:1, 0:128]), waits=[("s_pe", t_rmsmm)], inc="s_v",
                  serial=True)
            em.op("scalar", lambda e: e.activation(sb_row[0:1, 0:128],
                  sb_row[0:1, 0:128], AF.Ln, scale=1.0 / DSSM,
                  bias=sb_cvals[0:1, 0:1]),
                  waits=[("s_v", C["s_v"])], inc="s_ac", serial=True)
            em.op("scalar", lambda e: e.activation(sb_row[0:1, 0:128],
                  sb_row[0:1, 0:128], AF.Exp, scale=-0.5), inc="s_ac",
                  serial=True)
            t_rr = C["s_ac"]
            em.op("tensor", lambda e: e.matmul(pb7[:, 128:256],
                  sb_onesrowf[0:1, 0:128], sb_row[0:1, 0:128],
                  start=True, stop=True),
                  waits=[("s_ac", t_rr), ("s_v", t_gv)], inc="s_pe")
            t_rbm = C["s_pe"]
            em.op("vector", lambda e: e.tensor_copy(rb2, pb7[:, 128:256]),
                  waits=[("s_pe", t_rbm)], inc="s_v")
            t_rbv = C["s_v"]
            for h in range(NH):
                em.op("vector", lambda e, h=h, zc=zc, ych=ych:
                      e.scalar_tensor_tensor(
                          zc[:, h * 128:(h + 1) * 128],
                          ych[:, h * 128:(h + 1) * 128],
                          sb_normw[:, h:h + 1], rb2, ALU.mult, ALU.mult),
                      waits=([("s_v", t_rbv)] if h == 0 else ()), inc="s_v")
            t_ygnv = C["s_v"]
            t_ygn_spill = em.dma(
                "sync",
                d_ygn.rearrange("(h p) t -> p h t", p=128)[:, :, c * CL:(c + 1) * CL],
                zc.rearrange("p (h t) -> p h t", h=64),
                waits=[("s_v", t_ygnv)])
        t_p2 = em.cnt["s_syd"]
        ar.pop()

        BF = em.bar()
        ar.push()
        o_ygk = ar.take(32 * TPC)
        o_wo = ar.take(2 * 32 * 128)
        o_fst = ar.take(2 * TPC)
        o_fpt = ar.take(2 * 2 * TPC)
        o_fsq = ar.take(2 * TPC)
        ygk = bfv(o_ygk, (128, 32 * TPC))

        WOO = [0]
        t_ldy = 0
        t_attn_sp = [0, 0]
        t_part_sp = [0, 0]
        t_part_ld = [0, 0]
        t_fevac = {i: ("s_v", 0) for i in (2, 3, 4, 5)}
        t_kh_done = 0
        t_fpe = [0, 0]
        t_wslf = [None, None]
        for kh in range(2):
            wy = list(BF)
            if kh == 1:
                wy.append(("s_pe", t_kh_done))
            t_ldy = em.dma("sync", ygk.rearrange("p (a t) -> p a t", a=32),
                           d_ygn.rearrange("(a p) t -> p a t", p=128)[:, kh * 32:(kh + 1) * 32, :],
                           waits=wy)
            for dm in range(NDM):
                slot2 = dm % 2
                wsl = bfv(o_wo + slot2 * 32 * 128, (128, 32 * 128))
                w = list(BF) if t_wslf[slot2] is None else [t_wslf[slot2]]
                t_w = em.dma("gpsimd", wsl, woutp[:, WOO[0]:WOO[0] + 32 * 128],
                             waits=w)
                WOO[0] += 32 * 128
                fpt = f32v(o_fpt + slot2 * 2 * TPC, (128, TPC))
                fst = bfv(o_fst + slot2 * TPC, (128, TPC))
                if kh == 1:
                    wp = [("s_syd", t_part_sp[0]), ("s_syd", t_part_sp[1])]
                    if t_part_ld[slot2]:
                        wp.append(("s_v", t_part_ld[slot2]))
                    t_lp = em.dma("scalar", fpt, d_part[dm * 128:(dm + 1) * 128, :],
                                  waits=wp)
                toks = []
                for half in range(2):
                    if kh == 0:
                        bank = pe_rot4[(2 * dm + half) % 4]
                    else:
                        bank = 2 + (2 * dm + half) % 2
                    wv = [("s_gpd", t_w), ("s_syd", t_ldy), t_fevac[bank]]
                    for kt in range(32):
                        em.op("tensor", lambda e, ps=pb[bank],
                              lw=wsl[:, kt * 128:(kt + 1) * 128],
                              rhs=ygk[:, kt * TPC + half * 512: kt * TPC + (half + 1) * 512],
                              k0=(kt == 0), k1=(kt == 31):
                              e.matmul(ps[:, :], lw, rhs, start=k0, stop=k1),
                              waits=(wv if kt == 0 else ()),
                              inc=("s_pe" if kt == 31 else None))
                    t_pe = C["s_pe"]
                    if kh == 0:
                        w2 = [("s_pe", t_pe)]
                        if half == 0 and t_part_sp[slot2]:
                            w2.append(("s_syd", t_part_sp[slot2]))
                        em.op("vector", lambda e, o=fpt[:, half * 512:(half + 1) * 512],
                              ps=pb[bank]: e.tensor_copy(o, ps[:, :]),
                              waits=w2, inc="s_v")
                    else:
                        w2 = [("s_pe", t_pe), ("s_scd", t_lp)]
                        if half == 0 and t_attn_sp[slot2]:
                            w2.append(("s_syd", t_attn_sp[slot2]))
                        em.op("vector", lambda e, o=fst[:, half * 512:(half + 1) * 512],
                              ps=pb[bank], pr=fpt[:, half * 512:(half + 1) * 512]:
                              e.tensor_add(o, ps[:, :], pr), waits=w2, inc="s_v")
                    t_fevac[bank] = ("s_v", C["s_v"])
                    toks.append(C["s_v"])
                if kh == 0:
                    t_part_sp[slot2] = em.dma(
                        "sync", d_part[dm * 128:(dm + 1) * 128, :], fpt,
                        waits=[("s_v", toks[-1])])
                else:
                    t_part_ld[slot2] = toks[-1]
                    sqt = bfv(o_fsq + slot2 * TPC, (128, TPC))
                    em.op("scalar", lambda e, a=sqt, b=fst: e.activation(
                        a, b, AF.Square),
                        waits=[("s_v", toks[-1]),
                               ("s_pe", t_fpe[slot2])], inc="s_ac")
                    t_aq = C["s_ac"]
                    ws = [("s_v", toks[-1])]
                    if dm == 0:
                        ws += [t_fevac[4], t_fevac[5]]
                    t_fpe[slot2] = emit_stats_tile(
                        fst, sqt, dm == 0, dm == NDM - 1,
                        extra_wait_sum=ws, extra_wait_sq=[("s_ac", t_aq)],
                        sums=(4, 5), sqs=(0, 1))
                    t_attn_sp[slot2] = em.dma(
                        "sync", d_attn[dm * 128:(dm + 1) * 128, :], fst,
                        waits=[("s_v", toks[-1]), ("s_pe", t_fpe[slot2])])
                t_wslf[slot2] = ("s_v", toks[-1])
            t_kh_done = C["s_pe"]
        t_fstat = C["s_pe"]
        t_f = em.cnt["s_syd"]
        ar.pop()

        BG = em.bar()
        ar.push()
        o_xf = ar.take(NKT * TPC)
        o_gst = ar.take(4 * TPC)
        o_gmb = ar.take(8 * TPC)
        xf = bfv(o_xf, (128, NKT * TPC))
        m2b = f32v(o_gmb, (128, TPC))
        r2b = f32v(o_gmb + 2 * TPC, (128, TPC))
        m3b = f32v(o_gmb + 4 * TPC, (128, TPC))
        r3b = f32v(o_gmb + 6 * TPC, (128, TPC))

        t_gpe = [0, 0]
        t_ln2 = finalize_stats(m2b, r2b, t_fstat, H, sums=(4, 5), sqs=(0, 1))

        t_s2sp = [0, 0]
        for dm in range(NDM):
            slot2 = dm % 2
            ast = bfv(o_gst + slot2 * TPC, (128, TPC))
            xst = bfv(o_gst + (2 + slot2) * TPC, (128, TPC))
            w = [("s_v", t_ln2)]
            if t_gpe[slot2]:
                w.append(("s_pe", t_gpe[slot2]))
            if t_s2sp[slot2]:
                w.append(("s_syd", t_s2sp[slot2]))
            t_la = em.dma("scalar", ast, d_attn[dm * 128:(dm + 1) * 128, :],
                          waits=w)
            t_lx2 = em.dma("gpsimd", xst, xT[dm * 128:(dm + 1) * 128, :],
                           waits=w)
            em.op("vector", lambda e, a=ast: e.tensor_sub(a, a, m2b),
                  waits=[("s_scd", t_la), ("s_v", t_ln2)], inc="s_v")
            em.op("vector", lambda e, a=ast: e.tensor_mul(a, a, r2b), inc="s_v")
            em.op("vector", lambda e, a=ast, b=xst: e.tensor_add(a, a, b),
                  waits=[("s_gpd", t_lx2)], inc="s_v")
            t_s2 = C["s_v"]
            em.op("scalar", lambda e, a=xst, b=ast: e.activation(a, b, AF.Square),
                  waits=[("s_v", t_s2)], inc="s_ac")
            t_aq = C["s_ac"]
            t_gpe[slot2] = emit_stats_tile(
                ast, xst, dm == 0, dm == NDM - 1,
                extra_wait_sum=[("s_v", t_s2)],
                extra_wait_sq=[("s_ac", t_aq)])
            t_s2sp[slot2] = em.dma("sync", d_s2[dm * 128:(dm + 1) * 128, :], ast,
                                   waits=[("s_pe", t_gpe[slot2]),
                                          ("s_ac", t_aq)])
        t_s2stat = C["s_pe"]
        t_s2done = em.cnt["s_syd"]
        t_ln3i = finalize_stats(m3b, r3b, t_s2stat, H)

        t_gl2 = [0, 0]
        for dm in range(NDM):
            slot2 = dm % 2
            xft = xf[:, dm * TPC:(dm + 1) * TPC]
            w = [("s_v", t_ln3i), ("s_syd", t_s2done)]
            ast = bfv(o_gst + slot2 * TPC, (128, TPC))
            if t_gl2[slot2]:
                w.append(("s_v", t_gl2[slot2]))
            t_ls2 = em.dma("scalar", ast, d_s2[dm * 128:(dm + 1) * 128, :],
                           waits=w)
            em.op("vector", lambda e, a=xft, b=ast: e.tensor_sub(a, b, m3b),
                  waits=[("s_scd", t_ls2)], inc="s_v")
            em.op("vector", lambda e, a=xft: e.tensor_mul(a, a, r3b), inc="s_v")
            t_gl2[slot2] = C["s_v"]
        t_xf = C["s_v"]
        if debug:
            em.dma("sync", d_dbgxf.rearrange("(a p) t -> p a t", p=128),
                   xf.rearrange("p (a t) -> p a t", a=NKT),
                   waits=[("s_v", t_xf)])

        o_hf = ar.take(TPC)
        o_wbc = ar.take(TPC)
        o_ltm = ar.take(2 * 64)
        o_lfm = ar.take(2 * TPC)
        o_wtm = ar.take(64)
        o_wfm = ar.take(TPC)
        o_apk = ar.take(32 * 128)
        o_bmp = ar.take(H)
        o_wrp = ar.take(NKT * 8)
        o_wbsl = ar.take(2 * 32 * 128)
        o_fst2 = ar.take(2 * TPC)
        o_ost = ar.take(4 * TPC)
        hfm = bfv(o_hf, (128, TPC))
        wbc = bfv(o_wbc, (128, TPC))
        ltm = f32v(o_ltm, (128, 64))
        lfm = f32v(o_lfm, (8, TPC))
        wtm = bfv(o_wtm, (128, 8))
        wfm = bfv(o_wfm, (8, TPC))
        apk = bfv(o_apk, (128, 32 * 128))
        bmp = bfv(o_bmp, (128, H))
        wrp = bfv(o_wrp, (128, NKT * 8))

        t_apk = em.dma("gpsimd", apk, apack, waits=BG)
        t_bmp = em.dma("gpsimd", bmp, bmpack, waits=BG)
        t_wrp = em.dma("gpsimd", wrp, wrtp, waits=BG)

        for half in range(2):
            bank = half
            for kt in range(32):
                em.op("tensor", lambda e, ps=pb[bank],
                      lw=apk[:, kt * 128:(kt + 1) * 128],
                      rhs=xf[:, kt * TPC + half * 512: kt * TPC + (half + 1) * 512],
                      k0=(kt == 0), k1=(kt == 31):
                      e.matmul(ps[:, :], lw, rhs, start=k0, stop=k1),
                      waits=([("s_gpd", t_apk), ("s_v", t_xf),
                              ("s_v", t_ln3i)] if kt == 0 else ()),
                      inc=("s_pe" if kt == 31 else None))
            t_h = C["s_pe"]
            em.op("vector", lambda e, half=half, bank=bank: e.tensor_copy(
                hfm[:, half * 512:(half + 1) * 512], pb[bank][:, :]),
                waits=[("s_pe", t_h)], inc="s_v")
        t_hfm = C["s_v"]
        for half in range(2):
            bank = half
            for kt in range(32):
                em.op("tensor", lambda e, ps=pb[bank],
                      lw=wrp[:, kt * 8:(kt + 1) * 8],
                      rhs=xf[:, kt * TPC + half * 512: kt * TPC + (half + 1) * 512],
                      k0=(kt == 0), k1=(kt == 31):
                      e.matmul(ps[0:8, :], lw, rhs, start=k0, stop=k1),
                      waits=([("s_gpd", t_wrp), ("s_v", t_hfm)] if kt == 0 else ()),
                      inc=("s_pe" if kt == 31 else None))
            t_l = C["s_pe"]
            em.op("vector", lambda e, half=half, bank=bank: e.tensor_copy(
                lfm[0:8, half * 512:(half + 1) * 512], pb[bank][0:8, :]),
                waits=[("s_pe", t_l)], inc="s_v")
        t_lfm = C["s_v"]

        t_pb7r = t_rbv
        for tt in range(8):
            em.op("tensor", lambda e, tt=tt: e.transpose(
                pb7[:, 0:8], lfm[0:8, tt * 128:(tt + 1) * 128],
                sb_i128[0:8, 0:8]),
                waits=[("s_v", t_lfm), ("s_v", t_pb7r)], inc="s_pe")
            t_tr = C["s_pe"]
            em.op("vector", lambda e: e.tensor_copy(ltm[:, 0:8], pb7[:, 0:8]),
                  waits=[("s_pe", t_tr)], inc="s_v", serial=True)
            em.op("vector", lambda e: e.reduce_max(ltm[:, 8:9], ltm[:, 0:8],
                  axis=mybir.AxisListType.X), inc="s_v", serial=True)
            em.op("vector", lambda e: e.tensor_scalar(ltm[:, 48:56],
                  ltm[:, 0:8], ltm[:, 8:9], None, ALU.subtract), inc="s_v",
                  serial=True)
            em.op("scalar", lambda e: e.activation(ltm[:, 16:24], ltm[:, 48:56],
                  AF.Exp), waits=[("s_v", C["s_v"])], inc="s_ac", serial=True)
            t_e3 = C["s_ac"]
            em.op("vector", lambda e: e.tensor_scalar(ltm[:, 24:32],
                  ltm[:, 16:24], 1.0, None, ALU.is_ge),
                  waits=[("s_ac", t_e3)], inc="s_v", serial=True)
            em.op("vector", lambda e: e.tensor_mul(ltm[:, 32:40], ltm[:, 16:24],
                  ltm[:, 24:32]), inc="s_v", serial=True)
            em.op("vector", lambda e: e.tensor_sub(ltm[:, 32:40], ltm[:, 16:24],
                  ltm[:, 32:40]), inc="s_v", serial=True)
            em.op("vector", lambda e: e.reduce_max(ltm[:, 9:10], ltm[:, 32:40],
                  axis=mybir.AxisListType.X), inc="s_v", serial=True)
            em.op("vector", lambda e: e.tensor_scalar(ltm[:, 24:32],
                  ltm[:, 16:24], ltm[:, 9:10], None, ALU.is_ge), inc="s_v",
                  serial=True)
            em.op("vector", lambda e: e.tensor_mul(ltm[:, 32:40], ltm[:, 16:24],
                  ltm[:, 24:32]), inc="s_v", serial=True)
            em.op("vector", lambda e: e.tensor_scalar(ltm[:, 9:10], ltm[:, 9:10],
                  1.0, None, ALU.add), inc="s_v", serial=True)
            em.op("vector", lambda e: e.reciprocal(ltm[:, 10:11], ltm[:, 9:10]),
                  inc="s_v", serial=True)
            em.op("vector", lambda e: e.tensor_scalar(wtm[:, 0:8], ltm[:, 32:40],
                  ltm[:, 10:11], None, ALU.mult), inc="s_v", serial=True)
            t_w2 = C["s_v"]
            em.op("tensor", lambda e: e.transpose(pbf[0:8, 0:128], wtm[:, 0:8],
                  sb_identbf[:]), waits=[("s_v", t_w2), ("s_v", t_tr_evac)],
                  inc="s_pe")
            t_tw = C["s_pe"]
            em.op("vector", lambda e, tt=tt: e.tensor_copy(
                wfm[0:8, tt * 128:(tt + 1) * 128], pbf[0:8, 0:128]),
                waits=[("s_pe", t_tw)], inc="s_v", serial=True)
            t_pb7r = C["s_v"]
            t_tr_evac = C["s_v"]
        t_wfm = C["s_v"]
        for half in range(2):
            bank = half
            em.op("tensor", lambda e, half=half, bank=bank: e.matmul(
                pb[bank][:, :], sb_rep2[0:8, :],
                wfm[0:8, half * 512:(half + 1) * 512], start=True, stop=True),
                waits=[("s_v", t_wfm)], inc="s_pe")
            t_wb = C["s_pe"]
            em.op("scalar", lambda e, half=half, bank=bank: e.activation(
                wbc[:, half * 512:(half + 1) * 512], pb[bank][:, :], AF.Copy),
                waits=[("s_pe", t_wb)], inc="s_ac")
        t_wbc = C["s_ac"]
        em.op("vector", lambda e: e.tensor_mul(hfm, hfm, wbc),
              waits=[("s_ac", t_wbc), ("s_v", t_hfm)], inc="s_v")
        t_hw = C["s_v"]
        if debug:
            em.dma("sync", d_dbgw[:, 0:TPC], wbc, waits=[("s_v", t_hw)])
            em.dma("sync", d_dbgw[:, TPC:2 * TPC], hfm, waits=[("s_v", t_hw)])


        t_wslh = [None, None]
        t_hev = {0: ("s_v", t_hw), 1: ("s_v", t_hw)}
        t_ffn_sp = [0, 0]
        t_hpe = [0, 0]
        for dm in range(NDM):
            slot2 = dm % 2
            wsl = bfv(o_wbsl + slot2 * 32 * 128, (128, 32 * 128))
            w = [("s_v", t_xf)] if t_wslh[slot2] is None else [t_wslh[slot2]]
            t_w = em.dma("gpsimd", wsl, wbp[:, dm * 32 * 128:(dm + 1) * 32 * 128],
                         waits=w)
            fst = bfv(o_fst2 + slot2 * TPC, (128, TPC))
            sqt = bfv(o_gst + slot2 * TPC, (128, TPC))
            toks = []
            for half in range(2):
                bank = half
                wv = [("s_gpd", t_w), ("s_v", t_hw), ("s_gpd", t_bmp),
                      t_hev[bank]]
                for kt in range(32):
                    em.op("tensor", lambda e, ps=pb[bank],
                          lw=wsl[:, kt * 128:(kt + 1) * 128],
                          rhs=xf[:, kt * TPC + half * 512: kt * TPC + (half + 1) * 512],
                          k0=(kt == 0):
                          e.matmul(ps[:, :], lw, rhs, start=k0, stop=False),
                          waits=(wv if kt == 0 else ()), inc=None)
                em.op("tensor", lambda e, ps=pb[bank], dm=dm: e.matmul(
                    ps[:, :], sb_biasbf[0:1, dm * 128:(dm + 1) * 128],
                    sb_onesrowbf[0:1, 0:512], start=False, stop=False), inc=None)
                em.op("tensor", lambda e, ps=pb[bank], dm=dm, half=half: e.matmul(
                    ps[:, :], bmp[:, dm * 128:(dm + 1) * 128],
                    hfm[:, half * 512:(half + 1) * 512], start=False, stop=True),
                    inc="s_pe")
                t_pe = C["s_pe"]
                w2 = [("s_pe", t_pe)]
                if half == 0 and t_ffn_sp[slot2]:
                    w2.append(("s_syd", t_ffn_sp[slot2]))
                em.op("vector", lambda e, o=fst[:, half * 512:(half + 1) * 512],
                      ps=pb[bank]: e.tensor_copy(o, ps[:, :]), waits=w2,
                      inc="s_v")
                t_hev[bank] = ("s_v", C["s_v"])
                toks.append(C["s_v"])
            t_ev3 = toks[-1]
            em.op("scalar", lambda e, a=sqt, b=fst: e.activation(a, b, AF.Square),
                  waits=[("s_v", t_ev3), ("s_pe", t_hpe[slot2])], inc="s_ac")
            t_aq = C["s_ac"]
            t_hpe[slot2] = emit_stats_tile(
                fst, sqt, dm == 0, dm == NDM - 1,
                extra_wait_sum=[("s_v", t_ev3)] + ([("s_ac", t_wbc)] if dm == 0 else []),
                extra_wait_sq=[("s_ac", t_aq)])
            t_wslh[slot2] = ("s_v", t_ev3)
            t_ffn_sp[slot2] = em.dma("sync", d_ffn[dm * 128:(dm + 1) * 128, :],
                                     fst, waits=[("s_pe", t_hpe[slot2]),
                                                 ("s_ac", t_aq)])
        t_ffnstat = C["s_pe"]
        t_ln3 = finalize_stats(m2b, r2b, t_ffnstat, H)

        t_old = [0, 0]
        for dm in range(NDM):
            slot2 = dm % 2
            fst = bfv(o_fst2 + slot2 * TPC, (128, TPC))
            s2t = bfv(o_gst + (2 + slot2) * TPC, (128, TPC))
            ost = f32v(o_ost + slot2 * 2 * TPC, (128, TPC))
            w = [("s_v", t_ln3), ("s_syd", t_ffn_sp[0]), ("s_syd", t_ffn_sp[1])]
            if t_old[slot2]:
                w.append(("s_syd", t_old[slot2]))
            t_lf = em.dma("scalar", fst, d_ffn[dm * 128:(dm + 1) * 128, :],
                          waits=w)
            t_ls3 = em.dma("sync", s2t, d_s2[dm * 128:(dm + 1) * 128, :],
                           waits=w)
            em.op("vector", lambda e, o=ost, b=fst: e.tensor_sub(o, b, m2b),
                  waits=[("s_scd", t_lf), ("s_v", t_ln3)], inc="s_v")
            em.op("vector", lambda e, o=ost: e.tensor_mul(o, o, r2b), inc="s_v")
            em.op("vector", lambda e, o=ost, b=s2t: e.tensor_add(o, o, b),
                  waits=[("s_syd", t_ls3)], inc="s_v")
            t_o = C["s_v"]
            t_old[slot2] = em.dma("sync", out_T[dm * 128:(dm + 1) * 128, :], ost,
                                  waits=[("s_v", t_o)])
        t_final = em.cnt["s_syd"]

        if debug:
            for nm, (dst, src) in dbg.items():
                em.dma("sync", dst, src, waits=[("s_syd", t_final)])
            t_final = em.cnt["s_syd"]

        em.op("sync", lambda e: e.wait_ge(sems["s_syd"], t_final))

        em.check_deadlock()

        with nc.Block() as block:
            @block.tensor
            def _(te):
                em.replay_one("tensor", te, sems)

            @block.vector
            def _(v):
                em.replay_one("vector", v, sems)

            @block.scalar
            def _(s):
                em.replay_one("scalar", s, sems)

            @block.gpsimd
            def _(gp):
                em.replay_one("gpsimd", gp, sems)

            @block.sync
            def _(sy):
                em.replay_one("sync", sy, sems)

    return nc


_CACHE = {}


def _pack_weights(W_in, conv_w, dt_bias, A_log, D_param, norm_weight, W_out,
                  Wb, bias, Wr, A, Bm):
    f = np.float32
    W_in = np.asarray(W_in, f)

    def pack(Wsub, width=128):
        nt = Wsub.shape[1] // width
        out = np.empty((128, nt * NKT * width), f)
        for t in range(nt):
            blk = Wsub[:, t * width:(t + 1) * width].reshape(NKT, 128, width)
            out[:, t * NKT * width:(t + 1) * NKT * width] = \
                blk.transpose(1, 0, 2).reshape(128, NKT * width)
        return out

    winp = np.concatenate([
        pack(W_in[:, DSSM:DSSM + CONV]),
        pack(W_in[:, DSSM + CONV:], 64),
        pack(W_in[:, :DSSM]),
    ], axis=1)

    W_out = np.asarray(W_out, f)
    parts = []
    for kh in range(2):
        sub = W_out[kh * 4096:(kh + 1) * 4096, :]
        for dm in range(NDM):
            blk = sub[:, dm * 128:(dm + 1) * 128].reshape(32, 128, 128)
            parts.append(blk.transpose(1, 0, 2).reshape(128, 32 * 128))
    woutp = np.concatenate(parts, axis=1)

    WbT = np.ascontiguousarray(np.asarray(Wb, f).T)
    parts = []
    for dm in range(NDM):
        blk = WbT[:, dm * 128:(dm + 1) * 128].reshape(NKT, 128, 128)
        parts.append(blk.transpose(1, 0, 2).reshape(128, NKT * 128))
    wbp = np.concatenate(parts, axis=1)

    A2T = np.ascontiguousarray(np.asarray(A, f).reshape(128, H).T)
    apack = A2T.reshape(NKT, 128, 128).transpose(1, 0, 2).reshape(128, NKT * 128)
    bmpack = np.ascontiguousarray(
        np.asarray(Bm, f).transpose(0, 2, 1).reshape(128, H))
    WrT = np.ascontiguousarray(np.asarray(Wr, f).T)
    wrtp = WrT.reshape(NKT, 128, 8).transpose(1, 0, 2).reshape(128, NKT * 8)

    cw = np.asarray(conv_w, f)
    convdiag = np.zeros((128, NXT * 4 * 128), f)
    idx = np.arange(128)
    for i in range(NXT):
        for k in range(4):
            convdiag[idx, (i * 4 + k) * 128 + idx] = cw[i * 128:(i + 1) * 128, k]

    rep2m = np.zeros((8, 128), f)
    for e in range(8):
        rep2m[e, e * 16:(e + 1) * 16] = 2.0

    return dict(
        winp=np.ascontiguousarray(winp),
        woutp=np.ascontiguousarray(woutp),
        wbp=np.ascontiguousarray(wbp),
        apack=np.ascontiguousarray(apack),
        bmpack=bmpack,
        wrtp=np.ascontiguousarray(wrtp),
        convdiag=convdiag,
        i128=np.eye(128, dtype=f),
        i64=np.eye(64, dtype=f),
        trilbig=np.tril(np.full((128, 128), BIGV, f), -1),
        triu01=np.triu(np.ones((128, 128), f)),
        onescol=np.ones((128, 1), f),
        onesrow=np.ones((1, 512), f),
        dtbias=np.asarray(dt_bias, f).reshape(64, 1),
        alog=np.asarray(A_log, f).reshape(64, 1),
        drow=np.asarray(D_param, f).reshape(1, 64),
        normw=np.ascontiguousarray(
            np.asarray(norm_weight, f).reshape(64, 128).T),
        biasrow=np.asarray(bias, f).reshape(1, H),
        rep2=rep2m,
        cvals=np.ascontiguousarray(
            np.stack([np.full(128, EPS, f), np.ones(128, f)], axis=1)),
    )


def kernel(L0_attn_hidden_states, L0_attn_W_in, L0_attn_conv_w, L0_attn_dt_bias,
           L0_attn_A_log, L0_attn_D_param, L0_attn_norm_weight, L0_attn_W_out,
           L0_ffn_Wb, L0_ffn_bias, L0_ffn_Wr, L0_ffn_A, L0_ffn_Bm):
    x = np.asarray(L0_attn_hidden_states, np.float32)
    com = _pack_weights(L0_attn_W_in, L0_attn_conv_w, L0_attn_dt_bias,
                        L0_attn_A_log, L0_attn_D_param, L0_attn_norm_weight,
                        L0_attn_W_out, L0_ffn_Wb, L0_ffn_bias, L0_ffn_Wr,
                        L0_ffn_A, L0_ffn_Bm)
    if "nc" not in _CACHE:
        _CACHE["nc"] = build_program(debug=False)
    nc = _CACHE["nc"]

    in_maps = []
    for c in range(8):
        b, half = c // 2, c % 2
        m = dict(com)
        m["xT"] = np.ascontiguousarray(x[b, half * TPC:(half + 1) * TPC, :].T)
        if half == 0:
            m["xhT"] = np.zeros((H, 4), np.float32)
        else:
            m["xhT"] = np.ascontiguousarray(x[b, TPC - 4:TPC, :].T)
        m["gcol"] = np.full((128, 1), float(half), np.float32)
        in_maps.append(m)

    res = run_bass_kernel_spmd(nc, in_maps, core_ids=list(range(8)))
    out = np.empty((B, S, H), np.float32)
    for c in range(8):
        b, half = c // 2, c % 2
        out[b, half * TPC:(half + 1) * TPC, :] = res.results[c]["out_T"].T
    return out
